# version 08
# baseline (speedup 1.0000x reference)
import bisect
import sys as _sys
import numpy as np
import ml_dtypes

import concourse.bass as bass
import concourse.mybir as mybir
from concourse.bass_utils import run_bass_kernel_spmd

F32 = mybir.dt.float32
BF16 = mybir.dt.bfloat16
AF = mybir.ActivationFunctionType
ALU = mybir.AluOpType
AX = mybir.AxisListType

NCORES = 8
T = 2048
D = 1024
NCH = D // 128
NTT = 4
TT = 512

_ENGS = ("PE", "ACT", "DVE", "POOL", "SP")
_AUDIT = False
_DBG_NO_BC = False
_DBG_NO_BIAS = False
_DBG_NO_W1 = False
_TRUST_SAME = {"PE": True, "ACT": False, "DVE": False, "POOL": False,
               "SP": True}

CONST_VALS = [1e-5, 1e-6, 1.1920929e-07, 0.5, 1.0, -1.0]



NDMACLS = 6


class Prog:
    def __init__(self, nc):
        self.nc = nc
        self.items = {e: [] for e in _ENGS}
        self.state = {}
        self.ndma = {"SPQ": 0, "PQ": 0, "AQ": 0}
        self.marked = {e: set() for e in _ENGS}
        self.water = {e: {} for e in _ENGS}
        self.fence = {}
        self.tags = {}

    def _record(self, eng, emitfn, reads, writes, ev):
        waits = []
        wm = self.water[eng]

        def need(dep):
            if dep is None:
                return
            if dep[0] == "E":
                deng, didx = dep[1], dep[2]
                if deng == eng and _TRUST_SAME[eng]:
                    return
                k = ("E", deng)
                if wm.get(k, -1) >= didx:
                    return
                wm[k] = didx
                waits.append(dep)
                self.marked[deng].add(didx)
            else:
                q, cls, n = dep[1], dep[2], dep[3]
                k = ("D", q, cls)
                if wm.get(k, -1) >= n:
                    return
                wm[k] = n
                waits.append(dep)

        for r in reads:
            st = self.state.get(r)
            if st is not None:
                need(st[0])
        for w in writes:
            st = self.state.get(w)
            if st is not None:
                need(st[0])
                for rd in st[1]:
                    need(rd)
            elif w[0] in self.fence:
                for dep in self.fence[w[0]]:
                    need(dep)
        for r in reads:
            self.state.setdefault(r, [None, []])[1].append(ev)
        for w in writes:
            self.state[w] = [ev, []]
        self.items[eng].append((emitfn, waits, ev))

    def op(self, eng, emitfn, reads=(), writes=()):
        ev = ("E", eng, len(self.items[eng]))
        self._record(eng, emitfn, reads, writes, ev)
        if _AUDIT:
            f = _sys._getframe(1)
            self.tags.setdefault(eng, {})[ev[2]] = (
                f"{f.f_code.co_name}:{f.f_lineno}", list(reads), list(writes))
        return ev

    def dma(self, q, dst_ap, src_ap, reads=(), writes=(), **kw):
        eng = {"SPQ": "SP", "PQ": "POOL", "AQ": "ACT"}[q]
        i = self.ndma[q]
        self.ndma[q] += 1
        cls = i % NDMACLS
        n = i // NDMACLS + 1
        ev = ("D", q, cls, n)
        if _AUDIT:
            f = _sys._getframe(1)
            self.tags.setdefault(eng, {})[len(self.items[eng])] = (
                f"dma {q} {f.f_code.co_name}:{f.f_lineno}", list(reads), list(writes))

        def emitfn(e, _d=dst_ap, _s=src_ap, _q=q, _cls=cls, _n=n, _kw=kw):
            if _n > 1:
                e.wait_ge(self.qsem[(_q, _cls)], 16 * (_n - 1))
            return e.dma_start(_d, _s, **_kw).then_inc(self.qsem[(_q, _cls)], 16)

        self._record(eng, emitfn, reads, writes, ev)
        return ev

    def inherit(self, newkey, oldkey):
        st = self.state.pop(oldkey, None)
        if st is not None:
            cur = self.state.setdefault(newkey, [None, []])
            if st[0] is not None:
                cur[1].append(st[0])
            cur[1].extend(st[1])

    def emit(self, block):
        self.qsem = {}
        self._semguards = []
        for q in ("SPQ", "PQ", "AQ"):
            for cls in range(min(NDMACLS, self.ndma[q])):
                g = self.nc.semaphore(f"q_{q}{cls}")
                self.qsem[(q, cls)] = g.__enter__()
                self._semguards.append(g)
        self.esem = {}
        for e in _ENGS:
            if self.marked[e]:
                g = self.nc.semaphore(f"e_{e}")
                self.esem[e] = g.__enter__()
                self._semguards.append(g)
        markpos = {e: sorted(self.marked[e]) for e in _ENGS}
        final = dict(self.ndma)

        cb = getattr(self, "const_barrier", 0)

        def emit_stream(engobj, eng):
            ms = self.marked[eng]
            if cb and eng != "SP":
                for cls in range(min(NDMACLS, cb)):
                    tot = cb // NDMACLS + (1 if cls < cb % NDMACLS else 0)
                    engobj.wait_ge(self.qsem[("SPQ", cls)], 16 * tot)
            for emitfn, waits, ev in self.items[eng]:
                for dep in waits:
                    if dep[0] == "E":
                        v = bisect.bisect_right(markpos[dep[1]], dep[2])
                        engobj.wait_ge(self.esem[dep[1]], v)
                    else:
                        engobj.wait_ge(self.qsem[(dep[1], dep[2])], 16 * dep[3])
                ins = emitfn(engobj)
                if ev[0] == "E" and ev[2] in ms:
                    ins.then_inc(self.esem[eng], 1)
            if eng == "SP":
                for q, n in final.items():
                    for cls in range(min(NDMACLS, n)):
                        tot = n // NDMACLS + (1 if cls < n % NDMACLS else 0)
                        engobj.wait_ge(self.qsem[(q, cls)], 16 * tot)

        secs = {"PE": block.tensor, "ACT": block.scalar, "DVE": block.vector,
                "POOL": block.gpsimd, "SP": block.sync}
        for e in _ENGS:
            if self.items[e] or e == "SP":
                secs[e](lambda engobj, _e=e: emit_stream(engobj, _e))


def rk(t, *sub):
    return (t.name,) + sub



class Ctx:
    def __init__(self):
        self.nc = bass.Bass("TRN2", target_bir_lowering=False, debug=False)
        self.P = Prog(self.nc)
        self.uid = 0
        self._psum_guards = {}

    def u(self, pfx):
        self.uid += 1
        return f"{pfx}_{self.uid}"

    def sb(self, name, shape, dtype):
        return self.nc.alloc_sbuf_tensor(name, shape, dtype)

    def psum(self, name, shape, dtype=F32):
        g = self.nc.psum_tensor(name, shape, dtype)
        h = g.__enter__()
        self._psum_guards[name] = (g, h)
        return h

    def psum_free(self, h):
        g, _ = self._psum_guards.pop(h.name)
        g.__exit__(None, None, None)

    def init_scratch(self):
        n = self.nc.sbuf_bytes_remaining - 512
        self.scr_bytes = n
        h = self.nc.alloc_sbuf_tensor("scratch_arena", [128, n], mybir.dt.uint8)
        self.scr_base = self.nc.lookup_mloc(h).addr
        self.scr_off = 0
        self.scr_live = []

    def scr(self, pfx, shape, dtype):
        import numpy as _np
        per = int(_np.prod(shape[1:])) * mybir.dt.size(dtype)
        off = (self.scr_off + 31) & ~31
        assert off + per <= self.scr_bytes, (
            f"scratch overflow: {pfx} needs {per}B at {off}, have {self.scr_bytes}")
        nm = self.u(pfx)
        h = self.nc.alloc_sbuf_tensor_at(nm, list(shape), dtype,
                                         offset=self.scr_base + off)
        self.scr_off = off + per
        self.scr_live.append((h.name, off))
        if getattr(self, "_scr_residue", None):
            self.P.fence[h.name] = list(self._scr_residue)
        return h

    def scr_mark(self):
        return (self.scr_off, len(self.scr_live))

    def scr_release(self, mark):
        off, nlive = mark
        released = self.scr_live[nlive:]
        self.scr_live = self.scr_live[:nlive]
        self.scr_off = off
        names = {nm for nm, _ in released}
        res = {}
        P = self.P
        for key in [k for k in P.state if k[0] in names]:
            st = P.state.pop(key)
            evs = ([st[0]] if st[0] is not None else []) + st[1]
            for ev in evs:
                kk = (ev[0], ev[1])
                if kk not in res or res[kk][2] < ev[2]:
                    res[kk] = ev
        for nm in names:
            P.fence.pop(nm, None)
        old = getattr(self, "_scr_residue", [])
        merged = {}
        for ev in list(res.values()) + list(old):
            kk = (ev[0], ev[1])
            if kk not in merged or merged[kk][2] < ev[2]:
                merged[kk] = ev
        self._scr_residue = list(merged.values())

    def setup(self):
        nc, P = self.nc, self.P
        c_of = nc.inline_tensor(np.ones((128, 128), np.float32), "c_ones_f")
        c_ob = nc.inline_tensor(np.ones((128, 128), ml_dtypes.bfloat16), "c_ones_b")
        c_if = nc.inline_tensor(np.eye(128, dtype=np.float32), "c_id_f")
        c_ib = nc.inline_tensor(np.eye(128, dtype=ml_dtypes.bfloat16), "c_id_b")
        c_orb = nc.inline_tensor(np.ones((1, 512), ml_dtypes.bfloat16), "c_orow_b")
        cvnp = np.tile(np.array(CONST_VALS, np.float32)[None, :], (128, 1))
        c_cv = nc.inline_tensor(np.ascontiguousarray(cvnp), "c_vals")
        self.ones_f = self.sb("ones_f", [128, 128], F32)
        self.ones_b = self.sb("ones_b", [128, 128], BF16)
        self.id_f = self.sb("id_f", [128, 128], F32)
        self.id_b = self.sb("id_b", [128, 128], BF16)
        self.ones_row_b = self.sb("ones_row_b", [1, 512], BF16)
        self.cvals = self.sb("cvals", [128, len(CONST_VALS)], F32)
        for dst, src in ((self.ones_f, c_of), (self.ones_b, c_ob),
                         (self.id_f, c_if), (self.id_b, c_ib),
                         (self.ones_row_b, c_orb), (self.cvals, c_cv)):
            P.dma("SPQ", dst[:, :], src[:, :], writes=[rk(dst)])
        for i, v in enumerate(CONST_VALS):
            nc.const_aps.register(F32, v, self.cvals[:, i:i + 1])
        self.P.const_barrier = self.P.ndma["SPQ"]
        self.bank = [self.psum(f"bank{i}", [128, 512], F32) for i in range(8)]
        self.rowA = self.sb("rowA", [1, T], F32)
        self.rowB = self.sb("rowB", [1, T], F32)
        self.sq = self.sb("sqtmp", [128, 512], F32)
        self.wstream = self.sb("wstream", [128, 2, 16, 128], BF16)
        self.wstream2 = self.sb("wstream2", [128, 2, 8, 128], BF16)
        self.ftmp = self.sb("ftmp", [128, 512], F32)
        self.bc_sb = self.sb("bc_sb", [128, 512], F32)


    def bcast_row(self, row, coff, bank, n=TT, base=0, dtype_b=False):
        P = self.P
        ones = self.ones_b if dtype_b else self.ones_f
        P.op("PE", lambda e: e.matmul(
            bank[:, 0:n], ones[base:base + 1, :], row[base:base + 1, coff:coff + n],
            start=True, stop=True),
            reads=[rk(row), rk(ones)], writes=[rk(bank)])

    def tm_cols_to_row(self, tm, col, row, coff, bank, n=TT):
        P = self.P
        for w in range((n + 511) // 512):
            nj = min(4, (n - w * 512) // 128)
            for j in range(nj):
                P.op("PE", lambda e, j=j, w=w: e.matmul(
                    bank[0:1, j * 128:(j + 1) * 128],
                    tm[:, col + w * 4 + j:col + w * 4 + j + 1],
                    self.id_f[:, :], start=True, stop=True,
                    skip_group_check=True),
                    reads=[rk(tm), rk(self.id_f)], writes=[rk(bank)])
            P.op("ACT", lambda e, w=w, nj=nj: e.copy(
                row[0:1, coff + w * 512:coff + w * 512 + nj * 128],
                bank[0:1, 0:nj * 128]),
                reads=[rk(bank)], writes=[rk(row)])

    def row_to_tm_cols(self, row, coff, tm, col, bank, n=TT, scale=None):
        P = self.P
        nj = n // 128
        for j in range(nj):
            P.op("PE", lambda e, j=j: e.matmul(
                bank[:, j:j + 1], row[0:1, coff + j * 128:coff + (j + 1) * 128],
                self.ones_f[0:1, 0:1], start=True, stop=True,
                skip_group_check=True),
                reads=[rk(row), rk(self.ones_f)], writes=[rk(bank)])
        if scale is None:
            P.op("ACT", lambda e: e.copy(tm[:, col:col + nj], bank[:, 0:nj]),
                 reads=[rk(bank)], writes=[rk(tm)])
        else:
            P.op("ACT", lambda e: e.activation(tm[:, col:col + nj], bank[:, 0:nj],
                                               AF.Copy, scale=scale),
                 reads=[rk(bank)], writes=[rk(tm)])



def ln_block(ctx, x_sb, xn_bf, router_w=None, router_E=8, lg_tm=None):
    P = ctx.P
    bank = ctx.bank
    stA, stB, pcol, pbc0, pbc1 = (bank[0], bank[1], bank[2], bank[3], bank[4])
    prow = pcol
    jbanks = [bank[0], bank[1], bank[5], bank[6]]
    _mk = ctx.scr_mark()
    m_tm = ctx.scr("m_tm", [128, 16], F32)
    r_tm = ctx.scr("r_tm", [128, 16], F32)

    for tt in range(NTT):
        tsl = slice(tt * TT, (tt + 1) * TT)
        for c in range(NCH):
            P.op("ACT", lambda e, c=c, tsl=tsl: e.activation(
                ctx.sq[:, :], x_sb[:, c, tsl], AF.Square),
                reads=[rk(x_sb, c)], writes=[rk(ctx.sq)])
            P.op("PE", lambda e, c=c, tsl=tsl: e.matmul(
                stA[0:1, :], ctx.ones_f[:, 0:1], x_sb[:, c, tsl],
                start=(c == 0), stop=(c == NCH - 1), skip_group_check=True),
                reads=[rk(x_sb, c), rk(ctx.ones_f)], writes=[rk(stA)])
            P.op("PE", lambda e, c=c: e.matmul(
                stB[0:1, :], ctx.ones_f[:, 0:1], ctx.sq[:, :],
                start=(c == 0), stop=(c == NCH - 1), skip_group_check=True),
                reads=[rk(ctx.sq), rk(ctx.ones_f)], writes=[rk(stB)])
        P.op("ACT", lambda e, tsl=tsl: e.copy(ctx.rowA[0:1, tsl], stA[0:1, :]),
             reads=[rk(stA)], writes=[rk(ctx.rowA)])
        P.op("ACT", lambda e, tsl=tsl: e.copy(ctx.rowB[0:1, tsl], stB[0:1, :]),
             reads=[rk(stB)], writes=[rk(ctx.rowB)])
    ctx.row_to_tm_cols(ctx.rowA, 0, m_tm, 0, pcol, n=T, scale=1.0 / D)
    ctx.row_to_tm_cols(ctx.rowB, 0, r_tm, 0, pcol, n=T, scale=1.0 / D)
    P.op("DVE", lambda e: e.scalar_tensor_tensor(
        ctx.sq[:, 0:16], m_tm[:, :], 1.0, m_tm[:, :], ALU.mult, ALU.mult),
        reads=[rk(m_tm)], writes=[rk(ctx.sq)])
    P.op("DVE", lambda e: e.tensor_tensor(r_tm[:, :], r_tm[:, :],
                                          ctx.sq[:, 0:16], ALU.subtract),
         reads=[rk(r_tm), rk(ctx.sq)], writes=[rk(r_tm)])
    P.op("ACT", lambda e: e.activation(r_tm[:, :], r_tm[:, :], AF.Ln, bias=1e-5),
         reads=[rk(r_tm)], writes=[rk(r_tm)])
    P.op("ACT", lambda e: e.activation(r_tm[:, :], r_tm[:, :], AF.Exp, scale=-0.5),
         reads=[rk(r_tm)], writes=[rk(r_tm)])
    P.op("DVE", lambda e: e.tensor_tensor(m_tm[:, :], m_tm[:, :], r_tm[:, :],
                                          ALU.mult),
         reads=[rk(m_tm), rk(r_tm)], writes=[rk(m_tm)])
    ctx.tm_cols_to_row(r_tm, 0, ctx.rowA, 0, prow, n=T)
    ctx.tm_cols_to_row(m_tm, 0, ctx.rowB, 0, prow, n=T)
    for tt in range(NTT):
        tsl = slice(tt * TT, (tt + 1) * TT)
        ctx.bcast_row(ctx.rowA, tt * TT, pbc0)
        ctx.bcast_row(ctx.rowB, tt * TT, pbc1)
        for c in range(NCH):
            P.op("DVE", lambda e, c=c, tsl=tsl: e.tensor_tensor(
                ctx.ftmp[:, :], x_sb[:, c, tsl], pbc0[:, :], ALU.mult),
                reads=[rk(x_sb, c), rk(pbc0)], writes=[rk(ctx.ftmp)])
            if router_w is None:
                P.op("DVE", lambda e, c=c, tsl=tsl: e.tensor_tensor(
                    xn_bf[:, c, tsl], ctx.ftmp[:, :], pbc1[:, :], ALU.subtract),
                    reads=[rk(ctx.ftmp), rk(pbc1)], writes=[rk(xn_bf, c)])
            else:
                P.op("DVE", lambda e, c=c, tsl=tsl: e.tensor_tensor(
                    ctx.ftmp[:, :], ctx.ftmp[:, :], pbc1[:, :], ALU.subtract),
                    reads=[rk(ctx.ftmp), rk(pbc1)], writes=[rk(ctx.ftmp)])
                P.op("DVE", lambda e, c=c, tsl=tsl: e.tensor_copy(
                    xn_bf[:, c, tsl], ctx.ftmp[:, :]),
                    reads=[rk(ctx.ftmp)], writes=[rk(xn_bf, c)])
                for j in range(4):
                    jj = tt * 4 + j
                    P.op("PE", lambda e, c=c, j=j, jj=jj: e.matmul(
                        jbanks[j][:, 0:router_E],
                        ctx.ftmp[:, j * 128:(j + 1) * 128], router_w[:, c, :],
                        start=(c == 0), stop=(c == NCH - 1),
                        skip_group_check=True),
                        reads=[rk(ctx.ftmp), rk(router_w)], writes=[rk(jbanks[j])])
        if router_w is not None:
            for j in range(4):
                jj = tt * 4 + j
                P.op("ACT", lambda e, j=j, jj=jj: e.copy(
                    lg_tm[:, jj, :], jbanks[j][:, 0:router_E]),
                    reads=[rk(jbanks[j])], writes=[rk(lg_tm, jj)])
    ctx.scr_release(_mk)
    return lg_tm


def s2d(ap3, j, e):
    return ap3[:, j, e:e + 1]



def proj_stream(ctx, wsrc_fn, nm, nk, rhs_fn, evac_fn, tts=(0, 1, 2, 3),
                wkey="w", extra_mm=None, banks=(6, 7)):
    P = ctx.P
    wb = ctx.wstream
    for mb in range(nm):
        b = mb % 2
        src = wsrc_fn(mb)
        q = "PQ"
        if nk > 8:
            half = nk // 2
            P.dma(q, wb[:, b, 0:half, :], src[:, 0:half, :], writes=[rk(wb, b)])
            P.dma(q, wb[:, b, half:nk, :], src[:, half:nk, :], writes=[rk(wb, b)])
        else:
            P.dma(q, wb[:, b, 0:nk, :], src, writes=[rk(wb, b)])
        for ti, tt in enumerate(tts):
            bank = ctx.bank[banks[(mb * len(tts) + ti) % len(banks)]]
            for c in range(nk):
                rhs_ap, rreads = rhs_fn(c, tt)
                P.op("PE", lambda e, mb=mb, b=b, c=c, rhs_ap=rhs_ap, bank=bank: e.matmul(
                    bank[:, :], wb[:, b, c, :], rhs_ap,
                    start=(c == 0), stop=(c == nk - 1 and extra_mm is None),
                    skip_group_check=True),
                    reads=[rk(wb, b)] + rreads, writes=[rk(bank)])
            if extra_mm is not None:
                extra_mm(mb, tt, bank)
            evac_fn(mb, tt, bank)



def l0_attn(ctx, hb, x_sb, xn_bf):
    nc, P = ctx.nc, ctx.P
    bank = ctx.bank

    _mkF = ctx.scr_mark()
    t_f32 = ctx.scr("l0_t", [128, 4, T], F32)
    _mkM = ctx.scr_mark()
    mp_bf = ctx.scr("l0_mp", [128, NCH, 512], BF16)

    _mkA = ctx.scr_mark()
    wv_sb = ctx.scr("l0_wv", [1, 2048], F32)
    P.dma("SPQ", wv_sb[:, :], hb["L0_attn_W_V"].ap(), writes=[rk(wv_sb)])
    uch = ctx.scr("l0_uch", [128, 2048], F32)
    tpch = ctx.scr("l0_tpch", [128, 512], F32)
    wu_r = hb["L0_attn_W_U"].ap().rearrange("(c p) f -> c p f", p=128)
    wtp_r = hb["L0_attn_W_toP"].ap().rearrange("(c p) m -> c p m", p=128)
    t_a = ctx.scr("l0_ta", [128, 128], F32)
    t_b = ctx.scr("l0_tb", [128, 128], F32)
    for c in range(NCH):
        P.dma("SPQ", uch[:, :], wu_r[c], writes=[rk(uch)])
        P.dma("SPQ", tpch[:, :], wtp_r[c], writes=[rk(tpch)])
        for g in range(4):
            gsl = slice(g * 512, (g + 1) * 512)
            ctx.bcast_row(wv_sb, g * 512, bank[3])
            P.op("DVE", lambda e, gsl=gsl: e.tensor_tensor(
                ctx.ftmp[:, :], uch[:, gsl], bank[3][:, :], ALU.mult),
                reads=[rk(uch), rk(bank[3])], writes=[rk(ctx.ftmp)])
            P.op("DVE", lambda e: e.tensor_tensor(
                t_a[:, :], ctx.ftmp[:, 0:512:4], ctx.ftmp[:, 1:512:4], ALU.add),
                reads=[rk(ctx.ftmp)], writes=[rk(t_a)])
            P.op("DVE", lambda e: e.tensor_tensor(
                t_b[:, :], ctx.ftmp[:, 2:512:4], ctx.ftmp[:, 3:512:4], ALU.add),
                reads=[rk(ctx.ftmp)], writes=[rk(t_b)])
            P.op("DVE", lambda e: e.tensor_tensor(
                t_a[:, :], t_a[:, :], t_b[:, :], ALU.add),
                reads=[rk(t_a), rk(t_b)], writes=[rk(t_a)])
            P.op("DVE", lambda e, c=c, g=g: e.scalar_tensor_tensor(
                mp_bf[:, c, g * 128:(g + 1) * 128], t_a[:, :], 1.0,
                tpch[:, g * 128:(g + 1) * 128], ALU.mult, ALU.add),
                reads=[rk(t_a), rk(tpch)], writes=[rk(mp_bf, c)])

    ctx.scr_release(_mkA)
    if "mp" in ctx.dbg:
        P.dma("SPQ", ctx.dbg["mp"].ap(), mp_bf[:, :, :],
              reads=[rk(mp_bf, c) for c in range(NCH)])

    for mo in range(4):
        for tt in range(NTT):
            tsl = slice(tt * TT, (tt + 1) * TT)
            bk = bank[6 + (mo * NTT + tt) % 2]
            for c in range(NCH):
                P.op("PE", lambda e, mo=mo, c=c, tsl=tsl, bk=bk: e.matmul(
                    bk[:, :], mp_bf[:, c, mo * 128:(mo + 1) * 128], xn_bf[:, c, tsl],
                    start=(c == 0), stop=(c == NCH - 1), skip_group_check=True),
                    reads=[rk(mp_bf, c), rk(xn_bf, c)], writes=[rk(bk)])
            P.op("ACT", lambda e, mo=mo, tsl=tsl, bk=bk: e.copy(
                t_f32[:, mo, tsl], bk[:, :]),
                reads=[rk(bk)], writes=[rk(t_f32, mo)])

    ctx.scr_release(_mkM)
    s_sb = ctx.scr("l0_s", [64, T], F32)
    bmat_sb = ctx.scr("l0_bmat", [128, 4, 64], F32)
    wlam_sb0 = ctx.scr("l0_wlam", [128, 4, 64], F32)
    P.dma("SPQ", bmat_sb[:, :, :],
          hb["L0_attn_B_mat"].ap().rearrange("(c p) r -> p c r", p=128),
          writes=[rk(bmat_sb)])
    _mkC = ctx.scr_mark()
    u_sb = ctx.scr("l0_u", [64, T], F32)
    lam_sb = ctx.scr("l0_lam", [64, T], F32)
    for tt in range(NTT):
        tsl = slice(tt * TT, (tt + 1) * TT)
        for mo in range(4):
            P.op("PE", lambda e, mo=mo, tsl=tsl, tt=tt: e.matmul(
                bank[tt][0:64, :], bmat_sb[:, mo, :], t_f32[:, mo, tsl],
                start=(mo == 0), stop=(mo == 3), skip_group_check=True),
                reads=[rk(bmat_sb), rk(t_f32, mo)], writes=[rk(bank[tt])])
        P.op("ACT", lambda e, tsl=tsl, tt=tt: e.copy(u_sb[:, tsl], bank[tt][0:64, :]),
             reads=[rk(bank[tt])], writes=[rk(u_sb)])

    wlam_sb = wlam_sb0
    P.dma("SPQ", wlam_sb[:, :, :],
          hb["L0_attn_W_lam"].ap().rearrange("(c p) r -> p c r", p=128),
          writes=[rk(wlam_sb)])
    fg_sb = ctx.scr("l0_fg", [128, T], F32)
    wf_r = hb["L0_attn_W_F"].ap()

    def wf_src(mb):
        return wf_r[mb]

    def fg_rhs(c, tt):
        return xn_bf[:, c, tt * TT:(tt + 1) * TT], [rk(xn_bf, c)]

    for mo in range(4):
        def fg_evac(mb, tt, bk, mo=mo):
            tsl = slice(tt * TT, (tt + 1) * TT)
            P.op("ACT", lambda e, tsl=tsl, bk=bk: e.activation(
                fg_sb[:, tsl], bk[:, :], AF.Sigmoid),
                reads=[rk(bk)], writes=[rk(fg_sb)])
        proj_stream(ctx, lambda _mb, mo=mo: wf_src(mo), 1, NCH, fg_rhs, fg_evac,
                    wkey=f"wf{mo}")
        for tt in range(NTT):
            tsl = slice(tt * TT, (tt + 1) * TT)
            P.op("PE", lambda e, mo=mo, tsl=tsl, tt=tt: e.matmul(
                bank[tt][0:64, :], wlam_sb[:, mo, :], fg_sb[:, tsl],
                start=(mo == 0), stop=(mo == 3), skip_group_check=True),
                reads=[rk(wlam_sb), rk(fg_sb)], writes=[rk(bank[tt])])
            if mo == 3:
                P.op("ACT", lambda e, tsl=tsl, tt=tt: e.activation(
                    lam_sb[:, tsl], bank[tt][0:64, :], AF.Sigmoid),
                    reads=[rk(bank[tt])], writes=[rk(lam_sb)])

    if "t" in ctx.dbg:
        P.dma("SPQ", ctx.dbg["t"].ap(), t_f32[:, :, :],
              reads=[rk(t_f32, mo) for mo in range(4)])
    if "lam" in ctx.dbg:
        P.dma("SPQ", ctx.dbg["lam"].ap(), lam_sb[:, :], reads=[rk(lam_sb)])
    if "u" in ctx.dbg:
        P.dma("SPQ", ctx.dbg["u"].ap(), u_sb[:, :], reads=[rk(u_sb)])
    P.op("DVE", lambda e: e.tensor_tensor_scan(
        s_sb[:, :], lam_sb[:, :], u_sb[:, :], 0.0, ALU.mult, ALU.add),
        reads=[rk(lam_sb), rk(u_sb)], writes=[rk(s_sb)])
    ctx.scr_release(_mkC)

    if "s" in ctx.dbg:
        P.dma("SPQ", ctx.dbg["s"].ap(), s_sb[:, :], reads=[rk(s_sb)])
    cmat_sb = ctx.scr("l0_cmat", [64, 512], F32)
    P.dma("SPQ", cmat_sb[:, :], hb["L0_attn_C_mat"].ap(), writes=[rk(cmat_sb)])
    tt_bf = ctx.scr("l0_ttbf", [128, 4, T], BF16)
    for mo in range(4):
        for tt in range(NTT):
            tsl = slice(tt * TT, (tt + 1) * TT)
            bk = bank[6 + (mo * NTT + tt) % 2]
            P.op("PE", lambda e, mo=mo, tsl=tsl, bk=bk: e.matmul(
                bk[:, :], cmat_sb[:, mo * 128:(mo + 1) * 128], s_sb[:, tsl],
                start=True, stop=True),
                reads=[rk(cmat_sb), rk(s_sb)], writes=[rk(bk)])
            P.op("DVE", lambda e, mo=mo, tsl=tsl, bk=bk: e.scalar_tensor_tensor(
                tt_bf[:, mo, tsl], bk[:, :], 1.0, t_f32[:, mo, tsl],
                ALU.mult, ALU.add),
                reads=[rk(bk), rk(t_f32, mo)], writes=[rk(tt_bf, mo)])

    wfp_r = hb["L0_attn_W_fromP"].ap()

    def wfp_src(mb):
        return wfp_r[mb]

    def h_rhs(c, tt):
        return tt_bf[:, c, tt * TT:(tt + 1) * TT], [rk(tt_bf, c)]

    def h_evac(mb, tt, bk):
        tsl = slice(tt * TT, (tt + 1) * TT)
        P.op("DVE", lambda e, mb=mb, tsl=tsl, bk=bk: e.scalar_tensor_tensor(
            x_sb[:, mb, tsl], bk[:, :], 1.0, x_sb[:, mb, tsl], ALU.mult, ALU.add),
            reads=[rk(bk), rk(x_sb, mb)], writes=[rk(x_sb, mb)])

    if "ttbf" in ctx.dbg:
        P.dma("SPQ", ctx.dbg["ttbf"].ap(), tt_bf[:, :, :],
              reads=[rk(tt_bf, mo) for mo in range(4)])
    proj_stream(ctx, wfp_src, NCH, 4, h_rhs, h_evac, wkey="wfp")
    ctx.scr_release(_mkF)



def topk_combine_l0(ctx, lg_tm, c_tm):
    P = ctx.P
    _mk = ctx.scr_mark()
    m1 = ctx.scr("m1", [128, 16], F32)
    m2 = ctx.scr("m2", [128, 16], F32)
    ind1 = ctx.scr("ind1", [128, 16, 8], F32)
    ind2 = ctx.scr("ind2", [128, 16, 8], F32)
    l2t = ctx.scr("l2t", [128, 16, 8], F32)
    P.op("DVE", lambda e: e.reduce_max(m1[:, :], lg_tm[:, :, :], axis=AX.X),
         reads=[rk(lg_tm, j) for j in range(16)], writes=[rk(m1)])
    for j in range(16):
        P.op("DVE", lambda e, j=j: e.tensor_single_scalar(
            ind1[:, j, :], lg_tm[:, j, :], m1[:, j:j + 1], ALU.is_ge),
            reads=[rk(lg_tm, j), rk(m1)], writes=[rk(ind1, j)])
        P.op("DVE", lambda e, j=j: e.scalar_tensor_tensor(
            l2t[:, j, :], ind1[:, j, :], -1e30, lg_tm[:, j, :], ALU.mult, ALU.add),
            reads=[rk(ind1, j), rk(lg_tm, j)], writes=[rk(l2t, j)])
    P.op("DVE", lambda e: e.reduce_max(m2[:, :], l2t[:, :, :], axis=AX.X),
         reads=[rk(l2t, j) for j in range(16)],
         writes=[rk(m2)])
    for j in range(16):
        P.op("DVE", lambda e, j=j: e.tensor_single_scalar(
            ind2[:, j, :], l2t[:, j, :], m2[:, j:j + 1], ALU.is_ge),
            reads=[rk(l2t, j), rk(m2)], writes=[rk(ind2, j)])
    w1 = ctx.scr("w1", [128, 16], F32)
    w2 = ctx.scr("w2", [128, 16], F32)
    P.op("DVE", lambda e: e.tensor_tensor(w1[:, :], m1[:, :], m2[:, :], ALU.subtract),
         reads=[rk(m1), rk(m2)], writes=[rk(w1)])
    P.op("ACT", lambda e: e.activation(w1[:, :], w1[:, :], AF.Sigmoid),
         reads=[rk(w1)], writes=[rk(w1)])
    P.op("DVE", lambda e: e.tensor_scalar(w2[:, :], w1[:, :], -1.0, 1.0,
                                          ALU.mult, ALU.add),
         reads=[rk(w1)], writes=[rk(w2)])
    for j in range(16):
        P.op("DVE", lambda e, j=j: e.tensor_single_scalar(
            l2t[:, j, :], ind1[:, j, :], w1[:, j:j + 1], ALU.mult),
            reads=[rk(ind1, j), rk(w1)], writes=[rk(l2t, j)])
        P.op("DVE", lambda e, j=j: e.scalar_tensor_tensor(
            c_tm[:, j, :], ind2[:, j, :], w2[:, j:j + 1], l2t[:, j, :],
            ALU.mult, ALU.add),
            reads=[rk(ind2, j), rk(w2), rk(l2t, j)], writes=[rk(c_tm, j)])
    ctx.scr_release(_mk)
    return c_tm


def moe_ffn(ctx, hb, x_sb, xn_bf, c_tm, pfx, w1name, b1name, w2name, b2name,
            act=AF.Gelu, shared=None, out_ln=None, nexp=None):
    nc, P = ctx.nc, ctx.P
    bank = ctx.bank
    E, nkh = hb[w1name].shape[0], hb[w1name].shape[1]
    H = nkh * 128
    _mk = ctx.scr_mark()
    h_sb = ctx.scr(pfx + "_h", [128, nkh, TT], BF16)
    acc = ctx.scr(pfx + "_acc", [128, NCH, TT], F32)
    w1_t = hb[w1name].ap()
    w2_r = hb[w2name].ap().rearrange("e (k p) (oc m) -> e p k oc m", p=128, m=128)
    b1col = ctx.scr(pfx + "_b1c", [128, E, nkh], F32)
    P.dma("SPQ", b1col[:, :, :], hb[b1name].ap(), writes=[rk(b1col)])
    b2row = ctx.scr(pfx + "_b2r", [1, E, D], BF16)
    P.dma("PQ", b2row[:, :, :], hb[b2name].ap(), writes=[rk(b2row)])
    if shared is not None:
        shw_t = hb[shared["w"]].ap()
        shb = ctx.scr(pfx + "_shb", [128, NCH], F32)
        P.dma("SPQ", shb[:, :], hb[shared["b"]].ap(), writes=[rk(shb)])
    if out_ln is not None:
        gcol = ctx.scr(pfx + "_g", [128, NCH], F32)
        bcol = ctx.scr(pfx + "_bc", [128, NCH], F32)
        P.dma("SPQ", gcol[:, :], hb[out_ln["g"]].ap(), writes=[rk(gcol)])
        P.dma("SPQ", bcol[:, :], hb[out_ln["b"]].ap(), writes=[rk(bcol)])
    w2buf = ctx.wstream2
    ftmp2 = ctx.scr(pfx + "_f2", [128, 512], F32)

    for tt in range(NTT):
        tsl = slice(tt * TT, (tt + 1) * TT)

        def xn_rhs(c, _tt, tsl=tsl):
            return xn_bf[:, c, tsl], [rk(xn_bf, c)]

        if shared is not None:
            def sh_evac(mb, _tt, bk, tsl=tsl):
                P.op("ACT", lambda e, mb=mb, bk=bk: e.activation(
                    ctx.ftmp[:, :], bk[:, :], AF.Gelu, bias=shb[:, mb:mb + 1]),
                    reads=[rk(bk), rk(shb)], writes=[rk(ctx.ftmp)])
                P.op("DVE", lambda e, mb=mb: e.tensor_single_scalar(
                    acc[:, mb, :], ctx.ftmp[:, :], 0.5, ALU.mult),
                    reads=[rk(ctx.ftmp)], writes=[rk(acc, mb)])
            proj_stream(ctx, lambda mb: shw_t[mb],
                        NCH, NCH, xn_rhs, sh_evac, tts=(tt,), wkey=f"sh{tt}")
        else:
            for mb in range(NCH):
                P.op("DVE", lambda e, mb=mb: e.memset(acc[:, mb, :], 0.0),
                     writes=[rk(acc, mb)])

        for ex in range(nexp if nexp is not None else E):
            for j in range(4):
                jj = tt * 4 + j
                P.op("PE", lambda e, j=j, jj=jj, ex=ex: e.matmul(
                    bank[2][0:1, j * 128:(j + 1) * 128], s2d(c_tm, jj, ex),
                    ctx.id_f[:, :], start=True, stop=True,
                    skip_group_check=True),
                    reads=[rk(c_tm, jj)], writes=[rk(bank[2])])
            P.op("ACT", lambda e: e.copy(ctx.bc_sb[0:1, :], bank[2][0:1, :]),
                 reads=[rk(bank[2])], writes=[rk(ctx.bc_sb)])
            P.op("PE", lambda e: e.matmul(
                bank[3][:, :], ctx.ones_f[0:1, :], ctx.bc_sb[0:1, :],
                start=True, stop=True),
                reads=[rk(ctx.bc_sb), rk(ctx.ones_f)], writes=[rk(bank[3])])
            P.op("ACT", lambda e: e.copy(ctx.bc_sb[:, :], bank[3][:, :]),
                 reads=[rk(bank[3])], writes=[rk(ctx.bc_sb)])

            if _DBG_NO_W1:
                for mb in range(nkh):
                    P.op("DVE", lambda e, mb=mb: e.memset(h_sb[:, mb, :], 0.01),
                         writes=[rk(h_sb, mb)])
            else:
                def h_evac(mb, _tt, bk, ex=ex):
                    P.op("ACT", lambda e, mb=mb, bk=bk, ex=ex: e.activation(
                        h_sb[:, mb, :], bk[:, :], act, bias=s2d(b1col, ex, mb)),
                        reads=[rk(bk), rk(b1col)], writes=[rk(h_sb, mb)])
                proj_stream(ctx, lambda mb, ex=ex: w1_t[ex, mb],
                            nkh, NCH, xn_rhs, h_evac, tts=(tt,), wkey=f"w1_{tt}_{ex}")

            for k in range(nkh):
                b = k % 2
                P.dma("PQ", w2buf[:, b, :, :], w2_r[ex][:, k, :, :],
                      writes=[rk(w2buf, b)])
                for o in range(NCH):
                    P.op("PE", lambda e, k=k, b=b, o=o: e.matmul(
                        bank[o][:, :], w2buf[:, b, o, :], h_sb[:, k, :],
                        start=(k == 0), stop=False, skip_group_check=True),
                        reads=[rk(w2buf, b), rk(h_sb, k)], writes=[rk(bank[o])])
            for o in range(NCH):
                if not _DBG_NO_BIAS:
                    P.op("PE", lambda e, o=o, ex=ex: e.matmul(
                        bank[o][:, :], b2row[0:1, ex, o * 128:(o + 1) * 128],
                        ctx.ones_row_b[0:1, :], start=False, stop=True,
                        skip_group_check=True),
                        reads=[rk(b2row), rk(ctx.ones_row_b)], writes=[rk(bank[o])])
                if _DBG_NO_BC:
                    P.op("DVE", lambda e, o=o: e.tensor_tensor(
                        acc[:, o, :], acc[:, o, :], bank[o][:, :], ALU.add),
                        reads=[rk(acc, o), rk(bank[o])], writes=[rk(acc, o)])
                else:
                    P.op("DVE", lambda e, o=o: e.tensor_tensor(
                        ftmp2[:, :], bank[o][:, :], ctx.bc_sb[:, :], ALU.mult),
                        reads=[rk(bank[o]), rk(ctx.bc_sb)], writes=[rk(ftmp2)])
                    P.op("DVE", lambda e, o=o: e.tensor_tensor(
                        acc[:, o, :], acc[:, o, :], ftmp2[:, :], ALU.add),
                        reads=[rk(acc, o), rk(ftmp2)], writes=[rk(acc, o)])

        if out_ln is not None:
            ln_acc_add(ctx, acc, x_sb, tt, gcol, bcol)
        else:
            for o in range(NCH):
                P.op("DVE", lambda e, o=o, tsl=tsl: e.tensor_tensor(
                    x_sb[:, o, tsl], x_sb[:, o, tsl], acc[:, o, :], ALU.add),
                    reads=[rk(x_sb, o), rk(acc, o)], writes=[rk(x_sb, o)])
    ctx.scr_release(_mk)


def moe_ffn_v2(ctx, hb, x_sb, xn_bf, c_tm, pfx, w1name, b1name, w2name, b2name,
               act=AF.Gelu, shared=None, out_ln=None, groups=((0, 1, 2, 3),)):
    nc, P = ctx.nc, ctx.P
    bank = ctx.bank
    E, nkh = hb[w1name].shape[0], hb[w1name].shape[1]
    NQ = nkh // 8
    _mk = ctx.scr_mark()
    w1_t = hb[w1name].ap()
    w2_r = hb[w2name].ap().rearrange("e (k p) (oc m) -> e p k oc m", p=128, m=128)
    b1col = ctx.scr(pfx + "_b1c", [128, E, nkh], F32)
    P.dma("SPQ", b1col[:, :, :], hb[b1name].ap(), writes=[rk(b1col)])
    if shared is not None:
        shw_t = hb[shared["w"]].ap()
        shb = ctx.scr(pfx + "_shb", [128, NCH], F32)
        P.dma("SPQ", shb[:, :], hb[shared["b"]].ap(), writes=[rk(shb)])
    if out_ln is not None:
        gcol = ctx.scr(pfx + "_g", [128, NCH], F32)
        bcol = ctx.scr(pfx + "_bc", [128, NCH], F32)
        P.dma("SPQ", gcol[:, :], hb[out_ln["g"]].ap(), writes=[rk(gcol)])
        P.dma("SPQ", bcol[:, :], hb[out_ln["b"]].ap(), writes=[rk(bcol)])
    b2row = ctx.scr(pfx + "_b2r", [1, 2, D], BF16)
    b2_ap = hb[b2name].ap()
    ntt = len(groups[0])
    TTG = TT * ntt
    use_acc = (out_ln is not None) or (shared is not None)
    h_q = ctx.scr(pfx + "_hq", [128, 8, TTG], BF16)
    bcg = ctx.scr(pfx + "_bcg", [128, ntt, TT], F32)
    acc = ctx.scr(pfx + "_acc", [128, NCH, TTG], F32) if use_acc else None
    ftmp2 = ctx.scr(pfx + "_f2", [128, 2, TT], F32)
    w2buf = ctx.wstream2

    def xn_rhs(c, tt):
        return xn_bf[:, c, tt * TT:(tt + 1) * TT], [rk(xn_bf, c)]

    for tg in groups:
        if use_acc:
            if shared is not None:
                def sh_evac(mb, tt, bk):
                    ti = tg.index(tt)
                    P.op("ACT", lambda e, mb=mb, bk=bk: e.activation(
                        ctx.ftmp[:, :], bk[:, :], AF.Gelu, bias=shb[:, mb:mb + 1]),
                        reads=[rk(bk), rk(shb)], writes=[rk(ctx.ftmp)])
                    P.op("DVE", lambda e, mb=mb, ti=ti: e.tensor_single_scalar(
                        acc[:, mb, ti * TT:(ti + 1) * TT], ctx.ftmp[:, :], 0.5,
                        ALU.mult),
                        reads=[rk(ctx.ftmp)], writes=[rk(acc, mb)])
                proj_stream(ctx, lambda mb: shw_t[mb], NCH, NCH, xn_rhs, sh_evac,
                            tts=tg)
            else:
                for mb in range(NCH):
                    P.op("DVE", lambda e, mb=mb: e.memset(acc[:, mb, :], 0.0),
                         writes=[rk(acc, mb)])

        for ex in range(E):
            P.dma("PQ", b2row[:, ex % 2, :], b2_ap[0:1, ex, :],
                  writes=[rk(b2row, ex % 2)])
            for ti, tt in enumerate(tg):
                for j in range(4):
                    jj = tt * 4 + j
                    P.op("PE", lambda e, j=j, jj=jj, ex=ex: e.matmul(
                        bank[4][0:1, j * 128:(j + 1) * 128], s2d(c_tm, jj, ex),
                        ctx.id_f[:, :], start=True, stop=True,
                        skip_group_check=True),
                        reads=[rk(c_tm, jj)], writes=[rk(bank[4])])
                P.op("ACT", lambda e: e.copy(ctx.bc_sb[0:1, :], bank[4][0:1, :]),
                     reads=[rk(bank[4])], writes=[rk(ctx.bc_sb)])
                P.op("PE", lambda e: e.matmul(
                    bank[5][:, :], ctx.ones_f[0:1, :], ctx.bc_sb[0:1, :],
                    start=True, stop=True),
                    reads=[rk(ctx.bc_sb), rk(ctx.ones_f)], writes=[rk(bank[5])])
                P.op("ACT", lambda e, ti=ti: e.copy(bcg[:, ti, :], bank[5][:, :]),
                     reads=[rk(bank[5])], writes=[rk(bcg, ti)])

            for fq in range(NQ):
                def h_evac(mb, tt, bk, ex=ex, fq=fq):
                    ti = tg.index(tt)
                    P.op("ACT", lambda e, mb=mb, bk=bk, ti=ti: e.activation(
                        h_q[:, mb, ti * TT:(ti + 1) * TT], bk[:, :], act,
                        bias=s2d(b1col, ex, fq * 8 + mb)),
                        reads=[rk(bk), rk(b1col)], writes=[rk(h_q, mb)])
                proj_stream(ctx, lambda mb, ex=ex, fq=fq: w1_t[ex, fq * 8 + mb],
                            8, NCH, xn_rhs, h_evac, tts=tg)
                for og in range(4):
                    for k8 in range(8):
                        k = fq * 8 + k8
                        b = k8 % 2
                        P.dma("PQ", w2buf[:, b, 0:2, :],
                              w2_r[ex][:, k, og * 2:og * 2 + 2, :],
                              writes=[rk(w2buf, b)])
                        for oi in range(2):
                            for ti in range(ntt):
                                last = (k8 == 7 and fq > 0)
                                P.op("PE", lambda e, k8=k8, b=b, oi=oi, ti=ti,
                                     last=last: e.matmul(
                                         bank[oi * ntt + ti][:, :],
                                         w2buf[:, b, oi, :],
                                         h_q[:, k8, ti * TT:(ti + 1) * TT],
                                         start=(k8 == 0), stop=last,
                                         skip_group_check=True),
                                    reads=[rk(w2buf, b), rk(h_q, k8)],
                                    writes=[rk(bank[oi * ntt + ti])])
                    for oi in range(2):
                        o = og * 2 + oi
                        for ti, tt in enumerate(tg):
                            bk = bank[oi * ntt + ti]
                            if fq == 0:
                                P.op("PE", lambda e, o=o, ex=ex, bk=bk: e.matmul(
                                    bk[:, :], b2row[0:1, ex % 2, o * 128:(o + 1) * 128],
                                    ctx.ones_row_b[0:1, :], start=False, stop=True,
                                    skip_group_check=True),
                                    reads=[rk(b2row, ex % 2), rk(ctx.ones_row_b)],
                                    writes=[rk(bk)])
                            eng = "DVE" if oi == 0 else "POOL"
                            P.op("DVE", lambda e, ti=ti, bk=bk, oi=oi: e.tensor_tensor(
                                ftmp2[:, oi, :], bk[:, :], bcg[:, ti, :], ALU.mult),
                                reads=[rk(bk), rk(bcg, ti)], writes=[rk(ftmp2, oi)])
                            if use_acc:
                                P.op(eng, lambda e, o=o, ti=ti, oi=oi: e.tensor_tensor(
                                    acc[:, o, ti * TT:(ti + 1) * TT],
                                    acc[:, o, ti * TT:(ti + 1) * TT],
                                    ftmp2[:, oi, :], ALU.add),
                                    reads=[rk(acc, o), rk(ftmp2, oi)],
                                    writes=[rk(acc, o)])
                            else:
                                P.op(eng, lambda e, o=o, tt=tt, oi=oi: e.tensor_tensor(
                                    x_sb[:, o, tt * TT:(tt + 1) * TT],
                                    x_sb[:, o, tt * TT:(tt + 1) * TT],
                                    ftmp2[:, oi, :], ALU.add),
                                    reads=[rk(x_sb, o), rk(ftmp2, oi)],
                                    writes=[rk(x_sb, o)])

        if use_acc:
            for ti, tt in enumerate(tg):
                if out_ln is not None:
                    ln_acc_add(ctx, acc, x_sb, tt, gcol, bcol, toff=ti * TT)
                else:
                    for o in range(NCH):
                        P.op("DVE", lambda e, o=o, tt=tt, ti=ti: e.tensor_tensor(
                            x_sb[:, o, tt * TT:(tt + 1) * TT],
                            x_sb[:, o, tt * TT:(tt + 1) * TT],
                            acc[:, o, ti * TT:(ti + 1) * TT], ALU.add),
                            reads=[rk(x_sb, o), rk(acc, o)], writes=[rk(x_sb, o)])
    ctx.scr_release(_mk)


def ln_acc_add(ctx, acc, x_sb, tt, gcol, bcol, toff=0):
    P = ctx.P
    bank = ctx.bank
    stA, stB, pcol, pbc0, pbc1 = bank[0], bank[1], bank[2], bank[3], bank[4]
    prow = pcol
    tsl = slice(tt * TT, (tt + 1) * TT)
    for c in range(NCH):
        P.op("ACT", lambda e, c=c: e.activation(ctx.sq[:, :], acc[:, c, toff:toff + TT],
                                                AF.Square),
             reads=[rk(acc, c)], writes=[rk(ctx.sq)])
        P.op("PE", lambda e, c=c: e.matmul(
            stA[0:1, :], ctx.ones_f[:, 0:1], acc[:, c, toff:toff + TT],
            start=(c == 0), stop=(c == NCH - 1), skip_group_check=True),
            reads=[rk(acc, c), rk(ctx.ones_f)], writes=[rk(stA)])
        P.op("PE", lambda e, c=c: e.matmul(
            stB[0:1, :], ctx.ones_f[:, 0:1], ctx.sq[:, :],
            start=(c == 0), stop=(c == NCH - 1), skip_group_check=True),
            reads=[rk(ctx.sq), rk(ctx.ones_f)], writes=[rk(stB)])
    P.op("ACT", lambda e: e.copy(ctx.rowA[0:1, 0:TT], stA[0:1, :]),
         reads=[rk(stA)], writes=[rk(ctx.rowA)])
    P.op("ACT", lambda e: e.copy(ctx.rowB[0:1, 0:TT], stB[0:1, :]),
         reads=[rk(stB)], writes=[rk(ctx.rowB)])
    _mk2 = ctx.scr_mark()
    m_tm = ctx.scr("lnm", [128, 4], F32)
    r_tm = ctx.scr("lnr", [128, 4], F32)
    ctx.row_to_tm_cols(ctx.rowA, 0, m_tm, 0, pcol, n=TT, scale=1.0 / D)
    ctx.row_to_tm_cols(ctx.rowB, 0, r_tm, 0, pcol, n=TT, scale=1.0 / D)
    P.op("DVE", lambda e: e.scalar_tensor_tensor(
        ctx.sq[:, 0:4], m_tm[:, :], 1.0, m_tm[:, :], ALU.mult, ALU.mult),
        reads=[rk(m_tm)], writes=[rk(ctx.sq)])
    P.op("DVE", lambda e: e.tensor_tensor(r_tm[:, :], r_tm[:, :], ctx.sq[:, 0:4],
                                          ALU.subtract),
         reads=[rk(r_tm), rk(ctx.sq)], writes=[rk(r_tm)])
    P.op("ACT", lambda e: e.activation(r_tm[:, :], r_tm[:, :], AF.Ln, bias=1e-5),
         reads=[rk(r_tm)], writes=[rk(r_tm)])
    P.op("ACT", lambda e: e.activation(r_tm[:, :], r_tm[:, :], AF.Exp, scale=-0.5),
         reads=[rk(r_tm)], writes=[rk(r_tm)])
    P.op("DVE", lambda e: e.tensor_tensor(m_tm[:, :], m_tm[:, :], r_tm[:, :],
                                          ALU.mult),
         reads=[rk(m_tm), rk(r_tm)], writes=[rk(m_tm)])
    ctx.tm_cols_to_row(r_tm, 0, ctx.rowA, 0, prow, n=TT)
    ctx.tm_cols_to_row(m_tm, 0, ctx.rowB, 0, prow, n=TT)
    ctx.bcast_row(ctx.rowA, 0, pbc0)
    ctx.bcast_row(ctx.rowB, 0, pbc1)
    for c in range(NCH):
        P.op("DVE", lambda e, c=c: e.tensor_tensor(
            ctx.ftmp[:, :], acc[:, c, toff:toff + TT], pbc0[:, :], ALU.mult),
            reads=[rk(acc, c), rk(pbc0)], writes=[rk(ctx.ftmp)])
        P.op("DVE", lambda e, c=c: e.tensor_tensor(
            ctx.ftmp[:, :], ctx.ftmp[:, :], pbc1[:, :], ALU.subtract),
            reads=[rk(ctx.ftmp), rk(pbc1)], writes=[rk(ctx.ftmp)])
        P.op("DVE", lambda e, c=c: e.tensor_scalar(
            ctx.ftmp[:, :], ctx.ftmp[:, :], gcol[:, c:c + 1], bcol[:, c:c + 1],
            ALU.mult, ALU.add),
            reads=[rk(ctx.ftmp), rk(gcol), rk(bcol)], writes=[rk(ctx.ftmp)])
        P.op("DVE", lambda e, c=c, tsl=tsl: e.tensor_tensor(
            x_sb[:, c, tsl], x_sb[:, c, tsl], ctx.ftmp[:, :], ALU.add),
            reads=[rk(x_sb, c), rk(ctx.ftmp)], writes=[rk(x_sb, c)])
    ctx.scr_release(_mk2)



def _tile_lhsT(w, m=128):
    Dd, M = w.shape
    c = Dd // 128
    return np.ascontiguousarray(
        w.reshape(c, 128, M // m, m).transpose(2, 1, 0, 3))


def _tile_lhsT_e(w, m=128):
    E, Dd, M = w.shape
    c = Dd // 128
    return np.ascontiguousarray(
        w.reshape(E, c, 128, M // m, m).transpose(0, 3, 2, 1, 4))


def host_prep(inputs):
    def g(n):
        return np.ascontiguousarray(np.asarray(inputs[n], dtype=np.float32))

    a = {}
    a["L0_attn_W_toP"] = g("L0_attn_W_toP")
    a["L0_attn_W_U"] = np.ascontiguousarray(g("L0_attn_W_U").reshape(D, 2048))
    a["L0_attn_W_F"] = _tile_lhsT(g("L0_attn_W_F"))
    a["L0_attn_W_V"] = np.ascontiguousarray(g("L0_attn_W_V").reshape(1, 2048))
    a["L0_attn_W_lam"] = g("L0_attn_W_lam")
    a["L0_attn_B_mat"] = g("L0_attn_B_mat")
    a["L0_attn_C_mat"] = g("L0_attn_C_mat")
    a["L0_attn_W_fromP"] = _tile_lhsT(g("L0_attn_W_fromP"))
    a["L0_ffn_W1"] = _tile_lhsT_e(g("L0_ffn_W1"))
    a["L0_ffn_b1"] = np.ascontiguousarray(
        g("L0_ffn_b1").reshape(8, 32, 128).transpose(2, 0, 1))
    a["L0_ffn_W2"] = g("L0_ffn_W2")
    a["L0_ffn_b2"] = np.ascontiguousarray(g("L0_ffn_b2")[None])
    a["L0_ffn_gate_W"] = g("L0_ffn_gate_W")
    a["L0_ffn_shared_W"] = _tile_lhsT(g("L0_ffn_shared_W"))
    a["L0_ffn_shared_b_col"] = np.ascontiguousarray(
        g("L0_ffn_shared_b").reshape(NCH, 128).T)
    a["L0_ffn_gamma_col"] = np.ascontiguousarray(
        g("L0_ffn_gamma").reshape(NCH, 128).T)
    a["L0_ffn_beta_col"] = np.ascontiguousarray(
        g("L0_ffn_beta").reshape(NCH, 128).T)
    for w in ("Wq", "Wk", "Wv", "Wo"):
        a[f"L1_attn_{w}T"] = _tile_lhsT(np.ascontiguousarray(g(f"L1_attn_{w}").T))
    a["L1_ffn_Wr"] = g("L1_ffn_Wr")
    a["L1_ffn_br"] = np.ascontiguousarray(g("L1_ffn_br")[None])
    a["L1_ffn_W1"] = _tile_lhsT_e(g("L1_ffn_W1"))
    a["L1_ffn_b1"] = np.ascontiguousarray(
        g("L1_ffn_b1").reshape(8, 32, 128).transpose(2, 0, 1))
    a["L1_ffn_W2"] = g("L1_ffn_W2")
    a["L1_ffn_b2"] = np.ascontiguousarray(g("L1_ffn_b2")[None])
    a["L2_ffn_Wr"] = g("L2_ffn_Wr")
    a["L2_attn_W1T"] = _tile_lhsT(np.ascontiguousarray(g("L2_attn_W1").T))
    a["L2_attn_b2c"] = np.ascontiguousarray(g("L2_attn_b2").reshape(1, 1))
    a["L2_attn_w2c"] = np.ascontiguousarray(g("L2_attn_w2")[0].reshape(NCH, 128).T)
    a["L2_attn_b1c"] = np.ascontiguousarray(g("L2_attn_b1").reshape(NCH, 128).T)
    a["L2_attn_WcT"] = _tile_lhsT(np.ascontiguousarray(g("L2_attn_Wc").T))
    a["L2_attn_bcc"] = np.ascontiguousarray(g("L2_attn_bc").reshape(NCH, 128).T)
    a["L2_ffn_nwcol"] = np.ascontiguousarray(
        g("L2_ffn_norm_w").reshape(8, 8, 128).transpose(2, 0, 1))
    a["L2_ffn_fncol"] = np.ascontiguousarray(g("L2_ffn_f_norm").reshape(NCH, 128).T)
    a["L2_ffn_Wg"] = _tile_lhsT_e(g("L2_ffn_Wg"))
    a["L2_ffn_Wu"] = _tile_lhsT_e(g("L2_ffn_Wu"))
    a["L2_ffn_Wd"] = g("L2_ffn_Wd")
    a["L2_ffn_fWg"] = _tile_lhsT(g("L2_ffn_fWg"))
    a["L2_ffn_fWu"] = _tile_lhsT(g("L2_ffn_fWu"))
    a["L2_ffn_fWd"] = g("L2_ffn_fWd")
    perm = np.arange(3072)
    for sec in (0, 1024):
        for h in range(16):
            base = sec + h * 64
            perm[base:base + 64] = base + np.concatenate(
                [np.arange(0, 64, 2), np.arange(1, 64, 2)])
    a["L3_attn_WqkvT"] = _tile_lhsT(np.ascontiguousarray(g("L3_attn_Wqkv").T[:, perm]))
    a["L3_attn_bqkv_col"] = np.ascontiguousarray(
        g("L3_attn_bqkv")[perm].reshape(24, 128).T)
    a["L3_attn_gq_col"] = np.ascontiguousarray(
        g("L3_attn_gq")[perm[0:1024]].reshape(NCH, 128).T)
    a["L3_attn_gk_col"] = np.ascontiguousarray(
        g("L3_attn_gk")[perm[1024:2048] - 1024].reshape(NCH, 128).T)
    a["L3_attn_WoutT"] = _tile_lhsT(np.ascontiguousarray(g("L3_attn_Wout").T))
    a["L3_attn_bout_col"] = np.ascontiguousarray(
        g("L3_attn_bout").reshape(NCH, 128).T)
    a["L3_attn_gout_col"] = np.ascontiguousarray(
        g("L3_attn_gout").reshape(NCH, 128).T)
    a["L3_ffn_W_fcT"] = _tile_lhsT(np.ascontiguousarray(g("L3_ffn_W_fc").T))
    a["L3_ffn_lnw_col"] = np.ascontiguousarray(g("L3_ffn_ln_w").reshape(32, 128).T)
    a["L3_ffn_lnb_col"] = np.ascontiguousarray(g("L3_ffn_ln_b").reshape(32, 128).T)
    a["L3_ffn_W_projT"] = np.ascontiguousarray(g("L3_ffn_W_proj").T)
    return a


def build_launch(arrs, blocks, debug=None):
    ctx = Ctx()
    nc = ctx.nc
    hb = {}
    for name, a in arrs.items():
        hb[name] = nc.dram_tensor(name, list(a.shape), F32, kind="ExternalInput")
    x_in = nc.dram_tensor("x_in", [128, NCH, T], F32, kind="ExternalInput")
    x_out = nc.dram_tensor("x_out", [128, NCH, T], F32, kind="ExternalOutput")
    ctx.setup()
    P = ctx.P
    x_sb = ctx.sb("x_sb", [128, NCH, T], F32)
    xn_bf = ctx.sb("xn_bf", [128, NCH, T], BF16)
    ctx.init_scratch()
    P.dma("SPQ", x_sb[:, :, :], x_in.ap(),
          writes=[rk(x_sb, c) for c in range(NCH)])
    ctx.dbg = {}
    if debug:
        for nm, shape, dt_ in debug:
            ctx.dbg[nm] = nc.dram_tensor("dbg_" + nm, shape, dt_,
                                         kind="ExternalOutput")

    if blocks == ("ffn_dbg",):
        mk = ctx.scr_mark()
        c_tm = ctx.scr("dbg_ctm", [128, 16, 8], F32)
        for j in range(16):
            P.op("DVE", lambda e, j=j: e.memset(c_tm[:, j, :], 0.0),
                 writes=[rk(c_tm, j)])
            P.op("DVE", lambda e, j=j: e.memset(c_tm[:, j, 0:1], 1.0),
                 reads=[rk(c_tm, j)], writes=[rk(c_tm, j)])
        if debug and ("noln" in ctx.dbg):
            for c in range(NCH):
                for tt2 in range(NTT):
                    P.op("DVE", lambda e, c=c, tt2=tt2: e.tensor_copy(
                        xn_bf[:, c, tt2 * TT:(tt2 + 1) * TT],
                        x_sb[:, c, tt2 * TT:(tt2 + 1) * TT]),
                        reads=[rk(x_sb, c)], writes=[rk(xn_bf, c)])
        else:
            ln_block(ctx, x_sb, xn_bf)
        if "xn" in ctx.dbg:
            P.dma("SPQ", ctx.dbg["xn"].ap(), xn_bf[:, :, :],
                  reads=[rk(xn_bf, c) for c in range(NCH)])
        for c in range(NCH):
            P.op("DVE", lambda e, c=c: e.memset(x_sb[:, c, :], 0.0),
                 writes=[rk(x_sb, c)])
        moe_ffn(ctx, hb, x_sb, xn_bf, c_tm, "l0f", "L0_ffn_W1", "L0_ffn_b1",
                "L0_ffn_W2", "L0_ffn_b2", act=AF.Gelu, shared=None, out_ln=None,
                nexp=1)
        ctx.scr_release(mk)
    elif 0 in blocks:
        mk = ctx.scr_mark()
        gate_sb = ctx.scr("l0_gate", [128, NCH, 8], F32)
        lg = ctx.scr("l0_lg", [128, 16, 8], F32)
        c_tm = ctx.scr("l0_ctm", [128, 16, 8], F32)
        P.dma("SPQ", gate_sb[:, :, :],
              hb["L0_ffn_gate_W"].ap().rearrange("(c p) e -> p c e", p=128),
              writes=[rk(gate_sb)])
        ln_block(ctx, x_sb, xn_bf, router_w=gate_sb, lg_tm=lg)
        if "xn" in ctx.dbg:
            P.dma("SPQ", ctx.dbg["xn"].ap(), xn_bf[:, :, :],
                  reads=[rk(xn_bf, c) for c in range(NCH)])
        if "rowA" in ctx.dbg:
            P.dma("SPQ", ctx.dbg["rowA"].ap(), ctx.rowA[:, :], reads=[rk(ctx.rowA)])
        if "rowB" in ctx.dbg:
            P.dma("SPQ", ctx.dbg["rowB"].ap(), ctx.rowB[:, :], reads=[rk(ctx.rowB)])
        if "lg" in ctx.dbg:
            P.dma("SPQ", ctx.dbg["lg"].ap(), lg[:, :, :], reads=[rk(lg)])
        l0_attn(ctx, hb, x_sb, xn_bf)
        if "xattn" in ctx.dbg:
            P.dma("SPQ", ctx.dbg["xattn"].ap(), x_sb[:, :, :],
                  reads=[rk(x_sb, c) for c in range(NCH)])
        if "ctm" in ctx.dbg:
            _ev = topk_combine_l0(ctx, lg, c_tm)
            P.dma("SPQ", ctx.dbg["ctm"].ap(), c_tm[:, :, :],
                  reads=[rk(c_tm, j) for j in range(16)])
        topk_combine_l0(ctx, lg, c_tm)
        moe_ffn(ctx, hb, x_sb, xn_bf, c_tm, "l0f", "L0_ffn_W1", "L0_ffn_b1",
                "L0_ffn_W2", "L0_ffn_b2", act=AF.Gelu,
                shared={"w": "L0_ffn_shared_W", "b": "L0_ffn_shared_b_col"},
                out_ln={"g": "L0_ffn_gamma_col", "b": "L0_ffn_beta_col"})
        ctx.scr_release(mk)

    if "l2r" in blocks:
        ctx.ind1_o = nc.dram_tensor("ind1_o", [128, 16, 8], F32, kind="ExternalOutput")
        ctx.ind2_o = nc.dram_tensor("ind2_o", [128, 16, 8], F32, kind="ExternalOutput")
        ctx.p1_o = nc.dram_tensor("p1_o", [128, 16], F32, kind="ExternalOutput")
        ctx.p2_o = nc.dram_tensor("p2_o", [128, 16], F32, kind="ExternalOutput")
        ctx.cnt_o = nc.dram_tensor("cnt_o", [1, 8], F32, kind="ExternalOutput")
        ctx.xn2_o = nc.dram_tensor("xn2_o", [128, NCH, T], BF16, kind="ExternalOutput")

    if 1 in blocks:
        mk = ctx.scr_mark()
        r_sb = ctx.scr("l1_wr", [128, NCH, 8], F32)
        br_bc = ctx.scr("l1_br", [128, 8], F32)
        lg = ctx.scr("l1_lg", [128, 16, 8], F32)
        c_tm = ctx.scr("l1_ctm", [128, 16, 8], F32)
        P.dma("SPQ", r_sb[:, :, :],
              hb["L1_ffn_Wr"].ap().rearrange("(c p) e -> p c e", p=128),
              writes=[rk(r_sb)])
        P.dma("SPQ", br_bc[:, :],
              hb["L1_ffn_br"].ap().partition_broadcast(128)[:, 0, :],
              writes=[rk(br_bc)])
        ln_block(ctx, x_sb, xn_bf, router_w=r_sb, lg_tm=lg)
        l1_attn(ctx, hb, x_sb, xn_bf)
        topk_combine_l1(ctx, lg, c_tm, br_bc)
        moe_ffn(ctx, hb, x_sb, xn_bf, c_tm, "l1f", "L1_ffn_W1", "L1_ffn_b1",
                "L1_ffn_W2", "L1_ffn_b2", act=AF.Relu)
        ctx.scr_release(mk)

    if "l2r" in blocks:
        mk = ctx.scr_mark()
        r2_sb = ctx.scr("l2_wr", [128, NCH, 8], F32)
        lg2 = ctx.scr("l2_lg", [128, 16, 8], F32)
        ind1 = ctx.scr("l2_i1", [128, 16, 8], F32)
        ind2 = ctx.scr("l2_i2", [128, 16, 8], F32)
        p1 = ctx.scr("l2_p1", [128, 16], F32)
        p2 = ctx.scr("l2_p2", [128, 16], F32)
        cnt = ctx.scr("l2_cnt", [1, 8], F32)
        P.dma("SPQ", r2_sb[:, :, :],
              hb["L2_ffn_Wr"].ap().rearrange("(c p) e -> p c e", p=128),
              writes=[rk(r2_sb)])
        ln_block(ctx, x_sb, xn_bf, router_w=r2_sb, lg_tm=lg2)
        topk_l2_launch1(ctx, lg2, ind1, ind2, p1, p2, cnt)
        P.dma("SPQ", ctx.ind1_o.ap(), ind1[:, :, :],
              reads=[rk(ind1, j) for j in range(16)])
        P.dma("SPQ", ctx.ind2_o.ap(), ind2[:, :, :],
              reads=[rk(ind2, j) for j in range(16)])
        P.dma("SPQ", ctx.p1_o.ap(), p1[:, :], reads=[rk(p1)])
        P.dma("SPQ", ctx.p2_o.ap(), p2[:, :], reads=[rk(p2)])
        P.dma("SPQ", ctx.cnt_o.ap(), cnt[:, :], reads=[rk(cnt)])
        P.dma("SPQ", ctx.xn2_o.ap(), xn_bf[:, :, :],
              reads=[rk(xn_bf, c) for c in range(NCH)])
        ctx.scr_release(mk)

    P.dma("SPQ", x_out.ap(), x_sb[:, :, :],
          reads=[rk(x_sb, c) for c in range(NCH)])
    with nc.Block() as block:
        P.emit(block)
    return ctx


def _run(inputs, blocks=(0,), trace=False, debug=None, dbg_out=None):
    arrs = host_prep(inputs)
    x = np.asarray(inputs["L0_attn_x"], dtype=np.float32)
    ctx = build_launch(arrs, blocks, debug=debug)
    in_maps = []
    for b in range(NCORES):
        m = dict(arrs)
        m["x_in"] = np.ascontiguousarray(
            x[b].T.reshape(NCH, 128, T).transpose(1, 0, 2))
        in_maps.append(m)
    res = run_bass_kernel_spmd(ctx.nc, in_maps, core_ids=list(range(NCORES)),
                               trace=trace)
    out = np.empty((NCORES, T, D), np.float32)
    for b in range(NCORES):
        xo = res.results[b]["x_out"]
        out[b] = xo.transpose(1, 0, 2).reshape(D, T).T
    if trace:
        print("exec_time_ns:", res.exec_time_ns)
    if dbg_out is not None:
        dbg_out.append(res.results)
    return out


def kernel(**inputs):
    return _run(inputs, blocks=(0,))



def l1_attn(ctx, hb, x_sb, xn_bf):
    nc, P = ctx.nc, ctx.P
    bank = ctx.bank
    _mk = ctx.scr_mark()
    attn_buf = ctx.scr("l1_attn", [128, NCH, T], BF16)
    q_bf = ctx.scr("l1_q", [128, T], BF16)
    k_bf = ctx.scr("l1_k", [128, T], BF16)
    v_bf = ctx.scr("l1_v", [128, T], BF16)
    kT_sb = ctx.scr("l1_kT", [128, 2, 64], BF16)
    vT_sb = ctx.scr("l1_vT", [128, 2, 65], BF16)
    kv_bf = ctx.scr("l1_kv", [128, 65], BF16)
    z_sb = ctx.scr("l1_z", [65, T], F32)
    msk_bf = ctx.scr("l1_msk", [128, 2, 128], BF16)
    tri_h = nc.inline_tensor(np.triu(np.ones((128, 128), np.float32)), "c_tri")
    tri_sb = ctx.scr("l1_tri", [128, 128], F32)
    P.dma("SPQ", tri_sb[:, :], tri_h[:, :], writes=[rk(tri_sb)])
    b7, b6 = bank[7], bank[6]
    ctx.psum_free(bank[7])
    ctx.psum_free(bank[6])
    tp0 = ctx.psum(ctx.u("l1tp0"), [128, 512], BF16)
    tp1 = ctx.psum(ctx.u("l1tp1"), [128, 512], BF16)
    P.inherit(rk(tp0), rk(b7))
    P.inherit(rk(tp1), rk(b6))

    def proj_pair(wname, hp, dst, relu_eps):
        wr = hb[wname].ap()

        def src(mb, hp=hp):
            return wr[hp]

        def rhs(c, tt):
            return xn_bf[:, c, tt * TT:(tt + 1) * TT], [rk(xn_bf, c)]

        def evac(mb, tt, bk, dst=dst, relu_eps=relu_eps):
            tsl = slice(tt * TT, (tt + 1) * TT)
            if relu_eps:
                P.op("DVE", lambda e, bk=bk, tsl=tsl: e.tensor_scalar(
                    dst[:, tsl], bk[:, :], 0.0, 1e-6, ALU.max, ALU.add),
                    reads=[rk(bk)], writes=[rk(dst)])
            else:
                P.op("ACT", lambda e, bk=bk, tsl=tsl: e.copy(dst[:, tsl], bk[:, :]),
                     reads=[rk(bk)], writes=[rk(dst)])
        proj_stream(ctx, src, 1, NCH, rhs, evac, banks=(4, 5))

    for hp in range(NCH):
        proj_pair("L1_attn_WqT", hp, q_bf, True)
        proj_pair("L1_attn_WkT", hp, k_bf, True)
        proj_pair("L1_attn_WvT", hp, v_bf, False)
        for hbase in (0, 64):
            P.op("DVE", lambda e: e.memset(vT_sb[:, 0, 64:65], 1.0),
                 writes=[rk(vT_sb, 0, "one")])
            P.op("DVE", lambda e: e.memset(vT_sb[:, 1, 64:65], 1.0),
                 writes=[rk(vT_sb, 1, "one")])
            for ci in range(16):
                pb = ci % 2
                csl = slice(ci * 128, (ci + 1) * 128)
                tpb = tp0 if pb == 0 else tp1
                P.op("PE", lambda e, hbase=hbase, csl=csl: e.matmul(
                    bank[0][:, 0:128], k_bf[hbase:hbase + 64, csl],
                    q_bf[hbase:hbase + 64, csl], start=True, stop=True),
                    reads=[rk(k_bf), rk(q_bf)], writes=[rk(bank[0])])
                P.op("DVE", lambda e, pb=pb: e.tensor_tensor(
                    msk_bf[:, pb, :], bank[0][:, 0:128], tri_sb[:, :], ALU.mult),
                    reads=[rk(bank[0]), rk(tri_sb)], writes=[rk(msk_bf, pb)])
                P.op("PE", lambda e, hbase=hbase, csl=csl, tpb=tpb: e.transpose(
                    tpb[:, 0:64], k_bf[hbase:hbase + 64, csl],
                    ctx.id_b[hbase:hbase + 64, hbase:hbase + 64]),
                    reads=[rk(k_bf), rk(ctx.id_b)], writes=[rk(tpb, 0)])
                P.op("ACT", lambda e, pb=pb, tpb=tpb: e.copy(
                    kT_sb[:, pb, :], tpb[:, 0:64]),
                    reads=[rk(tpb, 0)], writes=[rk(kT_sb, pb)])
                P.op("PE", lambda e, hbase=hbase, csl=csl, tpb=tpb: e.transpose(
                    tpb[:, 64:128], v_bf[hbase:hbase + 64, csl],
                    ctx.id_b[hbase:hbase + 64, hbase:hbase + 64]),
                    reads=[rk(v_bf), rk(ctx.id_b)], writes=[rk(tpb, 1)])
                P.op("ACT", lambda e, pb=pb, tpb=tpb: e.copy(
                    vT_sb[:, pb, 0:64], tpb[:, 64:128]),
                    reads=[rk(tpb, 1)], writes=[rk(vT_sb, pb)])
                zrow = 64 if hbase == 0 else 0
                P.op("PE", lambda e, pb=pb, hbase=hbase, ci=ci: e.matmul(
                    bank[1][hbase:hbase + 64, 0:128], vT_sb[:, pb, 0:64],
                    msk_bf[:, pb, :], start=True, stop=(ci == 0),
                    skip_group_check=True),
                    reads=[rk(vT_sb, pb), rk(msk_bf, pb)],
                    writes=[rk(bank[1])])
                P.op("PE", lambda e, pb=pb, zrow=zrow, ci=ci: e.matmul(
                    bank[1][zrow:zrow + 1, 0:128], ctx.ones_b[:, 0:1],
                    msk_bf[:, pb, :], start=True, stop=(ci == 0),
                    skip_group_check=True),
                    reads=[rk(msk_bf, pb), rk(ctx.ones_b)], writes=[rk(bank[1])])
                if ci > 0:
                    P.op("PE", lambda e, hbase=hbase, csl=csl: e.matmul(
                        bank[1][hbase:hbase + 64, 0:128],
                        kv_bf[hbase:hbase + 64, 0:64],
                        q_bf[hbase:hbase + 64, csl], start=False, stop=True,
                        skip_group_check=True),
                        reads=[rk(kv_bf), rk(q_bf)], writes=[rk(bank[1])])
                    P.op("PE", lambda e, hbase=hbase, csl=csl, zrow=zrow: e.matmul(
                        bank[1][zrow:zrow + 1, 0:128],
                        kv_bf[hbase:hbase + 64, 64:65],
                        q_bf[hbase:hbase + 64, csl], start=False, stop=True,
                        skip_group_check=True),
                        reads=[rk(kv_bf), rk(q_bf)], writes=[rk(bank[1])])
                P.op("PE", lambda e, ci=ci, hbase=hbase, pb=pb: e.matmul(
                    bank[2][hbase:hbase + 64, 0:65], kT_sb[:, pb, :],
                    vT_sb[:, pb, :], start=(ci == 0), stop=(ci == 15),
                    skip_group_check=True),
                    reads=[rk(kT_sb, pb), rk(vT_sb, pb), rk(vT_sb, pb, "one")],
                    writes=[rk(bank[2])])
                P.op("ACT", lambda e, hbase=hbase: e.copy(
                    kv_bf[hbase:hbase + 64, :], bank[2][hbase:hbase + 64, 0:65]),
                    reads=[rk(bank[2])], writes=[rk(kv_bf)])
                P.op("ACT", lambda e, hp=hp, hbase=hbase, csl=csl: e.copy(
                    attn_buf[hbase:hbase + 64, hp, csl],
                    bank[1][hbase:hbase + 64, 0:128]),
                    reads=[rk(bank[1])], writes=[rk(attn_buf, hp, hbase)])
                P.op("ACT", lambda e, csl=csl, zrow=zrow: e.copy(
                    z_sb[zrow:zrow + 1, csl], bank[1][zrow:zrow + 1, 0:128]),
                    reads=[rk(bank[1])], writes=[rk(z_sb)])
            zrow = 64 if hbase == 0 else 0
            P.op("DVE", lambda e, zrow=zrow: e.tensor_single_scalar(
                z_sb[zrow:zrow + 1, :], z_sb[zrow:zrow + 1, :], 1e-6, ALU.add),
                reads=[rk(z_sb)], writes=[rk(z_sb)])
            P.op("DVE", lambda e, zrow=zrow: e.reciprocal(
                z_sb[zrow:zrow + 1, :], z_sb[zrow:zrow + 1, :]),
                reads=[rk(z_sb)], writes=[rk(z_sb)])
            for tt in range(NTT):
                tsl = slice(tt * TT, (tt + 1) * TT)
                P.op("PE", lambda e, tsl=tsl, hbase=hbase, zrow=zrow: e.matmul(
                    bank[3][hbase:hbase + 64, :], ctx.ones_f[zrow:zrow + 1, 0:64],
                    z_sb[zrow:zrow + 1, tsl], start=True, stop=True),
                    reads=[rk(z_sb), rk(ctx.ones_f)], writes=[rk(bank[3])])
                P.op("DVE", lambda e, hp=hp, hbase=hbase, tsl=tsl: e.tensor_tensor(
                    attn_buf[hbase:hbase + 64, hp, tsl],
                    attn_buf[hbase:hbase + 64, hp, tsl],
                    bank[3][hbase:hbase + 64, :], ALU.mult),
                    reads=[rk(attn_buf, hp, hbase), rk(bank[3])],
                    writes=[rk(attn_buf, hp, hbase)])
    ctx.psum_free(tp1)
    ctx.psum_free(tp0)
    nb6 = ctx.psum(ctx.u("bank6r"), [128, 512], F32)
    nb7 = ctx.psum(ctx.u("bank7r"), [128, 512], F32)
    P.inherit(rk(nb6), rk(tp1))
    P.inherit(rk(nb7), rk(tp0))
    ctx.bank[6] = nb6
    ctx.bank[7] = nb7
    bank = ctx.bank

    wo_r = hb["L1_attn_WoT"].ap()

    def wo_src(mb):
        return wo_r[mb]

    def wo_rhs(c, tt):
        return (attn_buf[:, c, tt * TT:(tt + 1) * TT],
                [rk(attn_buf, c, 0), rk(attn_buf, c, 64)])

    def wo_evac(mb, tt, bk):
        tsl = slice(tt * TT, (tt + 1) * TT)
        P.op("DVE", lambda e, mb=mb, tsl=tsl, bk=bk: e.scalar_tensor_tensor(
            x_sb[:, mb, tsl], bk[:, :], 1.0, x_sb[:, mb, tsl], ALU.mult, ALU.add),
            reads=[rk(bk), rk(x_sb, mb)], writes=[rk(x_sb, mb)])

    proj_stream(ctx, wo_src, NCH, NCH, wo_rhs, wo_evac, banks=(6, 7))
    ctx.scr_release(_mk)


def topk_combine_l1(ctx, lg_tm, c_tm, br_bc):
    P = ctx.P
    _mk = ctx.scr_mark()
    m1 = ctx.scr("m1", [128, 16], F32)
    nm1 = ctx.scr("nm1", [128, 16], F32)
    zz = ctx.scr("zz", [128, 16], F32)
    et = ctx.scr("et", [128, 16, 8], F32)
    for j in range(16):
        P.op("DVE", lambda e, j=j: e.tensor_tensor(
            lg_tm[:, j, :], lg_tm[:, j, :], br_bc[:, :], ALU.add),
            reads=[rk(lg_tm, j), rk(br_bc)], writes=[rk(lg_tm, j)])
    P.op("DVE", lambda e: e.reduce_max(m1[:, :], lg_tm[:, :, :], axis=AX.X),
         reads=[rk(lg_tm, j) for j in range(16)], writes=[rk(m1)])
    P.op("DVE", lambda e: e.reduce_max(nm1[:, :], lg_tm[:, :, :], axis=AX.X,
                                       negate=True),
         reads=[rk(lg_tm, j) for j in range(16)], writes=[rk(nm1)])
    for j in range(16):
        P.op("ACT", lambda e, j=j: e.activation(
            et[:, j, :], lg_tm[:, j, :], AF.Exp, bias=nm1[:, j:j + 1]),
            reads=[rk(lg_tm, j), rk(nm1)], writes=[rk(et, j)])
    P.op("DVE", lambda e: e.reduce_sum(zz[:, :], et[:, :, :], axis=AX.X),
         reads=[rk(et, j) for j in range(16)], writes=[rk(zz)])
    P.op("DVE", lambda e: e.reciprocal(zz[:, :], zz[:, :]),
         reads=[rk(zz)], writes=[rk(zz)])
    for j in range(16):
        P.op("DVE", lambda e, j=j: e.tensor_single_scalar(
            et[:, j, :], lg_tm[:, j, :], m1[:, j:j + 1], ALU.is_ge),
            reads=[rk(lg_tm, j), rk(m1)], writes=[rk(et, j)])
        P.op("DVE", lambda e, j=j: e.tensor_single_scalar(
            c_tm[:, j, :], et[:, j, :], zz[:, j:j + 1], ALU.mult),
            reads=[rk(et, j), rk(zz)], writes=[rk(c_tm, j)])
    ctx.scr_release(_mk)



def topk_l2_launch1(ctx, lg_tm, ind1, ind2, p1, p2, cnt_out):
    P = ctx.P
    bank = ctx.bank
    _mk = ctx.scr_mark()
    m1 = ctx.scr("m1", [128, 16], F32)
    nm1 = ctx.scr("nm1", [128, 16], F32)
    m2 = ctx.scr("m2", [128, 16], F32)
    zz = ctx.scr("zz", [128, 16], F32)
    et = ctx.scr("et", [128, 16, 8], F32)
    l2t = ctx.scr("l2t", [128, 16, 8], F32)
    P.op("DVE", lambda e: e.reduce_max(m1[:, :], lg_tm[:, :, :], axis=AX.X),
         reads=[rk(lg_tm, j) for j in range(16)], writes=[rk(m1)])
    P.op("DVE", lambda e: e.reduce_max(nm1[:, :], lg_tm[:, :, :], axis=AX.X,
                                       negate=True),
         reads=[rk(lg_tm, j) for j in range(16)], writes=[rk(nm1)])
    for j in range(16):
        P.op("DVE", lambda e, j=j: e.tensor_single_scalar(
            ind1[:, j, :], lg_tm[:, j, :], m1[:, j:j + 1], ALU.is_ge),
            reads=[rk(lg_tm, j), rk(m1)], writes=[rk(ind1, j)])
        P.op("DVE", lambda e, j=j: e.scalar_tensor_tensor(
            l2t[:, j, :], ind1[:, j, :], -1e30, lg_tm[:, j, :], ALU.mult, ALU.add),
            reads=[rk(ind1, j), rk(lg_tm, j)], writes=[rk(l2t, j)])
        P.op("ACT", lambda e, j=j: e.activation(
            et[:, j, :], lg_tm[:, j, :], AF.Exp, bias=nm1[:, j:j + 1]),
            reads=[rk(lg_tm, j), rk(nm1)], writes=[rk(et, j)])
    P.op("DVE", lambda e: e.reduce_max(m2[:, :], l2t[:, :, :], axis=AX.X),
         reads=[rk(l2t, j) for j in range(16)], writes=[rk(m2)])
    for j in range(16):
        P.op("DVE", lambda e, j=j: e.tensor_single_scalar(
            ind2[:, j, :], l2t[:, j, :], m2[:, j:j + 1], ALU.is_ge),
            reads=[rk(l2t, j), rk(m2)], writes=[rk(ind2, j)])
    P.op("DVE", lambda e: e.reduce_sum(zz[:, :], et[:, :, :], axis=AX.X),
         reads=[rk(et, j) for j in range(16)], writes=[rk(zz)])
    P.op("DVE", lambda e: e.reciprocal(p1[:, :], zz[:, :]),
         reads=[rk(zz)], writes=[rk(p1)])
    P.op("DVE", lambda e: e.tensor_tensor(m2[:, :], m2[:, :], m1[:, :],
                                          ALU.subtract),
         reads=[rk(m2), rk(m1)], writes=[rk(m2)])
    P.op("ACT", lambda e: e.activation(m2[:, :], m2[:, :], AF.Exp),
         reads=[rk(m2)], writes=[rk(m2)])
    P.op("DVE", lambda e: e.tensor_tensor(p2[:, :], m2[:, :], p1[:, :], ALU.mult),
         reads=[rk(m2), rk(p1)], writes=[rk(p2)])
    for j in range(16):
        P.op("PE", lambda e, j=j: e.matmul(
            bank[0][0:1, 0:8], ctx.ones_f[:, 0:1], ind1[:, j, :],
            start=(j == 0), stop=False, skip_group_check=True),
            reads=[rk(ind1, j), rk(ctx.ones_f)], writes=[rk(bank[0])])
        P.op("PE", lambda e, j=j: e.matmul(
            bank[0][0:1, 0:8], ctx.ones_f[:, 0:1], ind2[:, j, :],
            start=False, stop=(j == 15), skip_group_check=True),
            reads=[rk(ind2, j), rk(ctx.ones_f)], writes=[rk(bank[0])])
    P.op("ACT", lambda e: e.copy(cnt_out[:, :], bank[0][0:1, 0:8]),
         reads=[rk(bank[0])], writes=[rk(cnt_out)])
    ctx.scr_release(_mk)



def l2_attn(ctx, hb, x_sb, xn_bf):
    nc, P = ctx.nc, ctx.P
    bank = ctx.bank
    _mk = ctx.scr_mark()
    h2 = ctx.scr("l2_h2", [128, NCH, T], BF16)
    s_row = ctx.scr("l2_s", [1, T], F32)
    e_row = ctx.scr("l2_e", [1, T], F32)
    den = ctx.scr("l2_den", [1, T], F32)
    mrow = ctx.scr("l2_m", [1, 1], F32)
    w2col = ctx.scr("l2_w2c", [128, NCH], BF16)
    b1col = ctx.scr("l2_b1c", [128, NCH], F32)
    bccol = ctx.scr("l2_bcc", [128, NCH], F32)
    b2sb = ctx.scr("l2_b2s", [1, 1], F32)
    P.dma("SPQ", b2sb[:, :], hb["L2_attn_b2c"].ap(), writes=[rk(b2sb)])
    P.dma("PQ", w2col[:, :], hb["L2_attn_w2c"].ap(), writes=[rk(w2col)])
    P.dma("SPQ", b1col[:, :], hb["L2_attn_b1c"].ap(), writes=[rk(b1col)])
    P.dma("SPQ", bccol[:, :], hb["L2_attn_bcc"].ap(), writes=[rk(bccol)])
    w1_r = hb["L2_attn_W1T"].ap()

    def w1_src(mb):
        return w1_r[mb]

    def xn_rhs(c, tt):
        return xn_bf[:, c, tt * TT:(tt + 1) * TT], [rk(xn_bf, c)]

    def h2_evac(mb, tt, bk):
        tsl = slice(tt * TT, (tt + 1) * TT)
        P.op("ACT", lambda e, mb=mb, bk=bk, tsl=tsl: e.activation(
            h2[:, mb, tsl], bk[:, :], AF.Tanh, bias=b1col[:, mb:mb + 1]),
            reads=[rk(bk), rk(b1col)], writes=[rk(h2, mb)])

    proj_stream(ctx, w1_src, NCH, NCH, xn_rhs, h2_evac)
    for tt in range(NTT):
        tsl = slice(tt * TT, (tt + 1) * TT)
        for c in range(NCH):
            P.op("PE", lambda e, c=c, tsl=tsl: e.matmul(
                bank[0][0:1, :], w2col[:, c:c + 1], h2[:, c, tsl],
                start=(c == 0), stop=(c == NCH - 1), skip_group_check=True),
                reads=[rk(w2col), rk(h2, c)], writes=[rk(bank[0])])
        P.op("ACT", lambda e, tsl=tsl: e.copy(s_row[0:1, tsl], bank[0][0:1, :]),
             reads=[rk(bank[0])], writes=[rk(s_row)])
    P.op("DVE", lambda e: e.tensor_single_scalar(
        s_row[:, :], s_row[:, :], b2sb[0:1, 0:1], ALU.add),
        reads=[rk(s_row), rk(b2sb)], writes=[rk(s_row)])
    P.op("DVE", lambda e: e.reduce_max(mrow[:, :], s_row[:, :], axis=AX.X,
                                       negate=True),
         reads=[rk(s_row)], writes=[rk(mrow)])
    P.op("ACT", lambda e: e.activation(e_row[:, :], s_row[:, :], AF.Exp,
                                       bias=mrow[0:1, 0:1]),
         reads=[rk(s_row), rk(mrow)], writes=[rk(e_row)])
    P.op("DVE", lambda e: e.tensor_tensor_scan(
        den[:, :], e_row[:, :], e_row[:, :], 0.0, ALU.add, ALU.bypass),
        reads=[rk(e_row)], writes=[rk(den)])
    P.op("DVE", lambda e: e.reciprocal(den[:, :], den[:, :]),
         reads=[rk(den)], writes=[rk(den)])
    for c in range(NCH):
        for tt in range(NTT):
            tsl = slice(tt * TT, (tt + 1) * TT)
            ctx.bcast_row(e_row, tt * TT, bank[3])
            P.op("DVE", lambda e, c=c, tsl=tsl: e.tensor_tensor(
                h2[:, c, tsl], xn_bf[:, c, tsl], bank[3][:, :], ALU.mult),
                reads=[rk(xn_bf, c), rk(bank[3])], writes=[rk(h2, c)])
        P.op("DVE", lambda e, c=c: e.tensor_tensor_scan(
            h2[:, c, :], h2[:, c, :], h2[:, c, :], 0.0, ALU.add, ALU.bypass),
            reads=[rk(h2, c)], writes=[rk(h2, c)])
        for tt in range(NTT):
            tsl = slice(tt * TT, (tt + 1) * TT)
            ctx.bcast_row(den, tt * TT, bank[3])
            P.op("DVE", lambda e, c=c, tsl=tsl: e.tensor_tensor(
                h2[:, c, tsl], h2[:, c, tsl], bank[3][:, :], ALU.mult),
                reads=[rk(h2, c), rk(bank[3])], writes=[rk(h2, c)])
    wc_r = hb["L2_attn_WcT"].ap()

    def wc_src(mb):
        return wc_r[mb]

    def wc_rhs(c, tt):
        if c < NCH:
            return h2[:, c, tt * TT:(tt + 1) * TT], [rk(h2, c)]
        return xn_bf[:, c - NCH, tt * TT:(tt + 1) * TT], [rk(xn_bf, c - NCH)]

    def wc_evac(mb, tt, bk):
        tsl = slice(tt * TT, (tt + 1) * TT)
        P.op("ACT", lambda e, mb=mb, bk=bk: e.activation(
            ctx.ftmp[:, :], bk[:, :], AF.Tanh, bias=bccol[:, mb:mb + 1]),
            reads=[rk(bk), rk(bccol)], writes=[rk(ctx.ftmp)])
        P.op("DVE", lambda e, mb=mb, tsl=tsl: e.tensor_tensor(
            x_sb[:, mb, tsl], x_sb[:, mb, tsl], ctx.ftmp[:, :], ALU.add),
            reads=[rk(x_sb, mb), rk(ctx.ftmp)], writes=[rk(x_sb, mb)])

    proj_stream(ctx, wc_src, NCH, 2 * NCH, wc_rhs, wc_evac)
    ctx.scr_release(_mk)



def l2_ffn(ctx, hb, x_sb, xn_bf, ind1, ind2, p1, p2, cnts_sb, mask_sb):
    nc, P = ctx.nc, ctx.P
    bank = ctx.bank
    CAP = 4096.0
    _mk = ctx.scr_mark()
    c_tm = ctx.scr("l2_ctm", [128, 16, 8], F32)
    s0_tm = ctx.scr("l2_s0", [128, 16], F32)
    ov_tm = ctx.scr("l2_ovf", [128, 16], F32)

    _mkr = ctx.scr_mark()
    offs = ctx.scr("l2_offs", [8, 1], F32)
    em1 = ctx.scr("l2_em1", [8, T], F32)
    em2 = ctx.scr("l2_em2", [8, T], F32)
    kc1 = ctx.scr("l2_kc1", [8, T], F32)
    kc2 = ctx.scr("l2_kc2", [8, T], F32)
    prs = ctx.scr("l2_prs", [8, 2 * T], F32)
    kp1 = ctx.scr("l2_kp1", [128, 16, 8], F32)
    kp2 = ctx.scr("l2_kp2", [128, 16, 8], F32)
    P.op("PE", lambda e: e.matmul(bank[0][0:8, 0:1], cnts_sb[:, :], mask_sb[:, :],
                                  start=True, stop=True),
         reads=[rk(cnts_sb), rk(mask_sb)], writes=[rk(bank[0])])
    P.op("ACT", lambda e: e.activation(offs[:, :], bank[0][0:8, 0:1], AF.Copy,
                                       bias=-1.0),
         reads=[rk(bank[0])], writes=[rk(offs)])
    for nm_tm, em in ((ind1, em1), (ind2, em2)):
        for j in range(16):
            P.op("PE", lambda e, j=j, nm_tm=nm_tm: e.matmul(
                bank[1][0:8, (j % 4) * 128:(j % 4 + 1) * 128], nm_tm[:, j, :],
                ctx.id_f[:, :], start=True, stop=True, skip_group_check=True),
                reads=[rk(nm_tm, j), rk(ctx.id_f)], writes=[rk(bank[1])])
            if j % 4 == 3:
                P.op("ACT", lambda e, j=j, em=em: e.copy(
                    em[:, (j - 3) * 128:(j + 1) * 128], bank[1][0:8, :]),
                    reads=[rk(bank[1])], writes=[rk(em)])
    P.op("DVE", lambda e: e.tensor_copy(prs[:, 0:2 * T:2], em1[:, :]),
         reads=[rk(em1)], writes=[rk(prs)])
    P.op("DVE", lambda e: e.tensor_copy(prs[:, 1:2 * T:2], em2[:, :]),
         reads=[rk(em2)], writes=[rk(prs)])
    P.op("DVE", lambda e: e.tensor_tensor_scan(
        prs[:, :], prs[:, :], prs[:, :], 0.0, ALU.add, ALU.bypass),
        reads=[rk(prs)], writes=[rk(prs)])
    for em, kc, sl0 in ((em1, kc1, 0), (em2, kc2, 1)):
        P.op("DVE", lambda e, kc=kc, sl0=sl0: e.tensor_scalar(
            kc[:, :], prs[:, sl0:2 * T:2], offs[0:8, 0:1], CAP, ALU.add, ALU.is_lt),
            reads=[rk(prs), rk(offs)], writes=[rk(kc)])
        P.op("DVE", lambda e, em=em, kc=kc: e.tensor_tensor(
            kc[:, :], kc[:, :], em[:, :], ALU.mult),
            reads=[rk(kc), rk(em)], writes=[rk(kc)])
    for kc, kp in ((kc1, kp1), (kc2, kp2)):
        for j in range(16):
            jsl = slice(j * 128, (j + 1) * 128)
            P.op("PE", lambda e, kc=kc, jsl=jsl: e.transpose(
                bank[2][:, 0:8], kc[0:8, jsl], ctx.id_f[0:8, 0:8]),
                reads=[rk(kc), rk(ctx.id_f)], writes=[rk(bank[2])])
            P.op("ACT", lambda e, kp=kp, j=j: e.copy(kp[:, j, :], bank[2][:, 0:8]),
                 reads=[rk(bank[2])], writes=[rk(kp, j)])
    for j in range(16):
        P.op("DVE", lambda e, j=j: e.tensor_single_scalar(
            c_tm[:, j, :], kp1[:, j, :], p1[:, j:j + 1], ALU.mult),
            reads=[rk(kp1, j), rk(p1)], writes=[rk(c_tm, j)])
        P.op("DVE", lambda e, j=j: e.scalar_tensor_tensor(
            c_tm[:, j, :], kp2[:, j, :], p2[:, j:j + 1], c_tm[:, j, :],
            ALU.mult, ALU.add),
            reads=[rk(kp2, j), rk(p2), rk(c_tm, j)], writes=[rk(c_tm, j)])
        P.op("DVE", lambda e, j=j: e.tensor_tensor(
            kp1[:, j, :], kp1[:, j, :], kp2[:, j, :], ALU.add),
            reads=[rk(kp1, j), rk(kp2, j)], writes=[rk(kp1, j)])
    P.op("DVE", lambda e: e.reduce_sum(ov_tm[:, :], kp1[:, :, :], axis=AX.X),
         reads=[rk(kp1, j) for j in range(16)], writes=[rk(ov_tm)])
    P.op("DVE", lambda e: e.tensor_single_scalar(
        ov_tm[:, :], ov_tm[:, :], 2.0, ALU.is_lt),
        reads=[rk(ov_tm)], writes=[rk(ov_tm)])
    P.op("DVE", lambda e: e.reduce_sum(s0_tm[:, :], c_tm[:, :, :], axis=AX.X),
         reads=[rk(c_tm, j) for j in range(16)], writes=[rk(s0_tm)])
    P.op("DVE", lambda e: e.tensor_single_scalar(
        s0_tm[:, :], s0_tm[:, :], 1.0, ALU.add),
        reads=[rk(s0_tm)], writes=[rk(s0_tm)])
    ctx.scr_release(_mkr)

    nwcol = ctx.scr("l2_nw", [128, 8, 8], F32)
    fncol = ctx.scr("l2_fn", [128, NCH], F32)
    P.dma("SPQ", nwcol[:, :, :], hb["L2_ffn_nwcol"].ap(), writes=[rk(nwcol)])
    P.dma("SPQ", fncol[:, :], hb["L2_ffn_fncol"].ap(), writes=[rk(fncol)])
    xh = ctx.scr("l2_xh", [128, NCH, TT], BF16)
    xre = ctx.scr("l2_xre", [128, NCH, TT], BF16)
    gt = ctx.scr("l2_gt", [128, 16, TT], BF16)
    acc = ctx.scr("l2_acc", [128, NCH, TT], F32)
    facc = ctx.scr("l2_facc", [128, NCH, TT], F32)
    ftmp2 = ctx.scr("l2_f2", [128, TT], F32)
    wg_r = hb["L2_ffn_Wg"].ap()
    wu_r = hb["L2_ffn_Wu"].ap()
    wd_r = hb["L2_ffn_Wd"].ap().rearrange("e (k p) (oc m) -> e p k oc m", p=128, m=128)
    fg_r = hb["L2_ffn_fWg"].ap()
    fu_r = hb["L2_ffn_fWu"].ap()
    fd_r = hb["L2_ffn_fWd"].ap().rearrange("(k p) (oc m) -> p k oc m", p=128, m=128)
    w2buf = ctx.wstream2

    for tt in range(NTT):
        tsl = slice(tt * TT, (tt + 1) * TT)
        for c in range(NCH):
            P.op("ACT", lambda e, c=c, tsl=tsl: e.activation(
                ctx.sq[:, :], xn_bf[:, c, tsl], AF.Square),
                reads=[rk(xn_bf, c)], writes=[rk(ctx.sq)])
            P.op("PE", lambda e, c=c: e.matmul(
                bank[0][0:1, :], ctx.ones_f[:, 0:1], ctx.sq[:, :],
                start=(c == 0), stop=(c == NCH - 1), skip_group_check=True),
                reads=[rk(ctx.sq), rk(ctx.ones_f)], writes=[rk(bank[0])])
        P.op("ACT", lambda e, tsl=tsl: e.activation(
            ctx.rowA[0:1, 0:TT], bank[0][0:1, :], AF.Ln, bias=1e-6,
            scale=1.0 / D),
            reads=[rk(bank[0])], writes=[rk(ctx.rowA)])
        P.op("ACT", lambda e: e.activation(
            ctx.rowA[0:1, 0:TT], ctx.rowA[0:1, 0:TT], AF.Exp, scale=-0.5),
            reads=[rk(ctx.rowA)], writes=[rk(ctx.rowA)])
        ctx.bcast_row(ctx.rowA, 0, bank[3])
        for c in range(NCH):
            P.op("DVE", lambda e, c=c, tsl=tsl: e.tensor_tensor(
                xh[:, c, :], xn_bf[:, c, tsl], bank[3][:, :], ALU.mult),
                reads=[rk(xn_bf, c), rk(bank[3])], writes=[rk(xh, c)])
            P.op("DVE", lambda e, c=c: e.memset(acc[:, c, :], 0.0),
                 writes=[rk(acc, c)])

        def xre_rhs(c, _tt):
            return xre[:, c, :], [rk(xre, c)]

        for ex in range(8):
            for c in range(NCH):
                P.op("DVE", lambda e, c=c, ex=ex: e.tensor_single_scalar(
                    xre[:, c, :], xh[:, c, :], s2d(nwcol, ex, c), ALU.mult),
                    reads=[rk(xh, c), rk(nwcol)], writes=[rk(xre, c)])

            def g_evac(mb, _tt, bk):
                P.op("ACT", lambda e, mb=mb, bk=bk: e.activation(
                    gt[:, mb, :], bk[:, :], AF.Gelu),
                    reads=[rk(bk)], writes=[rk(gt, mb)])
            proj_stream(ctx, lambda mb, ex=ex: wg_r[ex, mb],
                        16, NCH, xre_rhs, g_evac, tts=(tt,))

            def u_evac(mb, _tt, bk):
                P.op("DVE", lambda e, mb=mb, bk=bk: e.tensor_tensor(
                    gt[:, mb, :], gt[:, mb, :], bk[:, :], ALU.mult),
                    reads=[rk(gt, mb), rk(bk)], writes=[rk(gt, mb)])
            proj_stream(ctx, lambda mb, ex=ex: wu_r[ex, mb],
                        16, NCH, xre_rhs, u_evac, tts=(tt,))

            for j in range(4):
                jj = tt * 4 + j
                P.op("PE", lambda e, j=j, jj=jj, ex=ex: e.matmul(
                    bank[2][0:1, j * 128:(j + 1) * 128], s2d(c_tm, jj, ex),
                    ctx.id_f[:, :], start=True, stop=True, skip_group_check=True),
                    reads=[rk(c_tm, jj)], writes=[rk(bank[2])])
            P.op("ACT", lambda e: e.copy(ctx.bc_sb[0:1, :], bank[2][0:1, :]),
                 reads=[rk(bank[2])], writes=[rk(ctx.bc_sb)])
            P.op("PE", lambda e: e.matmul(
                bank[3][:, :], ctx.ones_f[0:1, :], ctx.bc_sb[0:1, :],
                start=True, stop=True),
                reads=[rk(ctx.bc_sb), rk(ctx.ones_f)], writes=[rk(bank[3])])
            P.op("ACT", lambda e: e.copy(ctx.bc_sb[:, :], bank[3][:, :]),
                 reads=[rk(bank[3])], writes=[rk(ctx.bc_sb)])
            for k in range(16):
                b = k % 2
                P.dma("PQ", w2buf[:, b, :, :], wd_r[ex][:, k, :, :],
                      writes=[rk(w2buf, b)])
                for o in range(NCH):
                    P.op("PE", lambda e, k=k, b=b, o=o: e.matmul(
                        bank[o][:, :], w2buf[:, b, o, :], gt[:, k, :],
                        start=(k == 0), stop=(k == 15), skip_group_check=True),
                        reads=[rk(w2buf, b), rk(gt, k)], writes=[rk(bank[o])])
            for o in range(NCH):
                P.op("DVE", lambda e, o=o: e.tensor_tensor(
                    ftmp2[:, :], bank[o][:, :], ctx.bc_sb[:, :], ALU.mult),
                    reads=[rk(bank[o]), rk(ctx.bc_sb)], writes=[rk(ftmp2)])
                P.op("DVE", lambda e, o=o: e.tensor_tensor(
                    acc[:, o, :], acc[:, o, :], ftmp2[:, :], ALU.add),
                    reads=[rk(acc, o), rk(ftmp2)], writes=[rk(acc, o)])

        for c in range(NCH):
            P.op("DVE", lambda e, c=c: e.tensor_single_scalar(
                xre[:, c, :], xh[:, c, :], fncol[:, c:c + 1], ALU.mult),
                reads=[rk(xh, c), rk(fncol)], writes=[rk(xre, c)])

        def fg_evac(mb, _tt, bk):
            P.op("ACT", lambda e, mb=mb, bk=bk: e.activation(
                gt[:, mb, :], bk[:, :], AF.Gelu),
                reads=[rk(bk)], writes=[rk(gt, mb)])
        proj_stream(ctx, lambda mb: fg_r[mb],
                    4, NCH, xre_rhs, fg_evac, tts=(tt,))

        def fu_evac(mb, _tt, bk):
            P.op("DVE", lambda e, mb=mb, bk=bk: e.tensor_tensor(
                gt[:, mb, :], gt[:, mb, :], bk[:, :], ALU.mult),
                reads=[rk(gt, mb), rk(bk)], writes=[rk(gt, mb)])
        proj_stream(ctx, lambda mb: fu_r[mb],
                    4, NCH, xre_rhs, fu_evac, tts=(tt,))
        for k in range(4):
            b = k % 2
            P.dma("PQ", w2buf[:, b, :, :], fd_r[:, k, :, :], writes=[rk(w2buf, b)])
            for o in range(NCH):
                P.op("PE", lambda e, k=k, b=b, o=o: e.matmul(
                    bank[o][:, :], w2buf[:, b, o, :], gt[:, k, :],
                    start=(k == 0), stop=(k == 3), skip_group_check=True),
                    reads=[rk(w2buf, b), rk(gt, k)], writes=[rk(bank[o])])
        for o in range(NCH):
            P.op("ACT", lambda e, o=o: e.copy(facc[:, o, :], bank[o][:, :]),
                 reads=[rk(bank[o])], writes=[rk(facc, o)])

        ctx.tm_cols_to_row(s0_tm, tt * 4, ctx.rowA, 0, bank[1], n=TT)
        ctx.tm_cols_to_row(ov_tm, tt * 4, ctx.rowB, 0, bank[1], n=TT)
        ctx.bcast_row(ctx.rowA, 0, bank[4])
        ctx.bcast_row(ctx.rowB, 0, bank[5])
        for o in range(NCH):
            P.op("DVE", lambda e, o=o, tsl=tsl: e.tensor_tensor(
                ctx.ftmp[:, :], xn_bf[:, o, tsl], bank[4][:, :], ALU.mult),
                reads=[rk(xn_bf, o), rk(bank[4])], writes=[rk(ctx.ftmp)])
            P.op("DVE", lambda e, o=o: e.tensor_tensor(
                ctx.ftmp[:, :], ctx.ftmp[:, :], acc[:, o, :], ALU.add),
                reads=[rk(ctx.ftmp), rk(acc, o)], writes=[rk(ctx.ftmp)])
            P.op("DVE", lambda e, o=o, tsl=tsl: e.tensor_tensor(
                ftmp2[:, :], facc[:, o, :], xn_bf[:, o, tsl], ALU.add),
                reads=[rk(facc, o), rk(xn_bf, o)], writes=[rk(ftmp2)])
            P.op("DVE", lambda e: e.tensor_tensor(
                ftmp2[:, :], ftmp2[:, :], ctx.ftmp[:, :], ALU.subtract),
                reads=[rk(ftmp2), rk(ctx.ftmp)], writes=[rk(ftmp2)])
            P.op("DVE", lambda e: e.tensor_tensor(
                ftmp2[:, :], ftmp2[:, :], bank[5][:, :], ALU.mult),
                reads=[rk(ftmp2), rk(bank[5])], writes=[rk(ftmp2)])
            P.op("DVE", lambda e: e.tensor_tensor(
                ctx.ftmp[:, :], ctx.ftmp[:, :], ftmp2[:, :], ALU.add),
                reads=[rk(ctx.ftmp), rk(ftmp2)], writes=[rk(ctx.ftmp)])
            P.op("DVE", lambda e, o=o, tsl=tsl: e.tensor_tensor(
                x_sb[:, o, tsl], x_sb[:, o, tsl], ctx.ftmp[:, :], ALU.add),
                reads=[rk(x_sb, o), rk(ctx.ftmp)], writes=[rk(x_sb, o)])
    ctx.scr_release(_mk)



def _quant_thr(ctx, w_r, nm, nel, tag):
    P = ctx.P
    bank = ctx.bank
    gacc = ctx.scr(tag + "_gacc", [128, 1], F32)
    tcol = ctx.scr(tag + "_tcol", [128, 1], F32)
    thr = ctx.scr(tag + "_thr", [128, 1], F32)
    wf32 = ctx.wf32
    P.op("DVE", lambda e: e.memset(gacc[:, :], 0.0), writes=[rk(gacc)])
    for mb in range(nm):
        b = mb % 2
        P.dma("SPQ", wf32[:, b, :, :], w_r(mb), writes=[rk(wf32, b)])
        P.op("DVE", lambda e, b=b: e.tensor_reduce(
            tcol[:, :], wf32[:, b, :, :].rearrange("p a m -> p (a m)"),
            axis=AX.X, op=ALU.add, apply_absolute_value=True),
            reads=[rk(wf32, b)], writes=[rk(tcol)])
        P.op("DVE", lambda e: e.tensor_tensor(gacc[:, :], gacc[:, :], tcol[:, :],
                                              ALU.add),
             reads=[rk(gacc), rk(tcol)], writes=[rk(gacc)])
    P.op("PE", lambda e: e.matmul(bank[0][0:1, 0:1], gacc[:, :],
                                  ctx.ones_f[:, 0:1], start=True, stop=True),
         reads=[rk(gacc), rk(ctx.ones_f)], writes=[rk(bank[0])])
    P.op("ACT", lambda e: e.activation(tcol[0:1, 0:1], bank[0][0:1, 0:1],
                                       AF.Copy, scale=0.5 / nel, bias=5e-07),
         reads=[rk(bank[0])], writes=[rk(tcol)])
    P.op("PE", lambda e: e.matmul(bank[0][:, 0:1], ctx.ones_f[0:1, :],
                                  tcol[0:1, 0:1], start=True, stop=True),
         reads=[rk(tcol), rk(ctx.ones_f)], writes=[rk(bank[0])])
    P.op("ACT", lambda e: e.copy(thr[:, :], bank[0][:, 0:1]),
         reads=[rk(bank[0])], writes=[rk(thr)])
    return thr


def _quant_block(ctx, w_r, mb, thr, wq_bf):
    P = ctx.P
    b = mb % 2
    wf32 = ctx.wf32
    P.dma("SPQ", wf32[:, b, :, :], w_r(mb), writes=[rk(wf32, b)])
    P.op("ACT", lambda e, b=b: e.activation(
        ctx.q_aw[:, :], wf32[:, b, :, :].rearrange("p a m -> p (a m)"), AF.Abs),
        reads=[rk(wf32, b)], writes=[rk(ctx.q_aw)])
    P.op("DVE", lambda e: e.tensor_single_scalar(
        ctx.q_aw[:, :], ctx.q_aw[:, :], thr[:, 0:1], ALU.is_gt),
        reads=[rk(ctx.q_aw), rk(thr)], writes=[rk(ctx.q_aw)])
    P.op("ACT", lambda e, b=b: e.sign(
        ctx.q_sg[:, :], wf32[:, b, :, :].rearrange("p a m -> p (a m)")),
        reads=[rk(wf32, b)], writes=[rk(ctx.q_sg)])
    P.op("DVE", lambda e: e.tensor_tensor(
        wq_bf[:, :, :].rearrange("p a m -> p (a m)"), ctx.q_sg[:, :],
        ctx.q_aw[:, :], ALU.mult),
        reads=[rk(ctx.q_sg), rk(ctx.q_aw)], writes=[rk(wq_bf)])


def l3_ffn(ctx, hb, x_sb, xn_bf):
    nc, P = ctx.nc, ctx.P
    bank = ctx.bank
    H3 = 4096
    _mk = ctx.scr_mark()
    ctx.wf32 = ctx.scr("l3_wf32", [128, 2, 8, 128], F32)
    ctx.q_aw = ctx.scr("l3_aw", [128, 1024], F32)
    ctx.q_sg = ctx.scr("l3_sg", [128, 1024], F32)
    wq_bf = ctx.scr("l3_wq", [128, 8, 128], BF16)
    h_bf = ctx.scr("l3_h", [128, 32, TT], BF16)
    lnw = ctx.scr("l3_lnw", [128, 32], F32)
    lnb = ctx.scr("l3_lnb", [128, 32], F32)
    m_tm = ctx.scr("l3_mtm", [128, 4], F32)
    r_tm = ctx.scr("l3_rtm", [128, 4], F32)
    P.dma("SPQ", lnw[:, :], hb["L3_ffn_lnw_col"].ap(), writes=[rk(lnw)])
    P.dma("SPQ", lnb[:, :], hb["L3_ffn_lnb_col"].ap(), writes=[rk(lnb)])
    wfc_r0 = hb["L3_ffn_W_fcT"].ap()
    wpj_r0 = hb["L3_ffn_W_projT"].ap().rearrange("(k p) (oc m) -> p k oc m",
                                                 p=128, m=128)

    def wfc_r(mb):
        return wfc_r0[mb]

    def wpj_r(k):
        return wpj_r0[:, k, :, :]

    thr1 = _quant_thr(ctx, wfc_r, 32, H3 * D, "q1")
    thr2 = _quant_thr(ctx, wpj_r, 32, H3 * D, "q2")

    if "thr" in ctx.dbg:
        P.dma("SPQ", ctx.dbg["thr"].ap()[:, 0:1], thr1[:, :], reads=[rk(thr1)],
              allow_slow_non_contiguous=True)
        P.dma("SPQ", ctx.dbg["thr"].ap()[:, 1:2], thr2[:, :], reads=[rk(thr2)],
              allow_slow_non_contiguous=True)
    for tt in range(NTT):
        tsl = slice(tt * TT, (tt + 1) * TT)
        for mb in range(32):
            _quant_block(ctx, wfc_r, mb, thr1, wq_bf)
            bk = bank[6 + mb % 2]
            for c in range(NCH):
                P.op("PE", lambda e, mb=mb, c=c, tsl=tsl, bk=bk: e.matmul(
                    bk[:, :], wq_bf[:, c, :], xn_bf[:, c, tsl],
                    start=(c == 0), stop=(c == NCH - 1), skip_group_check=True),
                    reads=[rk(wq_bf), rk(xn_bf, c)], writes=[rk(bk)])
            P.op("ACT", lambda e, mb=mb, bk=bk: e.activation(
                h_bf[:, mb, :], bk[:, :], AF.Gelu),
                reads=[rk(bk)], writes=[rk(h_bf, mb)])
        if tt == 0 and "h3" in ctx.dbg:
            P.dma("SPQ", ctx.dbg["h3"].ap(), h_bf[:, :, :],
                  reads=[rk(h_bf, c) for c in range(32)])
        for c in range(32):
            P.op("ACT", lambda e, c=c: e.activation(
                ctx.sq[:, :], h_bf[:, c, :], AF.Square),
                reads=[rk(h_bf, c)], writes=[rk(ctx.sq)])
            P.op("PE", lambda e, c=c: e.matmul(
                bank[0][0:1, :], ctx.ones_b[:, 0:1], h_bf[:, c, :],
                start=(c == 0), stop=(c == 31), skip_group_check=True),
                reads=[rk(h_bf, c), rk(ctx.ones_b)], writes=[rk(bank[0])])
            P.op("PE", lambda e, c=c: e.matmul(
                bank[1][0:1, :], ctx.ones_f[:, 0:1], ctx.sq[:, :],
                start=(c == 0), stop=(c == 31), skip_group_check=True),
                reads=[rk(ctx.sq), rk(ctx.ones_f)], writes=[rk(bank[1])])
        P.op("ACT", lambda e: e.copy(ctx.rowA[0:1, 0:TT], bank[0][0:1, :]),
             reads=[rk(bank[0])], writes=[rk(ctx.rowA)])
        P.op("ACT", lambda e: e.copy(ctx.rowB[0:1, 0:TT], bank[1][0:1, :]),
             reads=[rk(bank[1])], writes=[rk(ctx.rowB)])
        ctx.row_to_tm_cols(ctx.rowA, 0, m_tm, 0, bank[1], n=TT, scale=1.0 / H3)
        ctx.row_to_tm_cols(ctx.rowB, 0, r_tm, 0, bank[1], n=TT, scale=1.0 / H3)
        P.op("DVE", lambda e: e.scalar_tensor_tensor(
            ctx.sq[:, 0:4], m_tm[:, :], 1.0, m_tm[:, :], ALU.mult, ALU.mult),
            reads=[rk(m_tm)], writes=[rk(ctx.sq)])
        P.op("DVE", lambda e: e.tensor_tensor(r_tm[:, :], r_tm[:, :],
                                              ctx.sq[:, 0:4], ALU.subtract),
             reads=[rk(r_tm), rk(ctx.sq)], writes=[rk(r_tm)])
        P.op("ACT", lambda e: e.activation(r_tm[:, :], r_tm[:, :], AF.Ln,
                                           bias=1e-5),
             reads=[rk(r_tm)], writes=[rk(r_tm)])
        P.op("ACT", lambda e: e.activation(r_tm[:, :], r_tm[:, :], AF.Exp,
                                           scale=-0.5),
             reads=[rk(r_tm)], writes=[rk(r_tm)])
        P.op("DVE", lambda e: e.tensor_tensor(m_tm[:, :], m_tm[:, :], r_tm[:, :],
                                              ALU.mult),
             reads=[rk(m_tm), rk(r_tm)], writes=[rk(m_tm)])
        ctx.tm_cols_to_row(r_tm, 0, ctx.rowA, 0, bank[1], n=TT)
        ctx.tm_cols_to_row(m_tm, 0, ctx.rowB, 0, bank[1], n=TT)
        ctx.bcast_row(ctx.rowA, 0, bank[4])
        ctx.bcast_row(ctx.rowB, 0, bank[5])
        for c in range(32):
            P.op("DVE", lambda e, c=c: e.tensor_tensor(
                ctx.ftmp[:, :], h_bf[:, c, :], bank[4][:, :], ALU.mult),
                reads=[rk(h_bf, c), rk(bank[4])], writes=[rk(ctx.ftmp)])
            P.op("DVE", lambda e, c=c: e.tensor_tensor(
                ctx.ftmp[:, :], ctx.ftmp[:, :], bank[5][:, :], ALU.subtract),
                reads=[rk(ctx.ftmp), rk(bank[5])], writes=[rk(ctx.ftmp)])
            P.op("DVE", lambda e, c=c: e.tensor_scalar(
                h_bf[:, c, :], ctx.ftmp[:, :], lnw[:, c:c + 1], lnb[:, c:c + 1],
                ALU.mult, ALU.add),
                reads=[rk(ctx.ftmp), rk(lnw), rk(lnb)], writes=[rk(h_bf, c)])
        if tt == 0 and "hn3" in ctx.dbg:
            P.dma("SPQ", ctx.dbg["hn3"].ap(), h_bf[:, :, :],
                  reads=[rk(h_bf, c) for c in range(32)])
        for k in range(32):
            _quant_block(ctx, wpj_r, k, thr2, wq_bf)
            for o in range(NCH):
                P.op("PE", lambda e, k=k, o=o: e.matmul(
                    bank[o][:, :], wq_bf[:, o, :], h_bf[:, k, :],
                    start=(k == 0), stop=(k == 31), skip_group_check=True),
                    reads=[rk(wq_bf), rk(h_bf, k)], writes=[rk(bank[o])])
        for o in range(NCH):
            P.op("DVE", lambda e, o=o, tsl=tsl: e.scalar_tensor_tensor(
                x_sb[:, o, tsl], bank[o][:, :], 1.0, x_sb[:, o, tsl],
                ALU.mult, ALU.add),
                reads=[rk(bank[o]), rk(x_sb, o)], writes=[rk(x_sb, o)])
    ctx.scr_release(_mk)



def l3_attn(ctx, hb, x_sb, xn_bf):
    nc, P = ctx.nc, ctx.P
    bank = ctx.bank
    RMSE = 1.1920929e-07
    _mk = ctx.scr_mark()
    vk_f = ctx.scr("l3_vk", [128, NCH, 65], F32)
    vk_bf = ctx.scr("l3_vkb", [128, NCH, 65], BF16)
    gq = ctx.scr("l3_gq", [128, NCH], F32)
    gk = ctx.scr("l3_gk", [128, NCH], F32)
    gout = ctx.scr("l3_go", [128, NCH], F32)
    bqkv = ctx.scr("l3_bq", [128, 24], F32)
    bout = ctx.scr("l3_bo", [128, NCH], F32)
    ctab = ctx.scr("l3_ct", [128, T], BF16)
    stab = ctx.scr("l3_st", [128, T], BF16)
    pswp = ctx.scr("l3_ps", [128, 128], BF16)
    P.dma("SPQ", gq[:, :], hb["L3_attn_gq_col"].ap(), writes=[rk(gq)])
    P.dma("SPQ", gk[:, :], hb["L3_attn_gk_col"].ap(), writes=[rk(gk)])
    P.dma("SPQ", gout[:, :], hb["L3_attn_gout_col"].ap(), writes=[rk(gout)])
    P.dma("SPQ", bqkv[:, :], hb["L3_attn_bqkv_col"].ap(), writes=[rk(bqkv)])
    P.dma("SPQ", bout[:, :], hb["L3_attn_bout_col"].ap(), writes=[rk(bout)])
    th = 1.0 / 10000.0 ** (np.arange(0, 64, 2, dtype=np.float64) / 64.0)
    tpos = np.arange(T, dtype=np.float64)
    fr = np.outer(th, tpos)
    cos_np = np.cos(fr)
    sin_np = np.sin(fr)
    ctab_np = np.zeros((128, T), np.float32)
    stab_np = np.zeros((128, T), np.float32)
    for blk in (0, 64):
        ctab_np[blk:blk + 32] = cos_np
        ctab_np[blk + 32:blk + 64] = cos_np
        stab_np[blk:blk + 32] = sin_np
        stab_np[blk + 32:blk + 64] = sin_np
    psw_np = np.zeros((128, 128), np.float32)
    for blk in (0, 64):
        for i in range(32):
            psw_np[blk + i, blk + i + 32] = 1.0
            psw_np[blk + i + 32, blk + i] = -1.0
    ch = nc.inline_tensor(ctab_np.astype(ml_dtypes.bfloat16), "c_l3cos")
    sh = nc.inline_tensor(stab_np.astype(ml_dtypes.bfloat16), "c_l3sin")
    ph = nc.inline_tensor(psw_np.astype(ml_dtypes.bfloat16), "c_l3swp")
    P.dma("SPQ", ctab[:, :], ch[:, :], writes=[rk(ctab)])
    P.dma("SPQ", stab[:, :], sh[:, :], writes=[rk(stab)])
    P.dma("SPQ", pswp[:, :], ph[:, :], writes=[rk(pswp)])

    wq_r = hb["L3_attn_WqkvT"].ap()

    def rope_relu(dst_ap, dkeys, col, wid, gcol_ap, gread, rbank, rowr, roff):
        P.op("DVE", lambda e: e.scalar_tensor_tensor(
            dst_ap, dst_ap, gcol_ap, rbank, ALU.mult, ALU.mult),
            reads=dkeys + [gread, rk(rbank)], writes=dkeys)
        P.op("PE", lambda e: e.matmul(
            bank[2][:, 0:wid], pswp[:, :], dst_ap, start=True, stop=True),
            reads=dkeys + [rk(pswp)], writes=[rk(bank[2])])
        P.op("DVE", lambda e: e.tensor_tensor(
            ctx.ftmp[:, 0:wid], dst_ap, ctab[:, col:col + wid], ALU.mult),
            reads=dkeys + [rk(ctab)], writes=[rk(ctx.ftmp)])
        P.op("DVE", lambda e: e.tensor_tensor(
            ctx.sq[:, 0:wid], bank[2][:, 0:wid], stab[:, col:col + wid], ALU.mult),
            reads=[rk(bank[2]), rk(stab)], writes=[rk(ctx.sq)])
        P.op("DVE", lambda e: e.tensor_tensor(
            ctx.ftmp[:, 0:wid], ctx.ftmp[:, 0:wid], ctx.sq[:, 0:wid], ALU.add),
            reads=[rk(ctx.ftmp), rk(ctx.sq)], writes=[rk(ctx.ftmp)])
        P.op("DVE", lambda e: e.tensor_single_scalar(
            dst_ap, ctx.ftmp[:, 0:wid], 0.0, ALU.max),
            reads=[rk(ctx.ftmp)], writes=dkeys)

    _mk2 = ctx.scr_mark()
    wk_sb = ctx.scr("l3_wk", [128, NCH, NCH, 128], BF16)
    wv_sb = ctx.scr("l3_wv", [128, NCH, NCH, 128], BF16)
    kc = ctx.scr("l3_kc", [128, NCH, 128], BF16)
    vc = ctx.scr("l3_vc", [128, NCH, 128], BF16)
    kT_sb = ctx.scr("l3_kT", [128, 64], BF16)
    vT_sb = ctx.scr("l3_vT", [128, 65], BF16)
    for fc in range(NCH):
        P.dma("PQ", wk_sb[:, :, fc, :], wq_r[8 + fc],
              writes=[rk(wk_sb, fc)])
        P.dma("PQ", wv_sb[:, :, fc, :], wq_r[16 + fc],
              writes=[rk(wv_sb, fc)])
    for c in range(NCH):
        P.op("DVE", lambda e, c=c: e.memset(vk_f[:, c, :], 0.0),
             writes=[rk(vk_f, c)])
    P.op("DVE", lambda e: e.memset(vT_sb[:, 64:65], 1.0), writes=[rk(vT_sb, "one")])
    b7, b6 = bank[7], bank[6]
    ctx.psum_free(bank[7])
    ctx.psum_free(bank[6])
    tp0 = ctx.psum(ctx.u("l3tp"), [128, 512], BF16)
    P.inherit(rk(tp0), rk(b7))
    nb6 = ctx.psum(ctx.u("l3b6"), [128, 512], F32)
    P.inherit(rk(nb6), rk(b6))
    for tc in range(16):
        csl = slice(tc * 128, (tc + 1) * 128)
        for fc in range(NCH):
            for c in range(NCH):
                P.op("PE", lambda e, fc=fc, c=c, csl=csl: e.matmul(
                    nb6[:, 0:128], wk_sb[:, c, fc, :], xn_bf[:, c, csl],
                    start=(c == 0), stop=(c == NCH - 1), skip_group_check=True),
                    reads=[rk(wk_sb, fc), rk(xn_bf, c)], writes=[rk(nb6)])
            P.op("ACT", lambda e, fc=fc: e.activation(
                kc[:, fc, :], nb6[:, 0:128], AF.Identity,
                bias=bqkv[:, 8 + fc:9 + fc]),
                reads=[rk(nb6), rk(bqkv)], writes=[rk(kc, fc)])
            for c in range(NCH):
                P.op("PE", lambda e, fc=fc, c=c, csl=csl: e.matmul(
                    bank[5][:, 0:128], wv_sb[:, c, fc, :], xn_bf[:, c, csl],
                    start=(c == 0), stop=(c == NCH - 1), skip_group_check=True),
                    reads=[rk(wv_sb, fc), rk(xn_bf, c)], writes=[rk(bank[5])])
            P.op("ACT", lambda e, fc=fc: e.activation(
                vc[:, fc, :], bank[5][:, 0:128], AF.Identity,
                bias=bqkv[:, 16 + fc:17 + fc]),
                reads=[rk(bank[5]), rk(bqkv)], writes=[rk(vc, fc)])
        for fc in range(NCH):
            P.op("ACT", lambda e, fc=fc: e.activation(
                ctx.sq[:, 0:128], kc[:, fc, :], AF.Square),
                reads=[rk(kc, fc)], writes=[rk(ctx.sq)])
            P.op("PE", lambda e, fc=fc: e.matmul(
                bank[0][0:1, 0:128], ctx.ones_f[:, 0:1], ctx.sq[:, 0:128],
                start=(fc == 0), stop=(fc == NCH - 1), skip_group_check=True),
                reads=[rk(ctx.sq), rk(ctx.ones_f)], writes=[rk(bank[0])])
        P.op("ACT", lambda e: e.activation(
            ctx.rowB[0:1, 0:128], bank[0][0:1, 0:128], AF.Ln, bias=RMSE,
            scale=1.0 / D),
            reads=[rk(bank[0])], writes=[rk(ctx.rowB)])
        P.op("ACT", lambda e: e.activation(
            ctx.rowB[0:1, 0:128], ctx.rowB[0:1, 0:128], AF.Exp, scale=-0.5),
            reads=[rk(ctx.rowB)], writes=[rk(ctx.rowB)])
        P.op("PE", lambda e: e.matmul(
            bank[3][:, 0:128], ctx.ones_f[0:1, :], ctx.rowB[0:1, 0:128],
            start=True, stop=True),
            reads=[rk(ctx.rowB), rk(ctx.ones_f)], writes=[rk(bank[3])])
        for fc in range(NCH):
            rope_relu(kc[:, fc, :], [rk(kc, fc)], tc * 128, 128,
                      gk[:, fc:fc + 1], rk(gk), bank[3][:, 0:128], None, 0)
        for h in range(16):
            hc, hb64 = h // 2, 64 * (h % 2)
            P.op("PE", lambda e, hc=hc, hb64=hb64: e.transpose(
                tp0[:, 0:64], kc[hb64:hb64 + 64, hc, :],
                ctx.id_b[hb64:hb64 + 64, hb64:hb64 + 64]),
                reads=[rk(kc, hc), rk(ctx.id_b)], writes=[rk(tp0)])
            P.op("ACT", lambda e: e.copy(kT_sb[:, :], tp0[:, 0:64]),
                 reads=[rk(tp0)], writes=[rk(kT_sb)])
            P.op("PE", lambda e, hc=hc, hb64=hb64: e.transpose(
                tp0[:, 64:128], vc[hb64:hb64 + 64, hc, :],
                ctx.id_b[hb64:hb64 + 64, hb64:hb64 + 64]),
                reads=[rk(vc, hc), rk(ctx.id_b)], writes=[rk(tp0)])
            P.op("ACT", lambda e: e.copy(vT_sb[:, 0:64], tp0[:, 64:128]),
                 reads=[rk(tp0)], writes=[rk(vT_sb)])
            P.op("PE", lambda e, hb64=hb64: e.matmul(
                bank[5][hb64:hb64 + 64, 0:65], kT_sb[:, :], vT_sb[:, :],
                start=True, stop=True, skip_group_check=True),
                reads=[rk(kT_sb), rk(vT_sb), rk(vT_sb, "one")],
                writes=[rk(bank[5])]) if hb64 == 0 else \
            P.op("PE", lambda e, hb64=hb64: e.matmul(
                bank[5][hb64:hb64 + 64, 0:65], kT_sb[:, :], vT_sb[:, :],
                start=True, stop=True, skip_group_check=True),
                reads=[rk(kT_sb), rk(vT_sb), rk(vT_sb, "one")],
                writes=[rk(bank[5])])
            P.op("DVE", lambda e, hc=hc, hb64=hb64: e.tensor_tensor(
                vk_f[hb64:hb64 + 64, hc, :], vk_f[hb64:hb64 + 64, hc, :],
                bank[5][hb64:hb64 + 64, 0:65], ALU.add),
                reads=[rk(vk_f, hc), rk(bank[5])], writes=[rk(vk_f, hc)])
    for c in range(NCH):
        P.op("DVE", lambda e, c=c: e.tensor_copy(vk_bf[:, c, :], vk_f[:, c, :]),
             reads=[rk(vk_f, c)], writes=[rk(vk_bf, c)])
    ctx.psum_free(nb6)
    ctx.psum_free(tp0)
    rb7 = ctx.psum(ctx.u("b7r"), [128, 512], F32)
    rb6 = ctx.psum(ctx.u("b6r"), [128, 512], F32)
    P.inherit(rk(rb7), rk(tp0))
    P.inherit(rk(rb6), rk(nb6))
    ctx.bank[6] = rb6
    ctx.bank[7] = rb7
    bank = ctx.bank
    ctx.scr_release(_mk2)

    qf = ctx.scr("l3_qf", [128, NCH, T], BF16)
    def q_src(mb):
        return wq_r[mb]

    def xn_rhs(c, tt):
        return xn_bf[:, c, tt * TT:(tt + 1) * TT], [rk(xn_bf, c)]

    def q_evac(mb, tt, bk):
        tsl = slice(tt * TT, (tt + 1) * TT)
        P.op("ACT", lambda e, mb=mb, bk=bk: e.copy(
            qf[:, mb, tsl], bk[:, :]),
            reads=[rk(bk)], writes=[rk(qf, mb)]) if False else None
        P.op("ACT", lambda e, mb=mb, bk=bk, tsl=tsl: e.activation(
            qf[:, mb, tsl], bk[:, :], AF.Identity, bias=bqkv[:, mb:mb + 1]),
            reads=[rk(bk), rk(bqkv)], writes=[rk(qf, mb)])
    proj_stream(ctx, q_src, NCH, NCH, xn_rhs, q_evac)
    for tt in range(NTT):
        tsl = slice(tt * TT, (tt + 1) * TT)
        for c in range(NCH):
            P.op("ACT", lambda e, c=c, tsl=tsl: e.activation(
                ctx.sq[:, 0:TT], qf[:, c, tsl], AF.Square),
                reads=[rk(qf, c)], writes=[rk(ctx.sq)])
            P.op("PE", lambda e, c=c: e.matmul(
                bank[0][0:1, :], ctx.ones_f[:, 0:1], ctx.sq[:, 0:TT],
                start=(c == 0), stop=(c == NCH - 1), skip_group_check=True),
                reads=[rk(ctx.sq), rk(ctx.ones_f)], writes=[rk(bank[0])])
        P.op("ACT", lambda e, tsl=tsl: e.activation(
            ctx.rowA[0:1, tsl], bank[0][0:1, :], AF.Ln, bias=RMSE, scale=1.0 / D),
            reads=[rk(bank[0])], writes=[rk(ctx.rowA)])
    P.op("ACT", lambda e: e.activation(ctx.rowA[0:1, :], ctx.rowA[0:1, :],
                                       AF.Exp, scale=-0.5),
         reads=[rk(ctx.rowA)], writes=[rk(ctx.rowA)])
    for tt in range(NTT):
        tsl = slice(tt * TT, (tt + 1) * TT)
        ctx.bcast_row(ctx.rowA, tt * TT, bank[3])
        for c in range(NCH):
            rope_relu(qf[:, c, tsl], [rk(qf, c)], tt * TT, TT,
                      gq[:, c:c + 1], rk(gq), bank[3][:, :], None, 0)

    if "qf" in ctx.dbg:
        P.dma("SPQ", ctx.dbg["qf"].ap(), qf[:, :, :],
              reads=[rk(qf, c) for c in range(NCH)])
    if "vk" in ctx.dbg:
        P.dma("SPQ", ctx.dbg["vk"].ap(), vk_bf[:, :, :],
              reads=[rk(vk_bf, c) for c in range(NCH)])
    zrow_sb = ctx.scr("l3_zr", [65, TT], F32)
    att = xn_bf
    for hp in range(NCH):
        for hb64 in (0, 64):
            zrow = 64 - hb64
            for tt in range(NTT):
                tsl = slice(tt * TT, (tt + 1) * TT)
                P.op("PE", lambda e, hp=hp, hb64=hb64, tsl=tsl: e.matmul(
                    bank[1][hb64:hb64 + 64, :], vk_bf[hb64:hb64 + 64, hp, 0:64],
                    qf[hb64:hb64 + 64, hp, tsl], start=True, stop=True,
                    skip_group_check=True),
                    reads=[rk(vk_bf, hp), rk(qf, hp)], writes=[rk(bank[1])])
                P.op("PE", lambda e, hp=hp, hb64=hb64, tsl=tsl, zrow=zrow: e.matmul(
                    bank[1][zrow:zrow + 1, :], vk_bf[hb64:hb64 + 64, hp, 64:65],
                    qf[hb64:hb64 + 64, hp, tsl], start=True, stop=True,
                    skip_group_check=True),
                    reads=[rk(vk_bf, hp), rk(qf, hp)], writes=[rk(bank[1])])
                P.op("ACT", lambda e, hp=hp, hb64=hb64, tsl=tsl: e.copy(
                    att[hb64:hb64 + 64, hp, tsl], bank[1][hb64:hb64 + 64, :]),
                    reads=[rk(bank[1])], writes=[rk(att, hp)])
                P.op("ACT", lambda e, zrow=zrow: e.copy(
                    zrow_sb[zrow:zrow + 1, :], bank[1][zrow:zrow + 1, :]),
                    reads=[rk(bank[1])], writes=[rk(zrow_sb)])
                P.op("DVE", lambda e, zrow=zrow: e.tensor_single_scalar(
                    zrow_sb[zrow:zrow + 1, :], zrow_sb[zrow:zrow + 1, :], 1e-6,
                    ALU.add),
                    reads=[rk(zrow_sb)], writes=[rk(zrow_sb)])
                P.op("DVE", lambda e, zrow=zrow: e.reciprocal(
                    zrow_sb[zrow:zrow + 1, :], zrow_sb[zrow:zrow + 1, :]),
                    reads=[rk(zrow_sb)], writes=[rk(zrow_sb)])
                P.op("PE", lambda e, hb64=hb64, zrow=zrow: e.matmul(
                    bank[3][hb64:hb64 + 64, :], ctx.ones_f[zrow:zrow + 1, 0:64],
                    zrow_sb[zrow:zrow + 1, :], start=True, stop=True),
                    reads=[rk(zrow_sb), rk(ctx.ones_f)], writes=[rk(bank[3])])
                P.op("DVE", lambda e, hp=hp, hb64=hb64, tsl=tsl: e.tensor_tensor(
                    att[hb64:hb64 + 64, hp, tsl], att[hb64:hb64 + 64, hp, tsl],
                    bank[3][hb64:hb64 + 64, :], ALU.mult),
                    reads=[rk(att, hp), rk(bank[3])], writes=[rk(att, hp)])

    pre = ctx.scr("l3_pre", [128, NCH, TT], BF16)
    wo_r = hb["L3_attn_WoutT"].ap()
    wb = ctx.wstream
    for tt in range(NTT):
        tsl = slice(tt * TT, (tt + 1) * TT)
        for mb in range(NCH):
            b = mb % 2
            P.dma("PQ", wb[:, b, 0:NCH, :],
                  wo_r[mb], writes=[rk(wb, b)])
            bk = bank[6 + mb % 2]
            for c in range(NCH):
                P.op("PE", lambda e, mb=mb, b=b, c=c, tsl=tsl, bk=bk: e.matmul(
                    bk[:, :], wb[:, b, c, :], att[:, c, tsl],
                    start=(c == 0), stop=(c == NCH - 1), skip_group_check=True),
                    reads=[rk(wb, b), rk(att, c)], writes=[rk(bk)])
            P.op("ACT", lambda e, mb=mb, bk=bk: e.activation(
                pre[:, mb, :], bk[:, :], AF.Identity, bias=bout[:, mb:mb + 1]),
                reads=[rk(bk), rk(bout)], writes=[rk(pre, mb)])
        for c in range(NCH):
            P.op("ACT", lambda e, c=c: e.activation(
                ctx.sq[:, 0:TT], pre[:, c, :], AF.Square),
                reads=[rk(pre, c)], writes=[rk(ctx.sq)])
            P.op("PE", lambda e, c=c: e.matmul(
                bank[0][0:1, :], ctx.ones_f[:, 0:1], ctx.sq[:, 0:TT],
                start=(c == 0), stop=(c == NCH - 1), skip_group_check=True),
                reads=[rk(ctx.sq), rk(ctx.ones_f)], writes=[rk(bank[0])])
        P.op("ACT", lambda e: e.activation(
            ctx.rowA[0:1, 0:TT], bank[0][0:1, :], AF.Ln, bias=RMSE, scale=1.0 / D),
            reads=[rk(bank[0])], writes=[rk(ctx.rowA)])
        P.op("ACT", lambda e: e.activation(
            ctx.rowA[0:1, 0:TT], ctx.rowA[0:1, 0:TT], AF.Exp, scale=-0.5),
            reads=[rk(ctx.rowA)], writes=[rk(ctx.rowA)])
        ctx.bcast_row(ctx.rowA, 0, bank[3])
        for c in range(NCH):
            P.op("DVE", lambda e, c=c: e.scalar_tensor_tensor(
                ctx.ftmp[:, :], pre[:, c, :], gout[:, c:c + 1], bank[3][:, :],
                ALU.mult, ALU.mult),
                reads=[rk(pre, c), rk(gout), rk(bank[3])], writes=[rk(ctx.ftmp)])
            P.op("DVE", lambda e, c=c, tsl=tsl: e.tensor_tensor(
                x_sb[:, c, tsl], x_sb[:, c, tsl], ctx.ftmp[:, :], ALU.add),
                reads=[rk(x_sb, c), rk(ctx.ftmp)], writes=[rk(x_sb, c)])
    ctx.scr_release(_mk)


def build_launch2(arrs, blocks=(2, 3), debug=None):
    ctx = Ctx()
    nc = ctx.nc
    hb = {}
    for name, a in arrs.items():
        if name.startswith(("L2_", "L3_")):
            hb[name] = nc.dram_tensor(name, list(a.shape), F32, kind="ExternalInput")
    x_in = nc.dram_tensor("x_in", [128, NCH, T], F32, kind="ExternalInput")
    xn_in = nc.dram_tensor("xn_in", [128, NCH, T], BF16, kind="ExternalInput")
    i1_in = nc.dram_tensor("i1_in", [128, 16, 8], F32, kind="ExternalInput")
    i2_in = nc.dram_tensor("i2_in", [128, 16, 8], F32, kind="ExternalInput")
    p1_in = nc.dram_tensor("p1_in", [128, 16], F32, kind="ExternalInput")
    p2_in = nc.dram_tensor("p2_in", [128, 16], F32, kind="ExternalInput")
    cn_in = nc.dram_tensor("cn_in", [8, 8], F32, kind="ExternalInput")
    cm_in = nc.dram_tensor("cm_in", [8, 1], F32, kind="ExternalInput")
    x_out = nc.dram_tensor("x_out", [128, NCH, T], F32, kind="ExternalOutput")
    ctx.setup()
    P = ctx.P
    x_sb = ctx.sb("x_sb", [128, NCH, T], F32)
    xn_bf = ctx.sb("xn_bf", [128, NCH, T], BF16)
    ind1 = ctx.sb("ind1", [128, 16, 8], F32)
    ind2 = ctx.sb("ind2", [128, 16, 8], F32)
    p1 = ctx.sb("p1", [128, 16], F32)
    p2 = ctx.sb("p2", [128, 16], F32)
    cnts = ctx.sb("cnts", [8, 8], F32)
    cmask = ctx.sb("cmask", [8, 1], F32)
    ctx.init_scratch()
    ctx.dbg = {}
    if debug:
        for nm, shape, dt_ in debug:
            ctx.dbg[nm] = nc.dram_tensor("dbg_" + nm, shape, dt_,
                                         kind="ExternalOutput")
    P.dma("SPQ", x_sb[:, :, :], x_in.ap(),
          writes=[rk(x_sb, c) for c in range(NCH)])
    P.dma("SPQ", xn_bf[:, :, :], xn_in.ap(),
          writes=[rk(xn_bf, c) for c in range(NCH)])
    P.dma("SPQ", ind1[:, :, :], i1_in.ap(),
          writes=[rk(ind1, j) for j in range(16)])
    P.dma("SPQ", ind2[:, :, :], i2_in.ap(),
          writes=[rk(ind2, j) for j in range(16)])
    P.dma("SPQ", p1[:, :], p1_in.ap(), writes=[rk(p1)])
    P.dma("SPQ", p2[:, :], p2_in.ap(), writes=[rk(p2)])
    P.dma("SPQ", cnts[:, :], cn_in.ap(), writes=[rk(cnts)])
    P.dma("SPQ", cmask[:, :], cm_in.ap(), writes=[rk(cmask)])

    if 2 in blocks:
        l2_attn(ctx, hb, x_sb, xn_bf)
        l2_ffn(ctx, hb, x_sb, xn_bf, ind1, ind2, p1, p2, cnts, cmask)
    if 3 in blocks:
        ln_block(ctx, x_sb, xn_bf)
        if "xn3" in ctx.dbg:
            P.dma("SPQ", ctx.dbg["xn3"].ap(), xn_bf[:, :, :],
                  reads=[rk(xn_bf, c) for c in range(NCH)])
        l3_ffn(ctx, hb, x_sb, xn_bf)
        if "xf3" in ctx.dbg:
            P.dma("SPQ", ctx.dbg["xf3"].ap(), x_sb[:, :, :],
                  reads=[rk(x_sb, c) for c in range(NCH)])
        l3_attn(ctx, hb, x_sb, xn_bf)

    P.dma("SPQ", x_out.ap(), x_sb[:, :, :],
          reads=[rk(x_sb, c) for c in range(NCH)])
    with nc.Block() as block:
        P.emit(block)
    return ctx


def build_fused(arrs, debug=None):
    ctx = Ctx()
    nc = ctx.nc
    hb = {}
    for name, a in arrs.items():
        hb[name] = nc.dram_tensor(name, list(a.shape), F32, kind="ExternalInput")
    x_in = nc.dram_tensor("x_in", [128, NCH, T], F32, kind="ExternalInput")
    cm_in = nc.dram_tensor("cm_in", [8, 1], F32, kind="ExternalInput")
    x_out = nc.dram_tensor("x_out", [128, NCH, T], F32, kind="ExternalOutput")
    cc_in = nc.dram_tensor("cc_cnt_in", [1, 8], F32)
    cc_out = nc.dram_tensor("cc_cnt_out", [8, 8], F32)
    ctx.setup()
    P = ctx.P
    x_sb = ctx.sb("x_sb", [128, NCH, T], F32)
    xn_bf = ctx.sb("xn_bf", [128, NCH, T], BF16)
    ind1 = ctx.sb("ind1", [128, 16, 8], F32)
    ind2 = ctx.sb("ind2", [128, 16, 8], F32)
    p1 = ctx.sb("p1", [128, 16], F32)
    p2 = ctx.sb("p2", [128, 16], F32)
    cnt_sb = ctx.sb("cnt_sb", [1, 8], F32)
    cnts = ctx.sb("cnts", [8, 8], F32)
    cmask = ctx.sb("cmask", [8, 1], F32)
    ctx.init_scratch()
    ctx.dbg = {}
    if debug:
        for nm, shape, dt_ in debug:
            ctx.dbg[nm] = nc.dram_tensor("dbg_" + nm, shape, dt_,
                                         kind="ExternalOutput")
    P.dma("SPQ", x_sb[:, :, :], x_in.ap(),
          writes=[rk(x_sb, c) for c in range(NCH)])
    P.dma("SPQ", cmask[:, :], cm_in.ap(), writes=[rk(cmask)])

    mk = ctx.scr_mark()
    gate_sb = ctx.scr("l0_gate", [128, NCH, 8], F32)
    lg = ctx.scr("l0_lg", [128, 16, 8], F32)
    c_tm = ctx.scr("l0_ctm", [128, 16, 8], F32)
    P.dma("SPQ", gate_sb[:, :, :],
          hb["L0_ffn_gate_W"].ap().rearrange("(c p) e -> p c e", p=128),
          writes=[rk(gate_sb)])
    ln_block(ctx, x_sb, xn_bf, router_w=gate_sb, lg_tm=lg)
    l0_attn(ctx, hb, x_sb, xn_bf)
    topk_combine_l0(ctx, lg, c_tm)
    moe_ffn_v2(ctx, hb, x_sb, xn_bf, c_tm, "l0f", "L0_ffn_W1", "L0_ffn_b1",
               "L0_ffn_W2", "L0_ffn_b2", act=AF.Gelu,
               shared={"w": "L0_ffn_shared_W", "b": "L0_ffn_shared_b_col"},
               out_ln={"g": "L0_ffn_gamma_col", "b": "L0_ffn_beta_col"},
               groups=((0, 1), (2, 3)))
    ctx.scr_release(mk)

    mk = ctx.scr_mark()
    r_sb = ctx.scr("l1_wr", [128, NCH, 8], F32)
    br_bc = ctx.scr("l1_br", [128, 8], F32)
    lg = ctx.scr("l1_lg", [128, 16, 8], F32)
    c_tm = ctx.scr("l1_ctm", [128, 16, 8], F32)
    P.dma("SPQ", r_sb[:, :, :],
          hb["L1_ffn_Wr"].ap().rearrange("(c p) e -> p c e", p=128),
          writes=[rk(r_sb)])
    P.dma("SPQ", br_bc[:, :],
          hb["L1_ffn_br"].ap().partition_broadcast(128)[:, 0, :],
          writes=[rk(br_bc)])
    ln_block(ctx, x_sb, xn_bf, router_w=r_sb, lg_tm=lg)
    l1_attn(ctx, hb, x_sb, xn_bf)
    topk_combine_l1(ctx, lg, c_tm, br_bc)
    moe_ffn_v2(ctx, hb, x_sb, xn_bf, c_tm, "l1f", "L1_ffn_W1", "L1_ffn_b1",
               "L1_ffn_W2", "L1_ffn_b2", act=AF.Relu,
               groups=((0, 1), (2, 3)))
    ctx.scr_release(mk)

    mk = ctx.scr_mark()
    r2_sb = ctx.scr("l2_wr", [128, NCH, 8], F32)
    lg2 = ctx.scr("l2_lg", [128, 16, 8], F32)
    P.dma("SPQ", r2_sb[:, :, :],
          hb["L2_ffn_Wr"].ap().rearrange("(c p) e -> p c e", p=128),
          writes=[rk(r2_sb)])
    ln_block(ctx, x_sb, xn_bf, router_w=r2_sb, lg_tm=lg2)
    topk_l2_launch1(ctx, lg2, ind1, ind2, p1, p2, cnt_sb)
    ctx.scr_release(mk)
    P.dma("PQ", cc_in.ap(), cnt_sb[:, :], reads=[rk(cnt_sb)],
          writes=[("ccin",)])
    P.op("POOL", lambda e: e.collective_compute(
        "AllGather", ALU.bypass, replica_groups=[list(range(8))],
        ins=[cc_in.ap().opt()], outs=[cc_out.ap().opt()]),
        reads=[("ccin",)], writes=[("ccout",)])

    l2_attn(ctx, hb, x_sb, xn_bf)
    P.dma("PQ", cnts[:, :], cc_out.ap(), reads=[("ccout",)],
          writes=[rk(cnts)])
    l2_ffn(ctx, hb, x_sb, xn_bf, ind1, ind2, p1, p2, cnts, cmask)

    ln_block(ctx, x_sb, xn_bf)
    l3_ffn(ctx, hb, x_sb, xn_bf)
    l3_attn(ctx, hb, x_sb, xn_bf)

    P.dma("SPQ", x_out.ap(), x_sb[:, :, :],
          reads=[rk(x_sb, c) for c in range(NCH)])
    with nc.Block() as block:
        P.emit(block)
    return ctx


def _fm(xb):
    return np.ascontiguousarray(xb.T.reshape(NCH, 128, T).transpose(1, 0, 2))


def _fm_inv(a):
    return a.transpose(1, 0, 2).reshape(D, T).T


def kernel(**inputs):
    arrs = host_prep(inputs)
    x = np.asarray(inputs["L0_attn_x"], dtype=np.float32)
    ctx = build_fused(arrs)
    in_maps = []
    for b in range(NCORES):
        m = dict(arrs)
        m["x_in"] = _fm(x[b])
        m["cm_in"] = np.ascontiguousarray(
            (np.arange(NCORES) < b).astype(np.float32)[:, None])
        in_maps.append(m)
    res = run_bass_kernel_spmd(ctx.nc, in_maps, core_ids=list(range(NCORES)))
    out = np.empty((NCORES, T, D), np.float32)
    for b in range(NCORES):
        out[b] = _fm_inv(res.results[b]["x_out"])
    return out


# version 09
# speedup vs baseline: 1.3779x; 1.3779x over previous
import bisect
import sys as _sys
import numpy as np
import ml_dtypes

import concourse.bass as bass
import concourse.mybir as mybir
from concourse.bass_utils import run_bass_kernel_spmd

F32 = mybir.dt.float32
BF16 = mybir.dt.bfloat16
AF = mybir.ActivationFunctionType
ALU = mybir.AluOpType
AX = mybir.AxisListType

NCORES = 8
T = 2048
D = 1024
NCH = D // 128
NTT = 4
TT = 512

_ENGS = ("PE", "ACT", "DVE", "POOL", "SP")
_AUDIT = False
_DBG_NO_BC = False
_DBG_NO_BIAS = False
_DBG_NO_W1 = False
_TRUST_SAME = {"PE": True, "ACT": False, "DVE": False, "POOL": False,
               "SP": True}

CONST_VALS = [1e-5, 1e-6, 1.1920929e-07, 0.5, 1.0, -1.0]



NDMACLS = 6


class Prog:
    def __init__(self, nc):
        self.nc = nc
        self.items = {e: [] for e in _ENGS}
        self.state = {}
        self.ndma = {"SPQ": 0, "PQ": 0, "AQ": 0}
        self.marked = {e: set() for e in _ENGS}
        self.water = {e: {} for e in _ENGS}
        self.fence = {}
        self.tags = {}

    def _record(self, eng, emitfn, reads, writes, ev):
        waits = []
        wm = self.water[eng]

        def need(dep):
            if dep is None:
                return
            if dep[0] == "E":
                deng, didx = dep[1], dep[2]
                if deng == eng and _TRUST_SAME[eng]:
                    return
                k = ("E", deng)
                if wm.get(k, -1) >= didx:
                    return
                wm[k] = didx
                waits.append(dep)
                self.marked[deng].add(didx)
            else:
                q, cls, n = dep[1], dep[2], dep[3]
                k = ("D", q, cls)
                if wm.get(k, -1) >= n:
                    return
                wm[k] = n
                waits.append(dep)

        for r in reads:
            st = self.state.get(r)
            if st is not None:
                need(st[0])
        for w in writes:
            st = self.state.get(w)
            if st is not None:
                need(st[0])
                for rd in st[1]:
                    need(rd)
            elif w[0] in self.fence:
                for dep in self.fence[w[0]]:
                    need(dep)
        for r in reads:
            self.state.setdefault(r, [None, []])[1].append(ev)
        for w in writes:
            self.state[w] = [ev, []]
        self.items[eng].append((emitfn, waits, ev))

    def op(self, eng, emitfn, reads=(), writes=()):
        ev = ("E", eng, len(self.items[eng]))
        self._record(eng, emitfn, reads, writes, ev)
        if _AUDIT:
            f = _sys._getframe(1)
            self.tags.setdefault(eng, {})[ev[2]] = (
                f"{f.f_code.co_name}:{f.f_lineno}", list(reads), list(writes))
        return ev

    def dma(self, q, dst_ap, src_ap, reads=(), writes=(), **kw):
        eng = {"SPQ": "SP", "PQ": "POOL", "AQ": "ACT"}[q]
        i = self.ndma[q]
        self.ndma[q] += 1
        cls = i % NDMACLS
        n = i // NDMACLS + 1
        ev = ("D", q, cls, n)
        if _AUDIT:
            f = _sys._getframe(1)
            self.tags.setdefault(eng, {})[len(self.items[eng])] = (
                f"dma {q} {f.f_code.co_name}:{f.f_lineno}", list(reads), list(writes))

        def emitfn(e, _d=dst_ap, _s=src_ap, _q=q, _cls=cls, _n=n, _kw=kw):
            if _n > 1:
                e.wait_ge(self.qsem[(_q, _cls)], 16 * (_n - 1))
            return e.dma_start(_d, _s, **_kw).then_inc(self.qsem[(_q, _cls)], 16)

        self._record(eng, emitfn, reads, writes, ev)
        return ev

    def inherit(self, newkey, oldkey):
        st = self.state.pop(oldkey, None)
        if st is not None:
            cur = self.state.setdefault(newkey, [None, []])
            if st[0] is not None:
                cur[1].append(st[0])
            cur[1].extend(st[1])

    def emit(self, block):
        self.qsem = {}
        self._semguards = []
        for q in ("SPQ", "PQ", "AQ"):
            for cls in range(min(NDMACLS, self.ndma[q])):
                g = self.nc.semaphore(f"q_{q}{cls}")
                self.qsem[(q, cls)] = g.__enter__()
                self._semguards.append(g)
        self.esem = {}
        for e in _ENGS:
            if self.marked[e]:
                g = self.nc.semaphore(f"e_{e}")
                self.esem[e] = g.__enter__()
                self._semguards.append(g)
        markpos = {e: sorted(self.marked[e]) for e in _ENGS}
        final = dict(self.ndma)

        cb = getattr(self, "const_barrier", 0)

        def emit_stream(engobj, eng):
            ms = self.marked[eng]
            if cb and eng != "SP":
                for cls in range(min(NDMACLS, cb)):
                    tot = cb // NDMACLS + (1 if cls < cb % NDMACLS else 0)
                    engobj.wait_ge(self.qsem[("SPQ", cls)], 16 * tot)
            for emitfn, waits, ev in self.items[eng]:
                for dep in waits:
                    if dep[0] == "E":
                        v = bisect.bisect_right(markpos[dep[1]], dep[2])
                        engobj.wait_ge(self.esem[dep[1]], v)
                    else:
                        engobj.wait_ge(self.qsem[(dep[1], dep[2])], 16 * dep[3])
                ins = emitfn(engobj)
                if ev[0] == "E" and ev[2] in ms:
                    ins.then_inc(self.esem[eng], 1)
            if eng == "SP":
                for q, n in final.items():
                    for cls in range(min(NDMACLS, n)):
                        tot = n // NDMACLS + (1 if cls < n % NDMACLS else 0)
                        engobj.wait_ge(self.qsem[(q, cls)], 16 * tot)

        secs = {"PE": block.tensor, "ACT": block.scalar, "DVE": block.vector,
                "POOL": block.gpsimd, "SP": block.sync}
        for e in _ENGS:
            if self.items[e] or e == "SP":
                secs[e](lambda engobj, _e=e: emit_stream(engobj, _e))


def rk(t, *sub):
    return (t.name,) + sub



class Ctx:
    def __init__(self):
        self.nc = bass.Bass("TRN2", target_bir_lowering=False, debug=False)
        self.P = Prog(self.nc)
        self.uid = 0
        self._psum_guards = {}

    def u(self, pfx):
        self.uid += 1
        return f"{pfx}_{self.uid}"

    def sb(self, name, shape, dtype):
        return self.nc.alloc_sbuf_tensor(name, shape, dtype)

    def psum(self, name, shape, dtype=F32):
        g = self.nc.psum_tensor(name, shape, dtype)
        h = g.__enter__()
        self._psum_guards[name] = (g, h)
        return h

    def psum_free(self, h):
        g, _ = self._psum_guards.pop(h.name)
        g.__exit__(None, None, None)

    def init_scratch(self):
        n = self.nc.sbuf_bytes_remaining - 512
        self.scr_bytes = n
        h = self.nc.alloc_sbuf_tensor("scratch_arena", [128, n], mybir.dt.uint8)
        self.scr_base = self.nc.lookup_mloc(h).addr
        self.scr_off = 0
        self.scr_live = []

    def scr(self, pfx, shape, dtype):
        import numpy as _np
        per = int(_np.prod(shape[1:])) * mybir.dt.size(dtype)
        off = (self.scr_off + 31) & ~31
        assert off + per <= self.scr_bytes, (
            f"scratch overflow: {pfx} needs {per}B at {off}, have {self.scr_bytes}")
        nm = self.u(pfx)
        h = self.nc.alloc_sbuf_tensor_at(nm, list(shape), dtype,
                                         offset=self.scr_base + off)
        self.scr_off = off + per
        self.scr_live.append((h.name, off))
        if getattr(self, "_scr_residue", None):
            self.P.fence[h.name] = list(self._scr_residue)
        return h

    def scr_mark(self):
        return (self.scr_off, len(self.scr_live))

    def scr_release(self, mark):
        off, nlive = mark
        released = self.scr_live[nlive:]
        self.scr_live = self.scr_live[:nlive]
        self.scr_off = off
        names = {nm for nm, _ in released}
        res = {}
        P = self.P
        for key in [k for k in P.state if k[0] in names]:
            st = P.state.pop(key)
            evs = ([st[0]] if st[0] is not None else []) + st[1]
            for ev in evs:
                kk = (ev[0], ev[1])
                if kk not in res or res[kk][2] < ev[2]:
                    res[kk] = ev
        for nm in names:
            P.fence.pop(nm, None)
        old = getattr(self, "_scr_residue", [])
        merged = {}
        for ev in list(res.values()) + list(old):
            kk = (ev[0], ev[1])
            if kk not in merged or merged[kk][2] < ev[2]:
                merged[kk] = ev
        self._scr_residue = list(merged.values())

    def setup(self):
        nc, P = self.nc, self.P
        c_of = nc.inline_tensor(np.ones((128, 128), np.float32), "c_ones_f")
        c_ob = nc.inline_tensor(np.ones((128, 128), ml_dtypes.bfloat16), "c_ones_b")
        c_if = nc.inline_tensor(np.eye(128, dtype=np.float32), "c_id_f")
        c_ib = nc.inline_tensor(np.eye(128, dtype=ml_dtypes.bfloat16), "c_id_b")
        c_orb = nc.inline_tensor(np.ones((1, 512), ml_dtypes.bfloat16), "c_orow_b")
        cvnp = np.tile(np.array(CONST_VALS, np.float32)[None, :], (128, 1))
        c_cv = nc.inline_tensor(np.ascontiguousarray(cvnp), "c_vals")
        self.ones_f = self.sb("ones_f", [128, 128], F32)
        self.ones_b = self.sb("ones_b", [128, 128], BF16)
        self.id_f = self.sb("id_f", [128, 128], F32)
        self.id_b = self.sb("id_b", [128, 128], BF16)
        self.ones_row_b = self.sb("ones_row_b", [1, 512], BF16)
        self.cvals = self.sb("cvals", [128, len(CONST_VALS)], F32)
        for dst, src in ((self.ones_f, c_of), (self.ones_b, c_ob),
                         (self.id_f, c_if), (self.id_b, c_ib),
                         (self.ones_row_b, c_orb), (self.cvals, c_cv)):
            P.dma("SPQ", dst[:, :], src[:, :], writes=[rk(dst)])
        for i, v in enumerate(CONST_VALS):
            nc.const_aps.register(F32, v, self.cvals[:, i:i + 1])
        self.P.const_barrier = self.P.ndma["SPQ"]
        self.bank = [self.psum(f"bank{i}", [128, 512], F32) for i in range(8)]
        self.rowA = self.sb("rowA", [1, T], F32)
        self.rowB = self.sb("rowB", [1, T], F32)
        self.sq = self.sb("sqtmp", [128, 512], F32)
        self.wstream = self.sb("wstream", [128, 3, 16, 128], BF16)
        self.wstream2 = self.sb("wstream2", [128, 3, 8, 128], BF16)
        self.ftmp = self.sb("ftmp", [128, 512], F32)
        self.bc_sb = self.sb("bc_sb", [128, 512], F32)


    def bcast_row(self, row, coff, bank, n=TT, base=0, dtype_b=False):
        P = self.P
        ones = self.ones_b if dtype_b else self.ones_f
        P.op("PE", lambda e: e.matmul(
            bank[:, 0:n], ones[base:base + 1, :], row[base:base + 1, coff:coff + n],
            start=True, stop=True),
            reads=[rk(row), rk(ones)], writes=[rk(bank)])

    def tm_cols_to_row(self, tm, col, row, coff, bank, n=TT):
        P = self.P
        for w in range((n + 511) // 512):
            nj = min(4, (n - w * 512) // 128)
            for j in range(nj):
                P.op("PE", lambda e, j=j, w=w: e.matmul(
                    bank[0:1, j * 128:(j + 1) * 128],
                    tm[:, col + w * 4 + j:col + w * 4 + j + 1],
                    self.id_f[:, :], start=True, stop=True,
                    skip_group_check=True),
                    reads=[rk(tm), rk(self.id_f)], writes=[rk(bank)])
            P.op("ACT", lambda e, w=w, nj=nj: e.copy(
                row[0:1, coff + w * 512:coff + w * 512 + nj * 128],
                bank[0:1, 0:nj * 128]),
                reads=[rk(bank)], writes=[rk(row)])

    def row_to_tm_cols(self, row, coff, tm, col, bank, n=TT, scale=None):
        P = self.P
        nj = n // 128
        for j in range(nj):
            P.op("PE", lambda e, j=j: e.matmul(
                bank[:, j:j + 1], row[0:1, coff + j * 128:coff + (j + 1) * 128],
                self.ones_f[0:1, 0:1], start=True, stop=True,
                skip_group_check=True),
                reads=[rk(row), rk(self.ones_f)], writes=[rk(bank)])
        if scale is None:
            P.op("ACT", lambda e: e.copy(tm[:, col:col + nj], bank[:, 0:nj]),
                 reads=[rk(bank)], writes=[rk(tm)])
        else:
            P.op("ACT", lambda e: e.activation(tm[:, col:col + nj], bank[:, 0:nj],
                                               AF.Copy, scale=scale),
                 reads=[rk(bank)], writes=[rk(tm)])



def ln_block(ctx, x_sb, xn_bf, router_w=None, router_E=8, lg_tm=None):
    P = ctx.P
    bank = ctx.bank
    stA, stB, pcol, pbc0, pbc1 = (bank[0], bank[1], bank[2], bank[3], bank[4])
    prow = pcol
    jbanks = [bank[0], bank[1], bank[5], bank[6]]
    _mk = ctx.scr_mark()
    m_tm = ctx.scr("m_tm", [128, 16], F32)
    r_tm = ctx.scr("r_tm", [128, 16], F32)

    for tt in range(NTT):
        tsl = slice(tt * TT, (tt + 1) * TT)
        for c in range(NCH):
            P.op("ACT", lambda e, c=c, tsl=tsl: e.activation(
                ctx.sq[:, :], x_sb[:, c, tsl], AF.Square),
                reads=[rk(x_sb, c)], writes=[rk(ctx.sq)])
            P.op("PE", lambda e, c=c, tsl=tsl: e.matmul(
                stA[0:1, :], ctx.ones_f[:, 0:1], x_sb[:, c, tsl],
                start=(c == 0), stop=(c == NCH - 1), skip_group_check=True),
                reads=[rk(x_sb, c), rk(ctx.ones_f)], writes=[rk(stA)])
            P.op("PE", lambda e, c=c: e.matmul(
                stB[0:1, :], ctx.ones_f[:, 0:1], ctx.sq[:, :],
                start=(c == 0), stop=(c == NCH - 1), skip_group_check=True),
                reads=[rk(ctx.sq), rk(ctx.ones_f)], writes=[rk(stB)])
        P.op("ACT", lambda e, tsl=tsl: e.copy(ctx.rowA[0:1, tsl], stA[0:1, :]),
             reads=[rk(stA)], writes=[rk(ctx.rowA)])
        P.op("ACT", lambda e, tsl=tsl: e.copy(ctx.rowB[0:1, tsl], stB[0:1, :]),
             reads=[rk(stB)], writes=[rk(ctx.rowB)])
    ctx.row_to_tm_cols(ctx.rowA, 0, m_tm, 0, pcol, n=T, scale=1.0 / D)
    ctx.row_to_tm_cols(ctx.rowB, 0, r_tm, 0, pcol, n=T, scale=1.0 / D)
    P.op("DVE", lambda e: e.scalar_tensor_tensor(
        ctx.sq[:, 0:16], m_tm[:, :], 1.0, m_tm[:, :], ALU.mult, ALU.mult),
        reads=[rk(m_tm)], writes=[rk(ctx.sq)])
    P.op("DVE", lambda e: e.tensor_tensor(r_tm[:, :], r_tm[:, :],
                                          ctx.sq[:, 0:16], ALU.subtract),
         reads=[rk(r_tm), rk(ctx.sq)], writes=[rk(r_tm)])
    P.op("ACT", lambda e: e.activation(r_tm[:, :], r_tm[:, :], AF.Ln, bias=1e-5),
         reads=[rk(r_tm)], writes=[rk(r_tm)])
    P.op("ACT", lambda e: e.activation(r_tm[:, :], r_tm[:, :], AF.Exp, scale=-0.5),
         reads=[rk(r_tm)], writes=[rk(r_tm)])
    P.op("DVE", lambda e: e.tensor_tensor(m_tm[:, :], m_tm[:, :], r_tm[:, :],
                                          ALU.mult),
         reads=[rk(m_tm), rk(r_tm)], writes=[rk(m_tm)])
    ctx.tm_cols_to_row(r_tm, 0, ctx.rowA, 0, prow, n=T)
    ctx.tm_cols_to_row(m_tm, 0, ctx.rowB, 0, prow, n=T)
    for tt in range(NTT):
        tsl = slice(tt * TT, (tt + 1) * TT)
        ctx.bcast_row(ctx.rowA, tt * TT, pbc0)
        ctx.bcast_row(ctx.rowB, tt * TT, pbc1)
        for c in range(NCH):
            P.op("DVE", lambda e, c=c, tsl=tsl: e.tensor_tensor(
                ctx.ftmp[:, :], x_sb[:, c, tsl], pbc0[:, :], ALU.mult),
                reads=[rk(x_sb, c), rk(pbc0)], writes=[rk(ctx.ftmp)])
            if router_w is None:
                P.op("DVE", lambda e, c=c, tsl=tsl: e.tensor_tensor(
                    xn_bf[:, c, tsl], ctx.ftmp[:, :], pbc1[:, :], ALU.subtract),
                    reads=[rk(ctx.ftmp), rk(pbc1)], writes=[rk(xn_bf, c)])
            else:
                P.op("DVE", lambda e, c=c, tsl=tsl: e.tensor_tensor(
                    ctx.ftmp[:, :], ctx.ftmp[:, :], pbc1[:, :], ALU.subtract),
                    reads=[rk(ctx.ftmp), rk(pbc1)], writes=[rk(ctx.ftmp)])
                P.op("DVE", lambda e, c=c, tsl=tsl: e.tensor_copy(
                    xn_bf[:, c, tsl], ctx.ftmp[:, :]),
                    reads=[rk(ctx.ftmp)], writes=[rk(xn_bf, c)])
                for j in range(4):
                    jj = tt * 4 + j
                    P.op("PE", lambda e, c=c, j=j, jj=jj: e.matmul(
                        jbanks[j][:, 0:router_E],
                        ctx.ftmp[:, j * 128:(j + 1) * 128], router_w[:, c, :],
                        start=(c == 0), stop=(c == NCH - 1),
                        skip_group_check=True),
                        reads=[rk(ctx.ftmp), rk(router_w)], writes=[rk(jbanks[j])])
        if router_w is not None:
            for j in range(4):
                jj = tt * 4 + j
                P.op("ACT", lambda e, j=j, jj=jj: e.copy(
                    lg_tm[:, jj, :], jbanks[j][:, 0:router_E]),
                    reads=[rk(jbanks[j])], writes=[rk(lg_tm, jj)])
    ctx.scr_release(_mk)
    return lg_tm


def s2d(ap3, j, e):
    return ap3[:, j, e:e + 1]



def proj_stream(ctx, wsrc_fn, nm, nk, rhs_fn, evac_fn, tts=(0, 1, 2, 3),
                wkey="w", extra_mm=None, banks=(6, 7)):
    P = ctx.P
    wb = ctx.wstream
    for mb in range(nm):
        b = mb % 3
        src = wsrc_fn(mb)
        q = "PQ"
        if nk > 8:
            half = nk // 2
            P.dma(q, wb[:, b, 0:half, :], src[:, 0:half, :], writes=[rk(wb, b)])
            P.dma(q, wb[:, b, half:nk, :], src[:, half:nk, :], writes=[rk(wb, b)])
        else:
            P.dma(q, wb[:, b, 0:nk, :], src, writes=[rk(wb, b)])
        for ti, tt in enumerate(tts):
            bank = ctx.bank[banks[(mb * len(tts) + ti) % len(banks)]]
            for c in range(nk):
                rhs_ap, rreads = rhs_fn(c, tt)
                P.op("PE", lambda e, mb=mb, b=b, c=c, rhs_ap=rhs_ap, bank=bank: e.matmul(
                    bank[:, :], wb[:, b, c, :], rhs_ap,
                    start=(c == 0), stop=(c == nk - 1 and extra_mm is None),
                    skip_group_check=True),
                    reads=[rk(wb, b)] + rreads, writes=[rk(bank)])
            if extra_mm is not None:
                extra_mm(mb, tt, bank)
            evac_fn(mb, tt, bank)



def l0_attn(ctx, hb, x_sb, xn_bf):
    nc, P = ctx.nc, ctx.P
    bank = ctx.bank

    _mkF = ctx.scr_mark()
    t_f32 = ctx.scr("l0_t", [128, 4, T], F32)
    _mkM = ctx.scr_mark()
    mp_bf = ctx.scr("l0_mp", [128, NCH, 512], BF16)

    _mkA = ctx.scr_mark()
    wv_sb = ctx.scr("l0_wv", [1, 2048], F32)
    P.dma("SPQ", wv_sb[:, :], hb["L0_attn_W_V"].ap(), writes=[rk(wv_sb)])
    uch = ctx.scr("l0_uch", [128, 2048], F32)
    tpch = ctx.scr("l0_tpch", [128, 512], F32)
    wu_r = hb["L0_attn_W_U"].ap().rearrange("(c p) f -> c p f", p=128)
    wtp_r = hb["L0_attn_W_toP"].ap().rearrange("(c p) m -> c p m", p=128)
    t_a = ctx.scr("l0_ta", [128, 128], F32)
    t_b = ctx.scr("l0_tb", [128, 128], F32)
    for c in range(NCH):
        P.dma("SPQ", uch[:, :], wu_r[c], writes=[rk(uch)])
        P.dma("SPQ", tpch[:, :], wtp_r[c], writes=[rk(tpch)])
        for g in range(4):
            gsl = slice(g * 512, (g + 1) * 512)
            ctx.bcast_row(wv_sb, g * 512, bank[3])
            P.op("DVE", lambda e, gsl=gsl: e.tensor_tensor(
                ctx.ftmp[:, :], uch[:, gsl], bank[3][:, :], ALU.mult),
                reads=[rk(uch), rk(bank[3])], writes=[rk(ctx.ftmp)])
            P.op("DVE", lambda e: e.tensor_tensor(
                t_a[:, :], ctx.ftmp[:, 0:512:4], ctx.ftmp[:, 1:512:4], ALU.add),
                reads=[rk(ctx.ftmp)], writes=[rk(t_a)])
            P.op("DVE", lambda e: e.tensor_tensor(
                t_b[:, :], ctx.ftmp[:, 2:512:4], ctx.ftmp[:, 3:512:4], ALU.add),
                reads=[rk(ctx.ftmp)], writes=[rk(t_b)])
            P.op("DVE", lambda e: e.tensor_tensor(
                t_a[:, :], t_a[:, :], t_b[:, :], ALU.add),
                reads=[rk(t_a), rk(t_b)], writes=[rk(t_a)])
            P.op("DVE", lambda e, c=c, g=g: e.scalar_tensor_tensor(
                mp_bf[:, c, g * 128:(g + 1) * 128], t_a[:, :], 1.0,
                tpch[:, g * 128:(g + 1) * 128], ALU.mult, ALU.add),
                reads=[rk(t_a), rk(tpch)], writes=[rk(mp_bf, c)])

    ctx.scr_release(_mkA)
    if "mp" in ctx.dbg:
        P.dma("SPQ", ctx.dbg["mp"].ap(), mp_bf[:, :, :],
              reads=[rk(mp_bf, c) for c in range(NCH)])

    for mo in range(4):
        for tt in range(NTT):
            tsl = slice(tt * TT, (tt + 1) * TT)
            bk = bank[6 + (mo * NTT + tt) % 2]
            for c in range(NCH):
                P.op("PE", lambda e, mo=mo, c=c, tsl=tsl, bk=bk: e.matmul(
                    bk[:, :], mp_bf[:, c, mo * 128:(mo + 1) * 128], xn_bf[:, c, tsl],
                    start=(c == 0), stop=(c == NCH - 1), skip_group_check=True),
                    reads=[rk(mp_bf, c), rk(xn_bf, c)], writes=[rk(bk)])
            P.op("ACT", lambda e, mo=mo, tsl=tsl, bk=bk: e.copy(
                t_f32[:, mo, tsl], bk[:, :]),
                reads=[rk(bk)], writes=[rk(t_f32, mo)])

    ctx.scr_release(_mkM)
    s_sb = ctx.scr("l0_s", [64, T], F32)
    bmat_sb = ctx.scr("l0_bmat", [128, 4, 64], F32)
    wlam_sb0 = ctx.scr("l0_wlam", [128, 4, 64], F32)
    P.dma("SPQ", bmat_sb[:, :, :],
          hb["L0_attn_B_mat"].ap().rearrange("(c p) r -> p c r", p=128),
          writes=[rk(bmat_sb)])
    _mkC = ctx.scr_mark()
    u_sb = ctx.scr("l0_u", [64, T], F32)
    lam_sb = ctx.scr("l0_lam", [64, T], F32)
    for tt in range(NTT):
        tsl = slice(tt * TT, (tt + 1) * TT)
        for mo in range(4):
            P.op("PE", lambda e, mo=mo, tsl=tsl, tt=tt: e.matmul(
                bank[tt][0:64, :], bmat_sb[:, mo, :], t_f32[:, mo, tsl],
                start=(mo == 0), stop=(mo == 3), skip_group_check=True),
                reads=[rk(bmat_sb), rk(t_f32, mo)], writes=[rk(bank[tt])])
        P.op("ACT", lambda e, tsl=tsl, tt=tt: e.copy(u_sb[:, tsl], bank[tt][0:64, :]),
             reads=[rk(bank[tt])], writes=[rk(u_sb)])

    wlam_sb = wlam_sb0
    P.dma("SPQ", wlam_sb[:, :, :],
          hb["L0_attn_W_lam"].ap().rearrange("(c p) r -> p c r", p=128),
          writes=[rk(wlam_sb)])
    fg_sb = ctx.scr("l0_fg", [128, T], F32)
    wf_r = hb["L0_attn_W_F"].ap()

    def wf_src(mb):
        return wf_r[mb]

    def fg_rhs(c, tt):
        return xn_bf[:, c, tt * TT:(tt + 1) * TT], [rk(xn_bf, c)]

    for mo in range(4):
        def fg_evac(mb, tt, bk, mo=mo):
            tsl = slice(tt * TT, (tt + 1) * TT)
            P.op("ACT", lambda e, tsl=tsl, bk=bk: e.activation(
                fg_sb[:, tsl], bk[:, :], AF.Sigmoid),
                reads=[rk(bk)], writes=[rk(fg_sb)])
        proj_stream(ctx, lambda _mb, mo=mo: wf_src(mo), 1, NCH, fg_rhs, fg_evac,
                    wkey=f"wf{mo}")
        for tt in range(NTT):
            tsl = slice(tt * TT, (tt + 1) * TT)
            P.op("PE", lambda e, mo=mo, tsl=tsl, tt=tt: e.matmul(
                bank[tt][0:64, :], wlam_sb[:, mo, :], fg_sb[:, tsl],
                start=(mo == 0), stop=(mo == 3), skip_group_check=True),
                reads=[rk(wlam_sb), rk(fg_sb)], writes=[rk(bank[tt])])
            if mo == 3:
                P.op("ACT", lambda e, tsl=tsl, tt=tt: e.activation(
                    lam_sb[:, tsl], bank[tt][0:64, :], AF.Sigmoid),
                    reads=[rk(bank[tt])], writes=[rk(lam_sb)])

    if "t" in ctx.dbg:
        P.dma("SPQ", ctx.dbg["t"].ap(), t_f32[:, :, :],
              reads=[rk(t_f32, mo) for mo in range(4)])
    if "lam" in ctx.dbg:
        P.dma("SPQ", ctx.dbg["lam"].ap(), lam_sb[:, :], reads=[rk(lam_sb)])
    if "u" in ctx.dbg:
        P.dma("SPQ", ctx.dbg["u"].ap(), u_sb[:, :], reads=[rk(u_sb)])
    P.op("DVE", lambda e: e.tensor_tensor_scan(
        s_sb[:, :], lam_sb[:, :], u_sb[:, :], 0.0, ALU.mult, ALU.add),
        reads=[rk(lam_sb), rk(u_sb)], writes=[rk(s_sb)])
    ctx.scr_release(_mkC)

    if "s" in ctx.dbg:
        P.dma("SPQ", ctx.dbg["s"].ap(), s_sb[:, :], reads=[rk(s_sb)])
    cmat_sb = ctx.scr("l0_cmat", [64, 512], F32)
    P.dma("SPQ", cmat_sb[:, :], hb["L0_attn_C_mat"].ap(), writes=[rk(cmat_sb)])
    tt_bf = ctx.scr("l0_ttbf", [128, 4, T], BF16)
    for mo in range(4):
        for tt in range(NTT):
            tsl = slice(tt * TT, (tt + 1) * TT)
            bk = bank[6 + (mo * NTT + tt) % 2]
            P.op("PE", lambda e, mo=mo, tsl=tsl, bk=bk: e.matmul(
                bk[:, :], cmat_sb[:, mo * 128:(mo + 1) * 128], s_sb[:, tsl],
                start=True, stop=True),
                reads=[rk(cmat_sb), rk(s_sb)], writes=[rk(bk)])
            P.op("DVE", lambda e, mo=mo, tsl=tsl, bk=bk: e.scalar_tensor_tensor(
                tt_bf[:, mo, tsl], bk[:, :], 1.0, t_f32[:, mo, tsl],
                ALU.mult, ALU.add),
                reads=[rk(bk), rk(t_f32, mo)], writes=[rk(tt_bf, mo)])

    wfp_r = hb["L0_attn_W_fromP"].ap()

    def wfp_src(mb):
        return wfp_r[mb]

    def h_rhs(c, tt):
        return tt_bf[:, c, tt * TT:(tt + 1) * TT], [rk(tt_bf, c)]

    def h_evac(mb, tt, bk):
        tsl = slice(tt * TT, (tt + 1) * TT)
        P.op("DVE", lambda e, mb=mb, tsl=tsl, bk=bk: e.scalar_tensor_tensor(
            x_sb[:, mb, tsl], bk[:, :], 1.0, x_sb[:, mb, tsl], ALU.mult, ALU.add),
            reads=[rk(bk), rk(x_sb, mb)], writes=[rk(x_sb, mb)])

    if "ttbf" in ctx.dbg:
        P.dma("SPQ", ctx.dbg["ttbf"].ap(), tt_bf[:, :, :],
              reads=[rk(tt_bf, mo) for mo in range(4)])
    proj_stream(ctx, wfp_src, NCH, 4, h_rhs, h_evac, wkey="wfp")
    ctx.scr_release(_mkF)



def topk_combine_l0(ctx, lg_tm, c_tm):
    P = ctx.P
    _mk = ctx.scr_mark()
    m1 = ctx.scr("m1", [128, 16], F32)
    m2 = ctx.scr("m2", [128, 16], F32)
    ind1 = ctx.scr("ind1", [128, 16, 8], F32)
    ind2 = ctx.scr("ind2", [128, 16, 8], F32)
    l2t = ctx.scr("l2t", [128, 16, 8], F32)
    P.op("DVE", lambda e: e.reduce_max(m1[:, :], lg_tm[:, :, :], axis=AX.X),
         reads=[rk(lg_tm, j) for j in range(16)], writes=[rk(m1)])
    for j in range(16):
        P.op("DVE", lambda e, j=j: e.tensor_single_scalar(
            ind1[:, j, :], lg_tm[:, j, :], m1[:, j:j + 1], ALU.is_ge),
            reads=[rk(lg_tm, j), rk(m1)], writes=[rk(ind1, j)])
        P.op("DVE", lambda e, j=j: e.scalar_tensor_tensor(
            l2t[:, j, :], ind1[:, j, :], -1e30, lg_tm[:, j, :], ALU.mult, ALU.add),
            reads=[rk(ind1, j), rk(lg_tm, j)], writes=[rk(l2t, j)])
    P.op("DVE", lambda e: e.reduce_max(m2[:, :], l2t[:, :, :], axis=AX.X),
         reads=[rk(l2t, j) for j in range(16)],
         writes=[rk(m2)])
    for j in range(16):
        P.op("DVE", lambda e, j=j: e.tensor_single_scalar(
            ind2[:, j, :], l2t[:, j, :], m2[:, j:j + 1], ALU.is_ge),
            reads=[rk(l2t, j), rk(m2)], writes=[rk(ind2, j)])
    w1 = ctx.scr("w1", [128, 16], F32)
    w2 = ctx.scr("w2", [128, 16], F32)
    P.op("DVE", lambda e: e.tensor_tensor(w1[:, :], m1[:, :], m2[:, :], ALU.subtract),
         reads=[rk(m1), rk(m2)], writes=[rk(w1)])
    P.op("ACT", lambda e: e.activation(w1[:, :], w1[:, :], AF.Sigmoid),
         reads=[rk(w1)], writes=[rk(w1)])
    P.op("DVE", lambda e: e.tensor_scalar(w2[:, :], w1[:, :], -1.0, 1.0,
                                          ALU.mult, ALU.add),
         reads=[rk(w1)], writes=[rk(w2)])
    for j in range(16):
        P.op("DVE", lambda e, j=j: e.tensor_single_scalar(
            l2t[:, j, :], ind1[:, j, :], w1[:, j:j + 1], ALU.mult),
            reads=[rk(ind1, j), rk(w1)], writes=[rk(l2t, j)])
        P.op("DVE", lambda e, j=j: e.scalar_tensor_tensor(
            c_tm[:, j, :], ind2[:, j, :], w2[:, j:j + 1], l2t[:, j, :],
            ALU.mult, ALU.add),
            reads=[rk(ind2, j), rk(w2), rk(l2t, j)], writes=[rk(c_tm, j)])
    ctx.scr_release(_mk)
    return c_tm


def moe_ffn(ctx, hb, x_sb, xn_bf, c_tm, pfx, w1name, b1name, w2name, b2name,
            act=AF.Gelu, shared=None, out_ln=None, nexp=None):
    nc, P = ctx.nc, ctx.P
    bank = ctx.bank
    E, nkh = hb[w1name].shape[0], hb[w1name].shape[1]
    H = nkh * 128
    _mk = ctx.scr_mark()
    h_sb = ctx.scr(pfx + "_h", [128, nkh, TT], BF16)
    acc = ctx.scr(pfx + "_acc", [128, NCH, TT], F32)
    w1_t = hb[w1name].ap()
    w2_r = hb[w2name].ap().rearrange("e (k p) (oc m) -> e p k oc m", p=128, m=128)
    b1col = ctx.scr(pfx + "_b1c", [128, E, nkh], F32)
    P.dma("SPQ", b1col[:, :, :], hb[b1name].ap(), writes=[rk(b1col)])
    b2row = ctx.scr(pfx + "_b2r", [1, 2, D], BF16)
    b2_ap = hb[b2name].ap()
    if shared is not None:
        shw_t = hb[shared["w"]].ap()
        shb = ctx.scr(pfx + "_shb", [128, NCH], F32)
        P.dma("SPQ", shb[:, :], hb[shared["b"]].ap(), writes=[rk(shb)])
    if out_ln is not None:
        gcol = ctx.scr(pfx + "_g", [128, NCH], F32)
        bcol = ctx.scr(pfx + "_bc", [128, NCH], F32)
        P.dma("SPQ", gcol[:, :], hb[out_ln["g"]].ap(), writes=[rk(gcol)])
        P.dma("SPQ", bcol[:, :], hb[out_ln["b"]].ap(), writes=[rk(bcol)])
    w2buf = ctx.wstream2
    ftmp2 = ctx.scr(pfx + "_f2", [128, 512], F32)

    for tt in range(NTT):
        tsl = slice(tt * TT, (tt + 1) * TT)

        def xn_rhs(c, _tt, tsl=tsl):
            return xn_bf[:, c, tsl], [rk(xn_bf, c)]

        if shared is not None:
            def sh_evac(mb, _tt, bk, tsl=tsl):
                P.op("ACT", lambda e, mb=mb, bk=bk: e.activation(
                    ctx.ftmp[:, :], bk[:, :], AF.Gelu, bias=shb[:, mb:mb + 1]),
                    reads=[rk(bk), rk(shb)], writes=[rk(ctx.ftmp)])
                P.op("DVE", lambda e, mb=mb: e.tensor_single_scalar(
                    acc[:, mb, :], ctx.ftmp[:, :], 0.5, ALU.mult),
                    reads=[rk(ctx.ftmp)], writes=[rk(acc, mb)])
            proj_stream(ctx, lambda mb: shw_t[mb],
                        NCH, NCH, xn_rhs, sh_evac, tts=(tt,), wkey=f"sh{tt}")
        else:
            for mb in range(NCH):
                P.op("DVE", lambda e, mb=mb: e.memset(acc[:, mb, :], 0.0),
                     writes=[rk(acc, mb)])

        for ex in range(nexp if nexp is not None else E):
            P.dma("PQ", b2row[:, ex % 2, :], b2_ap[0:1, ex, :],
                  writes=[rk(b2row, ex % 2)])
            for j in range(4):
                jj = tt * 4 + j
                P.op("PE", lambda e, j=j, jj=jj, ex=ex: e.matmul(
                    bank[2][0:1, j * 128:(j + 1) * 128], s2d(c_tm, jj, ex),
                    ctx.id_f[:, :], start=True, stop=True,
                    skip_group_check=True),
                    reads=[rk(c_tm, jj)], writes=[rk(bank[2])])
            P.op("ACT", lambda e: e.copy(ctx.bc_sb[0:1, :], bank[2][0:1, :]),
                 reads=[rk(bank[2])], writes=[rk(ctx.bc_sb)])
            P.op("PE", lambda e: e.matmul(
                bank[3][:, :], ctx.ones_f[0:1, :], ctx.bc_sb[0:1, :],
                start=True, stop=True),
                reads=[rk(ctx.bc_sb), rk(ctx.ones_f)], writes=[rk(bank[3])])
            P.op("ACT", lambda e: e.copy(ctx.bc_sb[:, :], bank[3][:, :]),
                 reads=[rk(bank[3])], writes=[rk(ctx.bc_sb)])

            if _DBG_NO_W1:
                for mb in range(nkh):
                    P.op("DVE", lambda e, mb=mb: e.memset(h_sb[:, mb, :], 0.01),
                         writes=[rk(h_sb, mb)])
            else:
                def h_evac(mb, _tt, bk, ex=ex):
                    P.op("ACT", lambda e, mb=mb, bk=bk, ex=ex: e.activation(
                        h_sb[:, mb, :], bk[:, :], act, bias=s2d(b1col, ex, mb)),
                        reads=[rk(bk), rk(b1col)], writes=[rk(h_sb, mb)])
                proj_stream(ctx, lambda mb, ex=ex: w1_t[ex, mb],
                            nkh, NCH, xn_rhs, h_evac, tts=(tt,), wkey=f"w1_{tt}_{ex}")

            for k in range(nkh):
                b = k % 3
                P.dma("PQ", w2buf[:, b, :, :], w2_r[ex][:, k, :, :],
                      writes=[rk(w2buf, b)])
                for o in range(NCH):
                    P.op("PE", lambda e, k=k, b=b, o=o: e.matmul(
                        bank[o][:, :], w2buf[:, b, o, :], h_sb[:, k, :],
                        start=(k == 0), stop=False, skip_group_check=True),
                        reads=[rk(w2buf, b), rk(h_sb, k)], writes=[rk(bank[o])])
            for o in range(NCH):
                if not _DBG_NO_BIAS:
                    P.op("PE", lambda e, o=o, ex=ex: e.matmul(
                        bank[o][:, :], b2row[0:1, ex % 2, o * 128:(o + 1) * 128],
                        ctx.ones_row_b[0:1, :], start=False, stop=True,
                        skip_group_check=True),
                        reads=[rk(b2row, ex % 2), rk(ctx.ones_row_b)],
                        writes=[rk(bank[o])])
                if _DBG_NO_BC:
                    P.op("DVE", lambda e, o=o: e.tensor_tensor(
                        acc[:, o, :], acc[:, o, :], bank[o][:, :], ALU.add),
                        reads=[rk(acc, o), rk(bank[o])], writes=[rk(acc, o)])
                else:
                    P.op("DVE", lambda e, o=o: e.tensor_tensor(
                        ftmp2[:, :], bank[o][:, :], ctx.bc_sb[:, :], ALU.mult),
                        reads=[rk(bank[o]), rk(ctx.bc_sb)], writes=[rk(ftmp2)])
                    P.op("DVE", lambda e, o=o: e.tensor_tensor(
                        acc[:, o, :], acc[:, o, :], ftmp2[:, :], ALU.add),
                        reads=[rk(acc, o), rk(ftmp2)], writes=[rk(acc, o)])

        if out_ln is not None:
            ln_acc_add(ctx, acc, x_sb, tt, gcol, bcol)
        else:
            for o in range(NCH):
                P.op("DVE", lambda e, o=o, tsl=tsl: e.tensor_tensor(
                    x_sb[:, o, tsl], x_sb[:, o, tsl], acc[:, o, :], ALU.add),
                    reads=[rk(x_sb, o), rk(acc, o)], writes=[rk(x_sb, o)])
    ctx.scr_release(_mk)


def ln_acc_add(ctx, acc, x_sb, tt, gcol, bcol):
    P = ctx.P
    bank = ctx.bank
    stA, stB, pcol, pbc0, pbc1 = bank[0], bank[1], bank[2], bank[3], bank[4]
    prow = pcol
    tsl = slice(tt * TT, (tt + 1) * TT)
    for c in range(NCH):
        P.op("ACT", lambda e, c=c: e.activation(ctx.sq[:, :], acc[:, c, :],
                                                AF.Square),
             reads=[rk(acc, c)], writes=[rk(ctx.sq)])
        P.op("PE", lambda e, c=c: e.matmul(
            stA[0:1, :], ctx.ones_f[:, 0:1], acc[:, c, :],
            start=(c == 0), stop=(c == NCH - 1), skip_group_check=True),
            reads=[rk(acc, c), rk(ctx.ones_f)], writes=[rk(stA)])
        P.op("PE", lambda e, c=c: e.matmul(
            stB[0:1, :], ctx.ones_f[:, 0:1], ctx.sq[:, :],
            start=(c == 0), stop=(c == NCH - 1), skip_group_check=True),
            reads=[rk(ctx.sq), rk(ctx.ones_f)], writes=[rk(stB)])
    P.op("ACT", lambda e: e.copy(ctx.rowA[0:1, 0:TT], stA[0:1, :]),
         reads=[rk(stA)], writes=[rk(ctx.rowA)])
    P.op("ACT", lambda e: e.copy(ctx.rowB[0:1, 0:TT], stB[0:1, :]),
         reads=[rk(stB)], writes=[rk(ctx.rowB)])
    _mk2 = ctx.scr_mark()
    m_tm = ctx.scr("lnm", [128, 4], F32)
    r_tm = ctx.scr("lnr", [128, 4], F32)
    ctx.row_to_tm_cols(ctx.rowA, 0, m_tm, 0, pcol, n=TT, scale=1.0 / D)
    ctx.row_to_tm_cols(ctx.rowB, 0, r_tm, 0, pcol, n=TT, scale=1.0 / D)
    P.op("DVE", lambda e: e.scalar_tensor_tensor(
        ctx.sq[:, 0:4], m_tm[:, :], 1.0, m_tm[:, :], ALU.mult, ALU.mult),
        reads=[rk(m_tm)], writes=[rk(ctx.sq)])
    P.op("DVE", lambda e: e.tensor_tensor(r_tm[:, :], r_tm[:, :], ctx.sq[:, 0:4],
                                          ALU.subtract),
         reads=[rk(r_tm), rk(ctx.sq)], writes=[rk(r_tm)])
    P.op("ACT", lambda e: e.activation(r_tm[:, :], r_tm[:, :], AF.Ln, bias=1e-5),
         reads=[rk(r_tm)], writes=[rk(r_tm)])
    P.op("ACT", lambda e: e.activation(r_tm[:, :], r_tm[:, :], AF.Exp, scale=-0.5),
         reads=[rk(r_tm)], writes=[rk(r_tm)])
    P.op("DVE", lambda e: e.tensor_tensor(m_tm[:, :], m_tm[:, :], r_tm[:, :],
                                          ALU.mult),
         reads=[rk(m_tm), rk(r_tm)], writes=[rk(m_tm)])
    ctx.tm_cols_to_row(r_tm, 0, ctx.rowA, 0, prow, n=TT)
    ctx.tm_cols_to_row(m_tm, 0, ctx.rowB, 0, prow, n=TT)
    ctx.bcast_row(ctx.rowA, 0, pbc0)
    ctx.bcast_row(ctx.rowB, 0, pbc1)
    for c in range(NCH):
        P.op("DVE", lambda e, c=c: e.tensor_tensor(
            ctx.ftmp[:, :], acc[:, c, :], pbc0[:, :], ALU.mult),
            reads=[rk(acc, c), rk(pbc0)], writes=[rk(ctx.ftmp)])
        P.op("DVE", lambda e, c=c: e.tensor_tensor(
            ctx.ftmp[:, :], ctx.ftmp[:, :], pbc1[:, :], ALU.subtract),
            reads=[rk(ctx.ftmp), rk(pbc1)], writes=[rk(ctx.ftmp)])
        P.op("DVE", lambda e, c=c: e.tensor_scalar(
            ctx.ftmp[:, :], ctx.ftmp[:, :], gcol[:, c:c + 1], bcol[:, c:c + 1],
            ALU.mult, ALU.add),
            reads=[rk(ctx.ftmp), rk(gcol), rk(bcol)], writes=[rk(ctx.ftmp)])
        P.op("DVE", lambda e, c=c, tsl=tsl: e.tensor_tensor(
            x_sb[:, c, tsl], x_sb[:, c, tsl], ctx.ftmp[:, :], ALU.add),
            reads=[rk(x_sb, c), rk(ctx.ftmp)], writes=[rk(x_sb, c)])
    ctx.scr_release(_mk2)



def _tile_lhsT(w, m=128):
    Dd, M = w.shape
    c = Dd // 128
    return np.ascontiguousarray(
        w.reshape(c, 128, M // m, m).transpose(2, 1, 0, 3))


def _tile_lhsT_e(w, m=128):
    E, Dd, M = w.shape
    c = Dd // 128
    return np.ascontiguousarray(
        w.reshape(E, c, 128, M // m, m).transpose(0, 3, 2, 1, 4))


def host_prep(inputs):
    def g(n):
        return np.ascontiguousarray(np.asarray(inputs[n], dtype=np.float32))

    a = {}
    a["L0_attn_W_toP"] = g("L0_attn_W_toP")
    a["L0_attn_W_U"] = np.ascontiguousarray(g("L0_attn_W_U").reshape(D, 2048))
    a["L0_attn_W_F"] = _tile_lhsT(g("L0_attn_W_F"))
    a["L0_attn_W_V"] = np.ascontiguousarray(g("L0_attn_W_V").reshape(1, 2048))
    a["L0_attn_W_lam"] = g("L0_attn_W_lam")
    a["L0_attn_B_mat"] = g("L0_attn_B_mat")
    a["L0_attn_C_mat"] = g("L0_attn_C_mat")
    a["L0_attn_W_fromP"] = _tile_lhsT(g("L0_attn_W_fromP"))
    a["L0_ffn_W1"] = _tile_lhsT_e(g("L0_ffn_W1"))
    a["L0_ffn_b1"] = np.ascontiguousarray(
        g("L0_ffn_b1").reshape(8, 32, 128).transpose(2, 0, 1))
    a["L0_ffn_W2"] = g("L0_ffn_W2")
    a["L0_ffn_b2"] = np.ascontiguousarray(g("L0_ffn_b2")[None])
    a["L0_ffn_gate_W"] = g("L0_ffn_gate_W")
    a["L0_ffn_shared_W"] = _tile_lhsT(g("L0_ffn_shared_W"))
    a["L0_ffn_shared_b_col"] = np.ascontiguousarray(
        g("L0_ffn_shared_b").reshape(NCH, 128).T)
    a["L0_ffn_gamma_col"] = np.ascontiguousarray(
        g("L0_ffn_gamma").reshape(NCH, 128).T)
    a["L0_ffn_beta_col"] = np.ascontiguousarray(
        g("L0_ffn_beta").reshape(NCH, 128).T)
    for w in ("Wq", "Wk", "Wv", "Wo"):
        a[f"L1_attn_{w}T"] = _tile_lhsT(np.ascontiguousarray(g(f"L1_attn_{w}").T))
    a["L1_ffn_Wr"] = g("L1_ffn_Wr")
    a["L1_ffn_br"] = np.ascontiguousarray(g("L1_ffn_br")[None])
    a["L1_ffn_W1"] = _tile_lhsT_e(g("L1_ffn_W1"))
    a["L1_ffn_b1"] = np.ascontiguousarray(
        g("L1_ffn_b1").reshape(8, 32, 128).transpose(2, 0, 1))
    a["L1_ffn_W2"] = g("L1_ffn_W2")
    a["L1_ffn_b2"] = np.ascontiguousarray(g("L1_ffn_b2")[None])
    a["L2_ffn_Wr"] = g("L2_ffn_Wr")
    a["L2_attn_W1T"] = _tile_lhsT(np.ascontiguousarray(g("L2_attn_W1").T))
    a["L2_attn_b2c"] = np.ascontiguousarray(g("L2_attn_b2").reshape(1, 1))
    a["L2_attn_w2c"] = np.ascontiguousarray(g("L2_attn_w2")[0].reshape(NCH, 128).T)
    a["L2_attn_b1c"] = np.ascontiguousarray(g("L2_attn_b1").reshape(NCH, 128).T)
    a["L2_attn_WcT"] = _tile_lhsT(np.ascontiguousarray(g("L2_attn_Wc").T))
    a["L2_attn_bcc"] = np.ascontiguousarray(g("L2_attn_bc").reshape(NCH, 128).T)
    a["L2_ffn_nwcol"] = np.ascontiguousarray(
        g("L2_ffn_norm_w").reshape(8, 8, 128).transpose(2, 0, 1))
    a["L2_ffn_fncol"] = np.ascontiguousarray(g("L2_ffn_f_norm").reshape(NCH, 128).T)
    a["L2_ffn_Wg"] = _tile_lhsT_e(g("L2_ffn_Wg"))
    a["L2_ffn_Wu"] = _tile_lhsT_e(g("L2_ffn_Wu"))
    a["L2_ffn_Wd"] = g("L2_ffn_Wd")
    a["L2_ffn_fWg"] = _tile_lhsT(g("L2_ffn_fWg"))
    a["L2_ffn_fWu"] = _tile_lhsT(g("L2_ffn_fWu"))
    a["L2_ffn_fWd"] = g("L2_ffn_fWd")
    perm = np.arange(3072)
    for sec in (0, 1024):
        for h in range(16):
            base = sec + h * 64
            perm[base:base + 64] = base + np.concatenate(
                [np.arange(0, 64, 2), np.arange(1, 64, 2)])
    a["L3_attn_WqkvT"] = _tile_lhsT(np.ascontiguousarray(g("L3_attn_Wqkv").T[:, perm]))
    a["L3_attn_bqkv_col"] = np.ascontiguousarray(
        g("L3_attn_bqkv")[perm].reshape(24, 128).T)
    a["L3_attn_gq_col"] = np.ascontiguousarray(
        g("L3_attn_gq")[perm[0:1024]].reshape(NCH, 128).T)
    a["L3_attn_gk_col"] = np.ascontiguousarray(
        g("L3_attn_gk")[perm[1024:2048] - 1024].reshape(NCH, 128).T)
    a["L3_attn_WoutT"] = _tile_lhsT(np.ascontiguousarray(g("L3_attn_Wout").T))
    a["L3_attn_bout_col"] = np.ascontiguousarray(
        g("L3_attn_bout").reshape(NCH, 128).T)
    a["L3_attn_gout_col"] = np.ascontiguousarray(
        g("L3_attn_gout").reshape(NCH, 128).T)
    a["L3_ffn_W_fcT"] = _tile_lhsT(np.ascontiguousarray(g("L3_ffn_W_fc").T))
    a["L3_ffn_lnw_col"] = np.ascontiguousarray(g("L3_ffn_ln_w").reshape(32, 128).T)
    a["L3_ffn_lnb_col"] = np.ascontiguousarray(g("L3_ffn_ln_b").reshape(32, 128).T)
    a["L3_ffn_W_projT"] = np.ascontiguousarray(g("L3_ffn_W_proj").T)
    return a


def build_launch(arrs, blocks, debug=None):
    ctx = Ctx()
    nc = ctx.nc
    hb = {}
    for name, a in arrs.items():
        hb[name] = nc.dram_tensor(name, list(a.shape), F32, kind="ExternalInput")
    x_in = nc.dram_tensor("x_in", [128, NCH, T], F32, kind="ExternalInput")
    x_out = nc.dram_tensor("x_out", [128, NCH, T], F32, kind="ExternalOutput")
    ctx.setup()
    P = ctx.P
    x_sb = ctx.sb("x_sb", [128, NCH, T], F32)
    xn_bf = ctx.sb("xn_bf", [128, NCH, T], BF16)
    ctx.init_scratch()
    P.dma("SPQ", x_sb[:, :, :], x_in.ap(),
          writes=[rk(x_sb, c) for c in range(NCH)])
    ctx.dbg = {}
    if debug:
        for nm, shape, dt_ in debug:
            ctx.dbg[nm] = nc.dram_tensor("dbg_" + nm, shape, dt_,
                                         kind="ExternalOutput")

    if blocks == ("ffn_dbg",):
        mk = ctx.scr_mark()
        c_tm = ctx.scr("dbg_ctm", [128, 16, 8], F32)
        for j in range(16):
            P.op("DVE", lambda e, j=j: e.memset(c_tm[:, j, :], 0.0),
                 writes=[rk(c_tm, j)])
            P.op("DVE", lambda e, j=j: e.memset(c_tm[:, j, 0:1], 1.0),
                 reads=[rk(c_tm, j)], writes=[rk(c_tm, j)])
        if debug and ("noln" in ctx.dbg):
            for c in range(NCH):
                for tt2 in range(NTT):
                    P.op("DVE", lambda e, c=c, tt2=tt2: e.tensor_copy(
                        xn_bf[:, c, tt2 * TT:(tt2 + 1) * TT],
                        x_sb[:, c, tt2 * TT:(tt2 + 1) * TT]),
                        reads=[rk(x_sb, c)], writes=[rk(xn_bf, c)])
        else:
            ln_block(ctx, x_sb, xn_bf)
        if "xn" in ctx.dbg:
            P.dma("SPQ", ctx.dbg["xn"].ap(), xn_bf[:, :, :],
                  reads=[rk(xn_bf, c) for c in range(NCH)])
        for c in range(NCH):
            P.op("DVE", lambda e, c=c: e.memset(x_sb[:, c, :], 0.0),
                 writes=[rk(x_sb, c)])
        moe_ffn(ctx, hb, x_sb, xn_bf, c_tm, "l0f", "L0_ffn_W1", "L0_ffn_b1",
                "L0_ffn_W2", "L0_ffn_b2", act=AF.Gelu, shared=None, out_ln=None,
                nexp=1)
        ctx.scr_release(mk)
    elif 0 in blocks:
        mk = ctx.scr_mark()
        gate_sb = ctx.scr("l0_gate", [128, NCH, 8], F32)
        lg = ctx.scr("l0_lg", [128, 16, 8], F32)
        c_tm = ctx.scr("l0_ctm", [128, 16, 8], F32)
        P.dma("SPQ", gate_sb[:, :, :],
              hb["L0_ffn_gate_W"].ap().rearrange("(c p) e -> p c e", p=128),
              writes=[rk(gate_sb)])
        ln_block(ctx, x_sb, xn_bf, router_w=gate_sb, lg_tm=lg)
        if "xn" in ctx.dbg:
            P.dma("SPQ", ctx.dbg["xn"].ap(), xn_bf[:, :, :],
                  reads=[rk(xn_bf, c) for c in range(NCH)])
        if "rowA" in ctx.dbg:
            P.dma("SPQ", ctx.dbg["rowA"].ap(), ctx.rowA[:, :], reads=[rk(ctx.rowA)])
        if "rowB" in ctx.dbg:
            P.dma("SPQ", ctx.dbg["rowB"].ap(), ctx.rowB[:, :], reads=[rk(ctx.rowB)])
        if "lg" in ctx.dbg:
            P.dma("SPQ", ctx.dbg["lg"].ap(), lg[:, :, :], reads=[rk(lg)])
        l0_attn(ctx, hb, x_sb, xn_bf)
        if "xattn" in ctx.dbg:
            P.dma("SPQ", ctx.dbg["xattn"].ap(), x_sb[:, :, :],
                  reads=[rk(x_sb, c) for c in range(NCH)])
        if "ctm" in ctx.dbg:
            _ev = topk_combine_l0(ctx, lg, c_tm)
            P.dma("SPQ", ctx.dbg["ctm"].ap(), c_tm[:, :, :],
                  reads=[rk(c_tm, j) for j in range(16)])
        topk_combine_l0(ctx, lg, c_tm)
        moe_ffn(ctx, hb, x_sb, xn_bf, c_tm, "l0f", "L0_ffn_W1", "L0_ffn_b1",
                "L0_ffn_W2", "L0_ffn_b2", act=AF.Gelu,
                shared={"w": "L0_ffn_shared_W", "b": "L0_ffn_shared_b_col"},
                out_ln={"g": "L0_ffn_gamma_col", "b": "L0_ffn_beta_col"})
        ctx.scr_release(mk)

    if "l2r" in blocks:
        ctx.ind1_o = nc.dram_tensor("ind1_o", [128, 16, 8], F32, kind="ExternalOutput")
        ctx.ind2_o = nc.dram_tensor("ind2_o", [128, 16, 8], F32, kind="ExternalOutput")
        ctx.p1_o = nc.dram_tensor("p1_o", [128, 16], F32, kind="ExternalOutput")
        ctx.p2_o = nc.dram_tensor("p2_o", [128, 16], F32, kind="ExternalOutput")
        ctx.cnt_o = nc.dram_tensor("cnt_o", [1, 8], F32, kind="ExternalOutput")
        ctx.xn2_o = nc.dram_tensor("xn2_o", [128, NCH, T], BF16, kind="ExternalOutput")

    if 1 in blocks:
        mk = ctx.scr_mark()
        r_sb = ctx.scr("l1_wr", [128, NCH, 8], F32)
        br_bc = ctx.scr("l1_br", [128, 8], F32)
        lg = ctx.scr("l1_lg", [128, 16, 8], F32)
        c_tm = ctx.scr("l1_ctm", [128, 16, 8], F32)
        P.dma("SPQ", r_sb[:, :, :],
              hb["L1_ffn_Wr"].ap().rearrange("(c p) e -> p c e", p=128),
              writes=[rk(r_sb)])
        P.dma("SPQ", br_bc[:, :],
              hb["L1_ffn_br"].ap().partition_broadcast(128)[:, 0, :],
              writes=[rk(br_bc)])
        ln_block(ctx, x_sb, xn_bf, router_w=r_sb, lg_tm=lg)
        l1_attn(ctx, hb, x_sb, xn_bf)
        topk_combine_l1(ctx, lg, c_tm, br_bc)
        moe_ffn(ctx, hb, x_sb, xn_bf, c_tm, "l1f", "L1_ffn_W1", "L1_ffn_b1",
                "L1_ffn_W2", "L1_ffn_b2", act=AF.Relu)
        ctx.scr_release(mk)

    if "l2r" in blocks:
        mk = ctx.scr_mark()
        r2_sb = ctx.scr("l2_wr", [128, NCH, 8], F32)
        lg2 = ctx.scr("l2_lg", [128, 16, 8], F32)
        ind1 = ctx.scr("l2_i1", [128, 16, 8], F32)
        ind2 = ctx.scr("l2_i2", [128, 16, 8], F32)
        p1 = ctx.scr("l2_p1", [128, 16], F32)
        p2 = ctx.scr("l2_p2", [128, 16], F32)
        cnt = ctx.scr("l2_cnt", [1, 8], F32)
        P.dma("SPQ", r2_sb[:, :, :],
              hb["L2_ffn_Wr"].ap().rearrange("(c p) e -> p c e", p=128),
              writes=[rk(r2_sb)])
        ln_block(ctx, x_sb, xn_bf, router_w=r2_sb, lg_tm=lg2)
        topk_l2_launch1(ctx, lg2, ind1, ind2, p1, p2, cnt)
        P.dma("SPQ", ctx.ind1_o.ap(), ind1[:, :, :],
              reads=[rk(ind1, j) for j in range(16)])
        P.dma("SPQ", ctx.ind2_o.ap(), ind2[:, :, :],
              reads=[rk(ind2, j) for j in range(16)])
        P.dma("SPQ", ctx.p1_o.ap(), p1[:, :], reads=[rk(p1)])
        P.dma("SPQ", ctx.p2_o.ap(), p2[:, :], reads=[rk(p2)])
        P.dma("SPQ", ctx.cnt_o.ap(), cnt[:, :], reads=[rk(cnt)])
        P.dma("SPQ", ctx.xn2_o.ap(), xn_bf[:, :, :],
              reads=[rk(xn_bf, c) for c in range(NCH)])
        ctx.scr_release(mk)

    P.dma("SPQ", x_out.ap(), x_sb[:, :, :],
          reads=[rk(x_sb, c) for c in range(NCH)])
    with nc.Block() as block:
        P.emit(block)
    return ctx


def _run(inputs, blocks=(0,), trace=False, debug=None, dbg_out=None):
    arrs = host_prep(inputs)
    x = np.asarray(inputs["L0_attn_x"], dtype=np.float32)
    ctx = build_launch(arrs, blocks, debug=debug)
    in_maps = []
    for b in range(NCORES):
        m = dict(arrs)
        m["x_in"] = np.ascontiguousarray(
            x[b].T.reshape(NCH, 128, T).transpose(1, 0, 2))
        in_maps.append(m)
    res = run_bass_kernel_spmd(ctx.nc, in_maps, core_ids=list(range(NCORES)),
                               trace=trace)
    out = np.empty((NCORES, T, D), np.float32)
    for b in range(NCORES):
        xo = res.results[b]["x_out"]
        out[b] = xo.transpose(1, 0, 2).reshape(D, T).T
    if trace:
        print("exec_time_ns:", res.exec_time_ns)
    if dbg_out is not None:
        dbg_out.append(res.results)
    return out


def kernel(**inputs):
    return _run(inputs, blocks=(0,))



def l1_attn(ctx, hb, x_sb, xn_bf):
    nc, P = ctx.nc, ctx.P
    bank = ctx.bank
    _mk = ctx.scr_mark()
    attn_buf = ctx.scr("l1_attn", [128, NCH, T], BF16)
    q_bf = ctx.scr("l1_q", [128, T], BF16)
    k_bf = ctx.scr("l1_k", [128, T], BF16)
    v_bf = ctx.scr("l1_v", [128, T], BF16)
    kT_sb = ctx.scr("l1_kT", [128, 2, 64], BF16)
    vT_sb = ctx.scr("l1_vT", [128, 2, 65], BF16)
    kv_bf = ctx.scr("l1_kv", [128, 65], BF16)
    z_sb = ctx.scr("l1_z", [65, T], F32)
    msk_bf = ctx.scr("l1_msk", [128, 2, 128], BF16)
    tri_h = nc.inline_tensor(np.triu(np.ones((128, 128), np.float32)), "c_tri")
    tri_sb = ctx.scr("l1_tri", [128, 128], F32)
    P.dma("SPQ", tri_sb[:, :], tri_h[:, :], writes=[rk(tri_sb)])
    b7, b6 = bank[7], bank[6]
    ctx.psum_free(bank[7])
    ctx.psum_free(bank[6])
    tp0 = ctx.psum(ctx.u("l1tp0"), [128, 512], BF16)
    tp1 = ctx.psum(ctx.u("l1tp1"), [128, 512], BF16)
    P.inherit(rk(tp0), rk(b7))
    P.inherit(rk(tp1), rk(b6))

    def proj_pair(wname, hp, dst, relu_eps):
        wr = hb[wname].ap()

        def src(mb, hp=hp):
            return wr[hp]

        def rhs(c, tt):
            return xn_bf[:, c, tt * TT:(tt + 1) * TT], [rk(xn_bf, c)]

        def evac(mb, tt, bk, dst=dst, relu_eps=relu_eps):
            tsl = slice(tt * TT, (tt + 1) * TT)
            if relu_eps:
                P.op("DVE", lambda e, bk=bk, tsl=tsl: e.tensor_scalar(
                    dst[:, tsl], bk[:, :], 0.0, 1e-6, ALU.max, ALU.add),
                    reads=[rk(bk)], writes=[rk(dst)])
            else:
                P.op("ACT", lambda e, bk=bk, tsl=tsl: e.copy(dst[:, tsl], bk[:, :]),
                     reads=[rk(bk)], writes=[rk(dst)])
        proj_stream(ctx, src, 1, NCH, rhs, evac, banks=(4, 5))

    for hp in range(NCH):
        proj_pair("L1_attn_WqT", hp, q_bf, True)
        proj_pair("L1_attn_WkT", hp, k_bf, True)
        proj_pair("L1_attn_WvT", hp, v_bf, False)
        for hbase in (0, 64):
            P.op("DVE", lambda e: e.memset(vT_sb[:, 0, 64:65], 1.0),
                 writes=[rk(vT_sb, 0, "one")])
            P.op("DVE", lambda e: e.memset(vT_sb[:, 1, 64:65], 1.0),
                 writes=[rk(vT_sb, 1, "one")])
            for ci in range(16):
                pb = ci % 2
                csl = slice(ci * 128, (ci + 1) * 128)
                tpb = tp0 if pb == 0 else tp1
                P.op("PE", lambda e, hbase=hbase, csl=csl: e.matmul(
                    bank[0][:, 0:128], k_bf[hbase:hbase + 64, csl],
                    q_bf[hbase:hbase + 64, csl], start=True, stop=True),
                    reads=[rk(k_bf), rk(q_bf)], writes=[rk(bank[0])])
                P.op("DVE", lambda e, pb=pb: e.tensor_tensor(
                    msk_bf[:, pb, :], bank[0][:, 0:128], tri_sb[:, :], ALU.mult),
                    reads=[rk(bank[0]), rk(tri_sb)], writes=[rk(msk_bf, pb)])
                P.op("PE", lambda e, hbase=hbase, csl=csl, tpb=tpb: e.transpose(
                    tpb[:, 0:64], k_bf[hbase:hbase + 64, csl],
                    ctx.id_b[hbase:hbase + 64, hbase:hbase + 64]),
                    reads=[rk(k_bf), rk(ctx.id_b)], writes=[rk(tpb, 0)])
                P.op("ACT", lambda e, pb=pb, tpb=tpb: e.copy(
                    kT_sb[:, pb, :], tpb[:, 0:64]),
                    reads=[rk(tpb, 0)], writes=[rk(kT_sb, pb)])
                P.op("PE", lambda e, hbase=hbase, csl=csl, tpb=tpb: e.transpose(
                    tpb[:, 64:128], v_bf[hbase:hbase + 64, csl],
                    ctx.id_b[hbase:hbase + 64, hbase:hbase + 64]),
                    reads=[rk(v_bf), rk(ctx.id_b)], writes=[rk(tpb, 1)])
                P.op("ACT", lambda e, pb=pb, tpb=tpb: e.copy(
                    vT_sb[:, pb, 0:64], tpb[:, 64:128]),
                    reads=[rk(tpb, 1)], writes=[rk(vT_sb, pb)])
                zrow = 64 if hbase == 0 else 0
                P.op("PE", lambda e, pb=pb, hbase=hbase, ci=ci: e.matmul(
                    bank[1][hbase:hbase + 64, 0:128], vT_sb[:, pb, 0:64],
                    msk_bf[:, pb, :], start=True, stop=(ci == 0),
                    skip_group_check=True),
                    reads=[rk(vT_sb, pb), rk(msk_bf, pb)],
                    writes=[rk(bank[1])])
                P.op("PE", lambda e, pb=pb, zrow=zrow, ci=ci: e.matmul(
                    bank[1][zrow:zrow + 1, 0:128], ctx.ones_b[:, 0:1],
                    msk_bf[:, pb, :], start=True, stop=(ci == 0),
                    skip_group_check=True),
                    reads=[rk(msk_bf, pb), rk(ctx.ones_b)], writes=[rk(bank[1])])
                if ci > 0:
                    P.op("PE", lambda e, hbase=hbase, csl=csl: e.matmul(
                        bank[1][hbase:hbase + 64, 0:128],
                        kv_bf[hbase:hbase + 64, 0:64],
                        q_bf[hbase:hbase + 64, csl], start=False, stop=True,
                        skip_group_check=True),
                        reads=[rk(kv_bf), rk(q_bf)], writes=[rk(bank[1])])
                    P.op("PE", lambda e, hbase=hbase, csl=csl, zrow=zrow: e.matmul(
                        bank[1][zrow:zrow + 1, 0:128],
                        kv_bf[hbase:hbase + 64, 64:65],
                        q_bf[hbase:hbase + 64, csl], start=False, stop=True,
                        skip_group_check=True),
                        reads=[rk(kv_bf), rk(q_bf)], writes=[rk(bank[1])])
                P.op("PE", lambda e, ci=ci, hbase=hbase, pb=pb: e.matmul(
                    bank[2][hbase:hbase + 64, 0:65], kT_sb[:, pb, :],
                    vT_sb[:, pb, :], start=(ci == 0), stop=(ci == 15),
                    skip_group_check=True),
                    reads=[rk(kT_sb, pb), rk(vT_sb, pb), rk(vT_sb, pb, "one")],
                    writes=[rk(bank[2])])
                P.op("ACT", lambda e, hbase=hbase: e.copy(
                    kv_bf[hbase:hbase + 64, :], bank[2][hbase:hbase + 64, 0:65]),
                    reads=[rk(bank[2])], writes=[rk(kv_bf)])
                P.op("ACT", lambda e, hp=hp, hbase=hbase, csl=csl: e.copy(
                    attn_buf[hbase:hbase + 64, hp, csl],
                    bank[1][hbase:hbase + 64, 0:128]),
                    reads=[rk(bank[1])], writes=[rk(attn_buf, hp, hbase)])
                P.op("ACT", lambda e, csl=csl, zrow=zrow: e.copy(
                    z_sb[zrow:zrow + 1, csl], bank[1][zrow:zrow + 1, 0:128]),
                    reads=[rk(bank[1])], writes=[rk(z_sb)])
            zrow = 64 if hbase == 0 else 0
            P.op("DVE", lambda e, zrow=zrow: e.tensor_single_scalar(
                z_sb[zrow:zrow + 1, :], z_sb[zrow:zrow + 1, :], 1e-6, ALU.add),
                reads=[rk(z_sb)], writes=[rk(z_sb)])
            P.op("DVE", lambda e, zrow=zrow: e.reciprocal(
                z_sb[zrow:zrow + 1, :], z_sb[zrow:zrow + 1, :]),
                reads=[rk(z_sb)], writes=[rk(z_sb)])
            for tt in range(NTT):
                tsl = slice(tt * TT, (tt + 1) * TT)
                P.op("PE", lambda e, tsl=tsl, hbase=hbase, zrow=zrow: e.matmul(
                    bank[3][hbase:hbase + 64, :], ctx.ones_f[zrow:zrow + 1, 0:64],
                    z_sb[zrow:zrow + 1, tsl], start=True, stop=True),
                    reads=[rk(z_sb), rk(ctx.ones_f)], writes=[rk(bank[3])])
                P.op("DVE", lambda e, hp=hp, hbase=hbase, tsl=tsl: e.tensor_tensor(
                    attn_buf[hbase:hbase + 64, hp, tsl],
                    attn_buf[hbase:hbase + 64, hp, tsl],
                    bank[3][hbase:hbase + 64, :], ALU.mult),
                    reads=[rk(attn_buf, hp, hbase), rk(bank[3])],
                    writes=[rk(attn_buf, hp, hbase)])
    ctx.psum_free(tp1)
    ctx.psum_free(tp0)
    nb6 = ctx.psum(ctx.u("bank6r"), [128, 512], F32)
    nb7 = ctx.psum(ctx.u("bank7r"), [128, 512], F32)
    P.inherit(rk(nb6), rk(tp1))
    P.inherit(rk(nb7), rk(tp0))
    ctx.bank[6] = nb6
    ctx.bank[7] = nb7
    bank = ctx.bank

    wo_r = hb["L1_attn_WoT"].ap()

    def wo_src(mb):
        return wo_r[mb]

    def wo_rhs(c, tt):
        return (attn_buf[:, c, tt * TT:(tt + 1) * TT],
                [rk(attn_buf, c, 0), rk(attn_buf, c, 64)])

    def wo_evac(mb, tt, bk):
        tsl = slice(tt * TT, (tt + 1) * TT)
        P.op("DVE", lambda e, mb=mb, tsl=tsl, bk=bk: e.scalar_tensor_tensor(
            x_sb[:, mb, tsl], bk[:, :], 1.0, x_sb[:, mb, tsl], ALU.mult, ALU.add),
            reads=[rk(bk), rk(x_sb, mb)], writes=[rk(x_sb, mb)])

    proj_stream(ctx, wo_src, NCH, NCH, wo_rhs, wo_evac, banks=(6, 7))
    ctx.scr_release(_mk)


def topk_combine_l1(ctx, lg_tm, c_tm, br_bc):
    P = ctx.P
    _mk = ctx.scr_mark()
    m1 = ctx.scr("m1", [128, 16], F32)
    nm1 = ctx.scr("nm1", [128, 16], F32)
    zz = ctx.scr("zz", [128, 16], F32)
    et = ctx.scr("et", [128, 16, 8], F32)
    for j in range(16):
        P.op("DVE", lambda e, j=j: e.tensor_tensor(
            lg_tm[:, j, :], lg_tm[:, j, :], br_bc[:, :], ALU.add),
            reads=[rk(lg_tm, j), rk(br_bc)], writes=[rk(lg_tm, j)])
    P.op("DVE", lambda e: e.reduce_max(m1[:, :], lg_tm[:, :, :], axis=AX.X),
         reads=[rk(lg_tm, j) for j in range(16)], writes=[rk(m1)])
    P.op("DVE", lambda e: e.reduce_max(nm1[:, :], lg_tm[:, :, :], axis=AX.X,
                                       negate=True),
         reads=[rk(lg_tm, j) for j in range(16)], writes=[rk(nm1)])
    for j in range(16):
        P.op("ACT", lambda e, j=j: e.activation(
            et[:, j, :], lg_tm[:, j, :], AF.Exp, bias=nm1[:, j:j + 1]),
            reads=[rk(lg_tm, j), rk(nm1)], writes=[rk(et, j)])
    P.op("DVE", lambda e: e.reduce_sum(zz[:, :], et[:, :, :], axis=AX.X),
         reads=[rk(et, j) for j in range(16)], writes=[rk(zz)])
    P.op("DVE", lambda e: e.reciprocal(zz[:, :], zz[:, :]),
         reads=[rk(zz)], writes=[rk(zz)])
    for j in range(16):
        P.op("DVE", lambda e, j=j: e.tensor_single_scalar(
            et[:, j, :], lg_tm[:, j, :], m1[:, j:j + 1], ALU.is_ge),
            reads=[rk(lg_tm, j), rk(m1)], writes=[rk(et, j)])
        P.op("DVE", lambda e, j=j: e.tensor_single_scalar(
            c_tm[:, j, :], et[:, j, :], zz[:, j:j + 1], ALU.mult),
            reads=[rk(et, j), rk(zz)], writes=[rk(c_tm, j)])
    ctx.scr_release(_mk)



def topk_l2_launch1(ctx, lg_tm, ind1, ind2, p1, p2, cnt_out):
    P = ctx.P
    bank = ctx.bank
    _mk = ctx.scr_mark()
    m1 = ctx.scr("m1", [128, 16], F32)
    nm1 = ctx.scr("nm1", [128, 16], F32)
    m2 = ctx.scr("m2", [128, 16], F32)
    zz = ctx.scr("zz", [128, 16], F32)
    et = ctx.scr("et", [128, 16, 8], F32)
    l2t = ctx.scr("l2t", [128, 16, 8], F32)
    P.op("DVE", lambda e: e.reduce_max(m1[:, :], lg_tm[:, :, :], axis=AX.X),
         reads=[rk(lg_tm, j) for j in range(16)], writes=[rk(m1)])
    P.op("DVE", lambda e: e.reduce_max(nm1[:, :], lg_tm[:, :, :], axis=AX.X,
                                       negate=True),
         reads=[rk(lg_tm, j) for j in range(16)], writes=[rk(nm1)])
    for j in range(16):
        P.op("DVE", lambda e, j=j: e.tensor_single_scalar(
            ind1[:, j, :], lg_tm[:, j, :], m1[:, j:j + 1], ALU.is_ge),
            reads=[rk(lg_tm, j), rk(m1)], writes=[rk(ind1, j)])
        P.op("DVE", lambda e, j=j: e.scalar_tensor_tensor(
            l2t[:, j, :], ind1[:, j, :], -1e30, lg_tm[:, j, :], ALU.mult, ALU.add),
            reads=[rk(ind1, j), rk(lg_tm, j)], writes=[rk(l2t, j)])
        P.op("ACT", lambda e, j=j: e.activation(
            et[:, j, :], lg_tm[:, j, :], AF.Exp, bias=nm1[:, j:j + 1]),
            reads=[rk(lg_tm, j), rk(nm1)], writes=[rk(et, j)])
    P.op("DVE", lambda e: e.reduce_max(m2[:, :], l2t[:, :, :], axis=AX.X),
         reads=[rk(l2t, j) for j in range(16)], writes=[rk(m2)])
    for j in range(16):
        P.op("DVE", lambda e, j=j: e.tensor_single_scalar(
            ind2[:, j, :], l2t[:, j, :], m2[:, j:j + 1], ALU.is_ge),
            reads=[rk(l2t, j), rk(m2)], writes=[rk(ind2, j)])
    P.op("DVE", lambda e: e.reduce_sum(zz[:, :], et[:, :, :], axis=AX.X),
         reads=[rk(et, j) for j in range(16)], writes=[rk(zz)])
    P.op("DVE", lambda e: e.reciprocal(p1[:, :], zz[:, :]),
         reads=[rk(zz)], writes=[rk(p1)])
    P.op("DVE", lambda e: e.tensor_tensor(m2[:, :], m2[:, :], m1[:, :],
                                          ALU.subtract),
         reads=[rk(m2), rk(m1)], writes=[rk(m2)])
    P.op("ACT", lambda e: e.activation(m2[:, :], m2[:, :], AF.Exp),
         reads=[rk(m2)], writes=[rk(m2)])
    P.op("DVE", lambda e: e.tensor_tensor(p2[:, :], m2[:, :], p1[:, :], ALU.mult),
         reads=[rk(m2), rk(p1)], writes=[rk(p2)])
    for j in range(16):
        P.op("PE", lambda e, j=j: e.matmul(
            bank[0][0:1, 0:8], ctx.ones_f[:, 0:1], ind1[:, j, :],
            start=(j == 0), stop=False, skip_group_check=True),
            reads=[rk(ind1, j), rk(ctx.ones_f)], writes=[rk(bank[0])])
        P.op("PE", lambda e, j=j: e.matmul(
            bank[0][0:1, 0:8], ctx.ones_f[:, 0:1], ind2[:, j, :],
            start=False, stop=(j == 15), skip_group_check=True),
            reads=[rk(ind2, j), rk(ctx.ones_f)], writes=[rk(bank[0])])
    P.op("ACT", lambda e: e.copy(cnt_out[:, :], bank[0][0:1, 0:8]),
         reads=[rk(bank[0])], writes=[rk(cnt_out)])
    ctx.scr_release(_mk)



def l2_attn(ctx, hb, x_sb, xn_bf):
    nc, P = ctx.nc, ctx.P
    bank = ctx.bank
    _mk = ctx.scr_mark()
    h2 = ctx.scr("l2_h2", [128, NCH, T], BF16)
    s_row = ctx.scr("l2_s", [1, T], F32)
    e_row = ctx.scr("l2_e", [1, T], F32)
    den = ctx.scr("l2_den", [1, T], F32)
    mrow = ctx.scr("l2_m", [1, 1], F32)
    w2col = ctx.scr("l2_w2c", [128, NCH], BF16)
    b1col = ctx.scr("l2_b1c", [128, NCH], F32)
    bccol = ctx.scr("l2_bcc", [128, NCH], F32)
    b2sb = ctx.scr("l2_b2s", [1, 1], F32)
    P.dma("SPQ", b2sb[:, :], hb["L2_attn_b2c"].ap(), writes=[rk(b2sb)])
    P.dma("PQ", w2col[:, :], hb["L2_attn_w2c"].ap(), writes=[rk(w2col)])
    P.dma("SPQ", b1col[:, :], hb["L2_attn_b1c"].ap(), writes=[rk(b1col)])
    P.dma("SPQ", bccol[:, :], hb["L2_attn_bcc"].ap(), writes=[rk(bccol)])
    w1_r = hb["L2_attn_W1T"].ap()

    def w1_src(mb):
        return w1_r[mb]

    def xn_rhs(c, tt):
        return xn_bf[:, c, tt * TT:(tt + 1) * TT], [rk(xn_bf, c)]

    def h2_evac(mb, tt, bk):
        tsl = slice(tt * TT, (tt + 1) * TT)
        P.op("ACT", lambda e, mb=mb, bk=bk, tsl=tsl: e.activation(
            h2[:, mb, tsl], bk[:, :], AF.Tanh, bias=b1col[:, mb:mb + 1]),
            reads=[rk(bk), rk(b1col)], writes=[rk(h2, mb)])

    proj_stream(ctx, w1_src, NCH, NCH, xn_rhs, h2_evac)
    for tt in range(NTT):
        tsl = slice(tt * TT, (tt + 1) * TT)
        for c in range(NCH):
            P.op("PE", lambda e, c=c, tsl=tsl: e.matmul(
                bank[0][0:1, :], w2col[:, c:c + 1], h2[:, c, tsl],
                start=(c == 0), stop=(c == NCH - 1), skip_group_check=True),
                reads=[rk(w2col), rk(h2, c)], writes=[rk(bank[0])])
        P.op("ACT", lambda e, tsl=tsl: e.copy(s_row[0:1, tsl], bank[0][0:1, :]),
             reads=[rk(bank[0])], writes=[rk(s_row)])
    P.op("DVE", lambda e: e.tensor_single_scalar(
        s_row[:, :], s_row[:, :], b2sb[0:1, 0:1], ALU.add),
        reads=[rk(s_row), rk(b2sb)], writes=[rk(s_row)])
    P.op("DVE", lambda e: e.reduce_max(mrow[:, :], s_row[:, :], axis=AX.X,
                                       negate=True),
         reads=[rk(s_row)], writes=[rk(mrow)])
    P.op("ACT", lambda e: e.activation(e_row[:, :], s_row[:, :], AF.Exp,
                                       bias=mrow[0:1, 0:1]),
         reads=[rk(s_row), rk(mrow)], writes=[rk(e_row)])
    P.op("DVE", lambda e: e.tensor_tensor_scan(
        den[:, :], e_row[:, :], e_row[:, :], 0.0, ALU.add, ALU.bypass),
        reads=[rk(e_row)], writes=[rk(den)])
    P.op("DVE", lambda e: e.reciprocal(den[:, :], den[:, :]),
         reads=[rk(den)], writes=[rk(den)])
    for c in range(NCH):
        for tt in range(NTT):
            tsl = slice(tt * TT, (tt + 1) * TT)
            ctx.bcast_row(e_row, tt * TT, bank[3])
            P.op("DVE", lambda e, c=c, tsl=tsl: e.tensor_tensor(
                h2[:, c, tsl], xn_bf[:, c, tsl], bank[3][:, :], ALU.mult),
                reads=[rk(xn_bf, c), rk(bank[3])], writes=[rk(h2, c)])
        P.op("DVE", lambda e, c=c: e.tensor_tensor_scan(
            h2[:, c, :], h2[:, c, :], h2[:, c, :], 0.0, ALU.add, ALU.bypass),
            reads=[rk(h2, c)], writes=[rk(h2, c)])
        for tt in range(NTT):
            tsl = slice(tt * TT, (tt + 1) * TT)
            ctx.bcast_row(den, tt * TT, bank[3])
            P.op("DVE", lambda e, c=c, tsl=tsl: e.tensor_tensor(
                h2[:, c, tsl], h2[:, c, tsl], bank[3][:, :], ALU.mult),
                reads=[rk(h2, c), rk(bank[3])], writes=[rk(h2, c)])
    wc_r = hb["L2_attn_WcT"].ap()

    def wc_src(mb):
        return wc_r[mb]

    def wc_rhs(c, tt):
        if c < NCH:
            return h2[:, c, tt * TT:(tt + 1) * TT], [rk(h2, c)]
        return xn_bf[:, c - NCH, tt * TT:(tt + 1) * TT], [rk(xn_bf, c - NCH)]

    def wc_evac(mb, tt, bk):
        tsl = slice(tt * TT, (tt + 1) * TT)
        P.op("ACT", lambda e, mb=mb, bk=bk: e.activation(
            ctx.ftmp[:, :], bk[:, :], AF.Tanh, bias=bccol[:, mb:mb + 1]),
            reads=[rk(bk), rk(bccol)], writes=[rk(ctx.ftmp)])
        P.op("DVE", lambda e, mb=mb, tsl=tsl: e.tensor_tensor(
            x_sb[:, mb, tsl], x_sb[:, mb, tsl], ctx.ftmp[:, :], ALU.add),
            reads=[rk(x_sb, mb), rk(ctx.ftmp)], writes=[rk(x_sb, mb)])

    proj_stream(ctx, wc_src, NCH, 2 * NCH, wc_rhs, wc_evac)
    ctx.scr_release(_mk)



def l2_ffn(ctx, hb, x_sb, xn_bf, ind1, ind2, p1, p2, cnts_sb, mask_sb):
    nc, P = ctx.nc, ctx.P
    bank = ctx.bank
    CAP = 4096.0
    _mk = ctx.scr_mark()
    c_tm = ctx.scr("l2_ctm", [128, 16, 8], F32)
    s0_tm = ctx.scr("l2_s0", [128, 16], F32)
    ov_tm = ctx.scr("l2_ovf", [128, 16], F32)

    _mkr = ctx.scr_mark()
    offs = ctx.scr("l2_offs", [8, 1], F32)
    em1 = ctx.scr("l2_em1", [8, T], F32)
    em2 = ctx.scr("l2_em2", [8, T], F32)
    kc1 = ctx.scr("l2_kc1", [8, T], F32)
    kc2 = ctx.scr("l2_kc2", [8, T], F32)
    prs = ctx.scr("l2_prs", [8, 2 * T], F32)
    kp1 = ctx.scr("l2_kp1", [128, 16, 8], F32)
    kp2 = ctx.scr("l2_kp2", [128, 16, 8], F32)
    P.op("PE", lambda e: e.matmul(bank[0][0:8, 0:1], cnts_sb[:, :], mask_sb[:, :],
                                  start=True, stop=True),
         reads=[rk(cnts_sb), rk(mask_sb)], writes=[rk(bank[0])])
    P.op("ACT", lambda e: e.activation(offs[:, :], bank[0][0:8, 0:1], AF.Copy,
                                       bias=-1.0),
         reads=[rk(bank[0])], writes=[rk(offs)])
    for nm_tm, em in ((ind1, em1), (ind2, em2)):
        for j in range(16):
            P.op("PE", lambda e, j=j, nm_tm=nm_tm: e.matmul(
                bank[1][0:8, (j % 4) * 128:(j % 4 + 1) * 128], nm_tm[:, j, :],
                ctx.id_f[:, :], start=True, stop=True, skip_group_check=True),
                reads=[rk(nm_tm, j), rk(ctx.id_f)], writes=[rk(bank[1])])
            if j % 4 == 3:
                P.op("ACT", lambda e, j=j, em=em: e.copy(
                    em[:, (j - 3) * 128:(j + 1) * 128], bank[1][0:8, :]),
                    reads=[rk(bank[1])], writes=[rk(em)])
    P.op("DVE", lambda e: e.tensor_copy(prs[:, 0:2 * T:2], em1[:, :]),
         reads=[rk(em1)], writes=[rk(prs)])
    P.op("DVE", lambda e: e.tensor_copy(prs[:, 1:2 * T:2], em2[:, :]),
         reads=[rk(em2)], writes=[rk(prs)])
    P.op("DVE", lambda e: e.tensor_tensor_scan(
        prs[:, :], prs[:, :], prs[:, :], 0.0, ALU.add, ALU.bypass),
        reads=[rk(prs)], writes=[rk(prs)])
    for em, kc, sl0 in ((em1, kc1, 0), (em2, kc2, 1)):
        P.op("DVE", lambda e, kc=kc, sl0=sl0: e.tensor_scalar(
            kc[:, :], prs[:, sl0:2 * T:2], offs[0:8, 0:1], CAP, ALU.add, ALU.is_lt),
            reads=[rk(prs), rk(offs)], writes=[rk(kc)])
        P.op("DVE", lambda e, em=em, kc=kc: e.tensor_tensor(
            kc[:, :], kc[:, :], em[:, :], ALU.mult),
            reads=[rk(kc), rk(em)], writes=[rk(kc)])
    for kc, kp in ((kc1, kp1), (kc2, kp2)):
        for j in range(16):
            jsl = slice(j * 128, (j + 1) * 128)
            P.op("PE", lambda e, kc=kc, jsl=jsl: e.transpose(
                bank[2][:, 0:8], kc[0:8, jsl], ctx.id_f[0:8, 0:8]),
                reads=[rk(kc), rk(ctx.id_f)], writes=[rk(bank[2])])
            P.op("ACT", lambda e, kp=kp, j=j: e.copy(kp[:, j, :], bank[2][:, 0:8]),
                 reads=[rk(bank[2])], writes=[rk(kp, j)])
    for j in range(16):
        P.op("DVE", lambda e, j=j: e.tensor_single_scalar(
            c_tm[:, j, :], kp1[:, j, :], p1[:, j:j + 1], ALU.mult),
            reads=[rk(kp1, j), rk(p1)], writes=[rk(c_tm, j)])
        P.op("DVE", lambda e, j=j: e.scalar_tensor_tensor(
            c_tm[:, j, :], kp2[:, j, :], p2[:, j:j + 1], c_tm[:, j, :],
            ALU.mult, ALU.add),
            reads=[rk(kp2, j), rk(p2), rk(c_tm, j)], writes=[rk(c_tm, j)])
        P.op("DVE", lambda e, j=j: e.tensor_tensor(
            kp1[:, j, :], kp1[:, j, :], kp2[:, j, :], ALU.add),
            reads=[rk(kp1, j), rk(kp2, j)], writes=[rk(kp1, j)])
    P.op("DVE", lambda e: e.reduce_sum(ov_tm[:, :], kp1[:, :, :], axis=AX.X),
         reads=[rk(kp1, j) for j in range(16)], writes=[rk(ov_tm)])
    P.op("DVE", lambda e: e.tensor_single_scalar(
        ov_tm[:, :], ov_tm[:, :], 2.0, ALU.is_lt),
        reads=[rk(ov_tm)], writes=[rk(ov_tm)])
    P.op("DVE", lambda e: e.reduce_sum(s0_tm[:, :], c_tm[:, :, :], axis=AX.X),
         reads=[rk(c_tm, j) for j in range(16)], writes=[rk(s0_tm)])
    P.op("DVE", lambda e: e.tensor_single_scalar(
        s0_tm[:, :], s0_tm[:, :], 1.0, ALU.add),
        reads=[rk(s0_tm)], writes=[rk(s0_tm)])
    ctx.scr_release(_mkr)

    nwcol = ctx.scr("l2_nw", [128, 8, 8], F32)
    fncol = ctx.scr("l2_fn", [128, NCH], F32)
    P.dma("SPQ", nwcol[:, :, :], hb["L2_ffn_nwcol"].ap(), writes=[rk(nwcol)])
    P.dma("SPQ", fncol[:, :], hb["L2_ffn_fncol"].ap(), writes=[rk(fncol)])
    xh = ctx.scr("l2_xh", [128, NCH, TT], BF16)
    xre = ctx.scr("l2_xre", [128, NCH, TT], BF16)
    gt = ctx.scr("l2_gt", [128, 16, TT], BF16)
    acc = ctx.scr("l2_acc", [128, NCH, TT], F32)
    facc = ctx.scr("l2_facc", [128, NCH, TT], F32)
    ftmp2 = ctx.scr("l2_f2", [128, TT], F32)
    wg_r = hb["L2_ffn_Wg"].ap()
    wu_r = hb["L2_ffn_Wu"].ap()
    wd_r = hb["L2_ffn_Wd"].ap().rearrange("e (k p) (oc m) -> e p k oc m", p=128, m=128)
    fg_r = hb["L2_ffn_fWg"].ap()
    fu_r = hb["L2_ffn_fWu"].ap()
    fd_r = hb["L2_ffn_fWd"].ap().rearrange("(k p) (oc m) -> p k oc m", p=128, m=128)
    w2buf = ctx.wstream2

    for tt in range(NTT):
        tsl = slice(tt * TT, (tt + 1) * TT)
        for c in range(NCH):
            P.op("ACT", lambda e, c=c, tsl=tsl: e.activation(
                ctx.sq[:, :], xn_bf[:, c, tsl], AF.Square),
                reads=[rk(xn_bf, c)], writes=[rk(ctx.sq)])
            P.op("PE", lambda e, c=c: e.matmul(
                bank[0][0:1, :], ctx.ones_f[:, 0:1], ctx.sq[:, :],
                start=(c == 0), stop=(c == NCH - 1), skip_group_check=True),
                reads=[rk(ctx.sq), rk(ctx.ones_f)], writes=[rk(bank[0])])
        P.op("ACT", lambda e, tsl=tsl: e.activation(
            ctx.rowA[0:1, 0:TT], bank[0][0:1, :], AF.Ln, bias=1e-6,
            scale=1.0 / D),
            reads=[rk(bank[0])], writes=[rk(ctx.rowA)])
        P.op("ACT", lambda e: e.activation(
            ctx.rowA[0:1, 0:TT], ctx.rowA[0:1, 0:TT], AF.Exp, scale=-0.5),
            reads=[rk(ctx.rowA)], writes=[rk(ctx.rowA)])
        ctx.bcast_row(ctx.rowA, 0, bank[3])
        for c in range(NCH):
            P.op("DVE", lambda e, c=c, tsl=tsl: e.tensor_tensor(
                xh[:, c, :], xn_bf[:, c, tsl], bank[3][:, :], ALU.mult),
                reads=[rk(xn_bf, c), rk(bank[3])], writes=[rk(xh, c)])
            P.op("DVE", lambda e, c=c: e.memset(acc[:, c, :], 0.0),
                 writes=[rk(acc, c)])

        def xre_rhs(c, _tt):
            return xre[:, c, :], [rk(xre, c)]

        for ex in range(8):
            for c in range(NCH):
                P.op("DVE", lambda e, c=c, ex=ex: e.tensor_single_scalar(
                    xre[:, c, :], xh[:, c, :], s2d(nwcol, ex, c), ALU.mult),
                    reads=[rk(xh, c), rk(nwcol)], writes=[rk(xre, c)])

            def g_evac(mb, _tt, bk):
                P.op("ACT", lambda e, mb=mb, bk=bk: e.activation(
                    gt[:, mb, :], bk[:, :], AF.Gelu),
                    reads=[rk(bk)], writes=[rk(gt, mb)])
            proj_stream(ctx, lambda mb, ex=ex: wg_r[ex, mb],
                        16, NCH, xre_rhs, g_evac, tts=(tt,))

            def u_evac(mb, _tt, bk):
                P.op("DVE", lambda e, mb=mb, bk=bk: e.tensor_tensor(
                    gt[:, mb, :], gt[:, mb, :], bk[:, :], ALU.mult),
                    reads=[rk(gt, mb), rk(bk)], writes=[rk(gt, mb)])
            proj_stream(ctx, lambda mb, ex=ex: wu_r[ex, mb],
                        16, NCH, xre_rhs, u_evac, tts=(tt,))

            for j in range(4):
                jj = tt * 4 + j
                P.op("PE", lambda e, j=j, jj=jj, ex=ex: e.matmul(
                    bank[2][0:1, j * 128:(j + 1) * 128], s2d(c_tm, jj, ex),
                    ctx.id_f[:, :], start=True, stop=True, skip_group_check=True),
                    reads=[rk(c_tm, jj)], writes=[rk(bank[2])])
            P.op("ACT", lambda e: e.copy(ctx.bc_sb[0:1, :], bank[2][0:1, :]),
                 reads=[rk(bank[2])], writes=[rk(ctx.bc_sb)])
            P.op("PE", lambda e: e.matmul(
                bank[3][:, :], ctx.ones_f[0:1, :], ctx.bc_sb[0:1, :],
                start=True, stop=True),
                reads=[rk(ctx.bc_sb), rk(ctx.ones_f)], writes=[rk(bank[3])])
            P.op("ACT", lambda e: e.copy(ctx.bc_sb[:, :], bank[3][:, :]),
                 reads=[rk(bank[3])], writes=[rk(ctx.bc_sb)])
            for k in range(16):
                b = k % 3
                P.dma("PQ", w2buf[:, b, :, :], wd_r[ex][:, k, :, :],
                      writes=[rk(w2buf, b)])
                for o in range(NCH):
                    P.op("PE", lambda e, k=k, b=b, o=o: e.matmul(
                        bank[o][:, :], w2buf[:, b, o, :], gt[:, k, :],
                        start=(k == 0), stop=(k == 15), skip_group_check=True),
                        reads=[rk(w2buf, b), rk(gt, k)], writes=[rk(bank[o])])
            for o in range(NCH):
                P.op("DVE", lambda e, o=o: e.tensor_tensor(
                    ftmp2[:, :], bank[o][:, :], ctx.bc_sb[:, :], ALU.mult),
                    reads=[rk(bank[o]), rk(ctx.bc_sb)], writes=[rk(ftmp2)])
                P.op("DVE", lambda e, o=o: e.tensor_tensor(
                    acc[:, o, :], acc[:, o, :], ftmp2[:, :], ALU.add),
                    reads=[rk(acc, o), rk(ftmp2)], writes=[rk(acc, o)])

        for c in range(NCH):
            P.op("DVE", lambda e, c=c: e.tensor_single_scalar(
                xre[:, c, :], xh[:, c, :], fncol[:, c:c + 1], ALU.mult),
                reads=[rk(xh, c), rk(fncol)], writes=[rk(xre, c)])

        def fg_evac(mb, _tt, bk):
            P.op("ACT", lambda e, mb=mb, bk=bk: e.activation(
                gt[:, mb, :], bk[:, :], AF.Gelu),
                reads=[rk(bk)], writes=[rk(gt, mb)])
        proj_stream(ctx, lambda mb: fg_r[mb],
                    4, NCH, xre_rhs, fg_evac, tts=(tt,))

        def fu_evac(mb, _tt, bk):
            P.op("DVE", lambda e, mb=mb, bk=bk: e.tensor_tensor(
                gt[:, mb, :], gt[:, mb, :], bk[:, :], ALU.mult),
                reads=[rk(gt, mb), rk(bk)], writes=[rk(gt, mb)])
        proj_stream(ctx, lambda mb: fu_r[mb],
                    4, NCH, xre_rhs, fu_evac, tts=(tt,))
        for k in range(4):
            b = k % 3
            P.dma("PQ", w2buf[:, b, :, :], fd_r[:, k, :, :], writes=[rk(w2buf, b)])
            for o in range(NCH):
                P.op("PE", lambda e, k=k, b=b, o=o: e.matmul(
                    bank[o][:, :], w2buf[:, b, o, :], gt[:, k, :],
                    start=(k == 0), stop=(k == 3), skip_group_check=True),
                    reads=[rk(w2buf, b), rk(gt, k)], writes=[rk(bank[o])])
        for o in range(NCH):
            P.op("ACT", lambda e, o=o: e.copy(facc[:, o, :], bank[o][:, :]),
                 reads=[rk(bank[o])], writes=[rk(facc, o)])

        ctx.tm_cols_to_row(s0_tm, tt * 4, ctx.rowA, 0, bank[1], n=TT)
        ctx.tm_cols_to_row(ov_tm, tt * 4, ctx.rowB, 0, bank[1], n=TT)
        ctx.bcast_row(ctx.rowA, 0, bank[4])
        ctx.bcast_row(ctx.rowB, 0, bank[5])
        for o in range(NCH):
            P.op("DVE", lambda e, o=o, tsl=tsl: e.tensor_tensor(
                ctx.ftmp[:, :], xn_bf[:, o, tsl], bank[4][:, :], ALU.mult),
                reads=[rk(xn_bf, o), rk(bank[4])], writes=[rk(ctx.ftmp)])
            P.op("DVE", lambda e, o=o: e.tensor_tensor(
                ctx.ftmp[:, :], ctx.ftmp[:, :], acc[:, o, :], ALU.add),
                reads=[rk(ctx.ftmp), rk(acc, o)], writes=[rk(ctx.ftmp)])
            P.op("DVE", lambda e, o=o, tsl=tsl: e.tensor_tensor(
                ftmp2[:, :], facc[:, o, :], xn_bf[:, o, tsl], ALU.add),
                reads=[rk(facc, o), rk(xn_bf, o)], writes=[rk(ftmp2)])
            P.op("DVE", lambda e: e.tensor_tensor(
                ftmp2[:, :], ftmp2[:, :], ctx.ftmp[:, :], ALU.subtract),
                reads=[rk(ftmp2), rk(ctx.ftmp)], writes=[rk(ftmp2)])
            P.op("DVE", lambda e: e.tensor_tensor(
                ftmp2[:, :], ftmp2[:, :], bank[5][:, :], ALU.mult),
                reads=[rk(ftmp2), rk(bank[5])], writes=[rk(ftmp2)])
            P.op("DVE", lambda e: e.tensor_tensor(
                ctx.ftmp[:, :], ctx.ftmp[:, :], ftmp2[:, :], ALU.add),
                reads=[rk(ctx.ftmp), rk(ftmp2)], writes=[rk(ctx.ftmp)])
            P.op("DVE", lambda e, o=o, tsl=tsl: e.tensor_tensor(
                x_sb[:, o, tsl], x_sb[:, o, tsl], ctx.ftmp[:, :], ALU.add),
                reads=[rk(x_sb, o), rk(ctx.ftmp)], writes=[rk(x_sb, o)])
    ctx.scr_release(_mk)



def _quant_thr(ctx, w_r, nm, nel, tag):
    P = ctx.P
    bank = ctx.bank
    gacc = ctx.scr(tag + "_gacc", [128, 1], F32)
    tcol = ctx.scr(tag + "_tcol", [128, 1], F32)
    thr = ctx.scr(tag + "_thr", [128, 1], F32)
    wf32 = ctx.wf32
    P.op("DVE", lambda e: e.memset(gacc[:, :], 0.0), writes=[rk(gacc)])
    for mb in range(nm):
        b = mb % 3
        P.dma("SPQ", wf32[:, b, :, :], w_r(mb), writes=[rk(wf32, b)])
        P.op("DVE", lambda e, b=b: e.tensor_reduce(
            tcol[:, :], wf32[:, b, :, :].rearrange("p a m -> p (a m)"),
            axis=AX.X, op=ALU.add, apply_absolute_value=True),
            reads=[rk(wf32, b)], writes=[rk(tcol)])
        P.op("DVE", lambda e: e.tensor_tensor(gacc[:, :], gacc[:, :], tcol[:, :],
                                              ALU.add),
             reads=[rk(gacc), rk(tcol)], writes=[rk(gacc)])
    P.op("PE", lambda e: e.matmul(bank[0][0:1, 0:1], gacc[:, :],
                                  ctx.ones_f[:, 0:1], start=True, stop=True),
         reads=[rk(gacc), rk(ctx.ones_f)], writes=[rk(bank[0])])
    P.op("ACT", lambda e: e.activation(tcol[0:1, 0:1], bank[0][0:1, 0:1],
                                       AF.Copy, scale=0.5 / nel, bias=5e-07),
         reads=[rk(bank[0])], writes=[rk(tcol)])
    P.op("PE", lambda e: e.matmul(bank[0][:, 0:1], ctx.ones_f[0:1, :],
                                  tcol[0:1, 0:1], start=True, stop=True),
         reads=[rk(tcol), rk(ctx.ones_f)], writes=[rk(bank[0])])
    P.op("ACT", lambda e: e.copy(thr[:, :], bank[0][:, 0:1]),
         reads=[rk(bank[0])], writes=[rk(thr)])
    return thr


def _quant_block(ctx, w_r, mb, thr, wq_bf):
    P = ctx.P
    b = mb % 3
    wf32 = ctx.wf32
    P.dma("SPQ", wf32[:, b, :, :], w_r(mb), writes=[rk(wf32, b)])
    P.op("ACT", lambda e, b=b: e.activation(
        ctx.q_aw[:, :], wf32[:, b, :, :].rearrange("p a m -> p (a m)"), AF.Abs),
        reads=[rk(wf32, b)], writes=[rk(ctx.q_aw)])
    P.op("DVE", lambda e: e.tensor_single_scalar(
        ctx.q_aw[:, :], ctx.q_aw[:, :], thr[:, 0:1], ALU.is_gt),
        reads=[rk(ctx.q_aw), rk(thr)], writes=[rk(ctx.q_aw)])
    P.op("ACT", lambda e, b=b: e.sign(
        ctx.q_sg[:, :], wf32[:, b, :, :].rearrange("p a m -> p (a m)")),
        reads=[rk(wf32, b)], writes=[rk(ctx.q_sg)])
    P.op("DVE", lambda e: e.tensor_tensor(
        wq_bf[:, :, :].rearrange("p a m -> p (a m)"), ctx.q_sg[:, :],
        ctx.q_aw[:, :], ALU.mult),
        reads=[rk(ctx.q_sg), rk(ctx.q_aw)], writes=[rk(wq_bf)])


def l3_ffn(ctx, hb, x_sb, xn_bf):
    nc, P = ctx.nc, ctx.P
    bank = ctx.bank
    H3 = 4096
    _mk = ctx.scr_mark()
    ctx.wf32 = ctx.scr("l3_wf32", [128, 3, 8, 128], F32)
    ctx.q_aw = ctx.scr("l3_aw", [128, 1024], F32)
    ctx.q_sg = ctx.scr("l3_sg", [128, 1024], F32)
    wq_bf = ctx.scr("l3_wq", [128, 8, 128], BF16)
    h_bf = ctx.scr("l3_h", [128, 32, TT], BF16)
    lnw = ctx.scr("l3_lnw", [128, 32], F32)
    lnb = ctx.scr("l3_lnb", [128, 32], F32)
    m_tm = ctx.scr("l3_mtm", [128, 4], F32)
    r_tm = ctx.scr("l3_rtm", [128, 4], F32)
    P.dma("SPQ", lnw[:, :], hb["L3_ffn_lnw_col"].ap(), writes=[rk(lnw)])
    P.dma("SPQ", lnb[:, :], hb["L3_ffn_lnb_col"].ap(), writes=[rk(lnb)])
    wfc_r0 = hb["L3_ffn_W_fcT"].ap()
    wpj_r0 = hb["L3_ffn_W_projT"].ap().rearrange("(k p) (oc m) -> p k oc m",
                                                 p=128, m=128)

    def wfc_r(mb):
        return wfc_r0[mb]

    def wpj_r(k):
        return wpj_r0[:, k, :, :]

    thr1 = _quant_thr(ctx, wfc_r, 32, H3 * D, "q1")
    thr2 = _quant_thr(ctx, wpj_r, 32, H3 * D, "q2")

    if "thr" in ctx.dbg:
        P.dma("SPQ", ctx.dbg["thr"].ap()[:, 0:1], thr1[:, :], reads=[rk(thr1)],
              allow_slow_non_contiguous=True)
        P.dma("SPQ", ctx.dbg["thr"].ap()[:, 1:2], thr2[:, :], reads=[rk(thr2)],
              allow_slow_non_contiguous=True)
    for tt in range(NTT):
        tsl = slice(tt * TT, (tt + 1) * TT)
        for mb in range(32):
            _quant_block(ctx, wfc_r, mb, thr1, wq_bf)
            bk = bank[6 + mb % 2]
            for c in range(NCH):
                P.op("PE", lambda e, mb=mb, c=c, tsl=tsl, bk=bk: e.matmul(
                    bk[:, :], wq_bf[:, c, :], xn_bf[:, c, tsl],
                    start=(c == 0), stop=(c == NCH - 1), skip_group_check=True),
                    reads=[rk(wq_bf), rk(xn_bf, c)], writes=[rk(bk)])
            P.op("ACT", lambda e, mb=mb, bk=bk: e.activation(
                h_bf[:, mb, :], bk[:, :], AF.Gelu),
                reads=[rk(bk)], writes=[rk(h_bf, mb)])
        if tt == 0 and "h3" in ctx.dbg:
            P.dma("SPQ", ctx.dbg["h3"].ap(), h_bf[:, :, :],
                  reads=[rk(h_bf, c) for c in range(32)])
        for c in range(32):
            P.op("ACT", lambda e, c=c: e.activation(
                ctx.sq[:, :], h_bf[:, c, :], AF.Square),
                reads=[rk(h_bf, c)], writes=[rk(ctx.sq)])
            P.op("PE", lambda e, c=c: e.matmul(
                bank[0][0:1, :], ctx.ones_b[:, 0:1], h_bf[:, c, :],
                start=(c == 0), stop=(c == 31), skip_group_check=True),
                reads=[rk(h_bf, c), rk(ctx.ones_b)], writes=[rk(bank[0])])
            P.op("PE", lambda e, c=c: e.matmul(
                bank[1][0:1, :], ctx.ones_f[:, 0:1], ctx.sq[:, :],
                start=(c == 0), stop=(c == 31), skip_group_check=True),
                reads=[rk(ctx.sq), rk(ctx.ones_f)], writes=[rk(bank[1])])
        P.op("ACT", lambda e: e.copy(ctx.rowA[0:1, 0:TT], bank[0][0:1, :]),
             reads=[rk(bank[0])], writes=[rk(ctx.rowA)])
        P.op("ACT", lambda e: e.copy(ctx.rowB[0:1, 0:TT], bank[1][0:1, :]),
             reads=[rk(bank[1])], writes=[rk(ctx.rowB)])
        ctx.row_to_tm_cols(ctx.rowA, 0, m_tm, 0, bank[1], n=TT, scale=1.0 / H3)
        ctx.row_to_tm_cols(ctx.rowB, 0, r_tm, 0, bank[1], n=TT, scale=1.0 / H3)
        P.op("DVE", lambda e: e.scalar_tensor_tensor(
            ctx.sq[:, 0:4], m_tm[:, :], 1.0, m_tm[:, :], ALU.mult, ALU.mult),
            reads=[rk(m_tm)], writes=[rk(ctx.sq)])
        P.op("DVE", lambda e: e.tensor_tensor(r_tm[:, :], r_tm[:, :],
                                              ctx.sq[:, 0:4], ALU.subtract),
             reads=[rk(r_tm), rk(ctx.sq)], writes=[rk(r_tm)])
        P.op("ACT", lambda e: e.activation(r_tm[:, :], r_tm[:, :], AF.Ln,
                                           bias=1e-5),
             reads=[rk(r_tm)], writes=[rk(r_tm)])
        P.op("ACT", lambda e: e.activation(r_tm[:, :], r_tm[:, :], AF.Exp,
                                           scale=-0.5),
             reads=[rk(r_tm)], writes=[rk(r_tm)])
        P.op("DVE", lambda e: e.tensor_tensor(m_tm[:, :], m_tm[:, :], r_tm[:, :],
                                              ALU.mult),
             reads=[rk(m_tm), rk(r_tm)], writes=[rk(m_tm)])
        ctx.tm_cols_to_row(r_tm, 0, ctx.rowA, 0, bank[1], n=TT)
        ctx.tm_cols_to_row(m_tm, 0, ctx.rowB, 0, bank[1], n=TT)
        ctx.bcast_row(ctx.rowA, 0, bank[4])
        ctx.bcast_row(ctx.rowB, 0, bank[5])
        for c in range(32):
            P.op("DVE", lambda e, c=c: e.tensor_tensor(
                ctx.ftmp[:, :], h_bf[:, c, :], bank[4][:, :], ALU.mult),
                reads=[rk(h_bf, c), rk(bank[4])], writes=[rk(ctx.ftmp)])
            P.op("DVE", lambda e, c=c: e.tensor_tensor(
                ctx.ftmp[:, :], ctx.ftmp[:, :], bank[5][:, :], ALU.subtract),
                reads=[rk(ctx.ftmp), rk(bank[5])], writes=[rk(ctx.ftmp)])
            P.op("DVE", lambda e, c=c: e.tensor_scalar(
                h_bf[:, c, :], ctx.ftmp[:, :], lnw[:, c:c + 1], lnb[:, c:c + 1],
                ALU.mult, ALU.add),
                reads=[rk(ctx.ftmp), rk(lnw), rk(lnb)], writes=[rk(h_bf, c)])
        if tt == 0 and "hn3" in ctx.dbg:
            P.dma("SPQ", ctx.dbg["hn3"].ap(), h_bf[:, :, :],
                  reads=[rk(h_bf, c) for c in range(32)])
        for k in range(32):
            _quant_block(ctx, wpj_r, k, thr2, wq_bf)
            for o in range(NCH):
                P.op("PE", lambda e, k=k, o=o: e.matmul(
                    bank[o][:, :], wq_bf[:, o, :], h_bf[:, k, :],
                    start=(k == 0), stop=(k == 31), skip_group_check=True),
                    reads=[rk(wq_bf), rk(h_bf, k)], writes=[rk(bank[o])])
        for o in range(NCH):
            P.op("DVE", lambda e, o=o, tsl=tsl: e.scalar_tensor_tensor(
                x_sb[:, o, tsl], bank[o][:, :], 1.0, x_sb[:, o, tsl],
                ALU.mult, ALU.add),
                reads=[rk(bank[o]), rk(x_sb, o)], writes=[rk(x_sb, o)])
    ctx.scr_release(_mk)



def l3_attn(ctx, hb, x_sb, xn_bf):
    nc, P = ctx.nc, ctx.P
    bank = ctx.bank
    RMSE = 1.1920929e-07
    _mk = ctx.scr_mark()
    vk_f = ctx.scr("l3_vk", [128, NCH, 65], F32)
    vk_bf = ctx.scr("l3_vkb", [128, NCH, 65], BF16)
    gq = ctx.scr("l3_gq", [128, NCH], F32)
    gk = ctx.scr("l3_gk", [128, NCH], F32)
    gout = ctx.scr("l3_go", [128, NCH], F32)
    bqkv = ctx.scr("l3_bq", [128, 24], F32)
    bout = ctx.scr("l3_bo", [128, NCH], F32)
    ctab = ctx.scr("l3_ct", [128, T], BF16)
    stab = ctx.scr("l3_st", [128, T], BF16)
    pswp = ctx.scr("l3_ps", [128, 128], BF16)
    P.dma("SPQ", gq[:, :], hb["L3_attn_gq_col"].ap(), writes=[rk(gq)])
    P.dma("SPQ", gk[:, :], hb["L3_attn_gk_col"].ap(), writes=[rk(gk)])
    P.dma("SPQ", gout[:, :], hb["L3_attn_gout_col"].ap(), writes=[rk(gout)])
    P.dma("SPQ", bqkv[:, :], hb["L3_attn_bqkv_col"].ap(), writes=[rk(bqkv)])
    P.dma("SPQ", bout[:, :], hb["L3_attn_bout_col"].ap(), writes=[rk(bout)])
    th = 1.0 / 10000.0 ** (np.arange(0, 64, 2, dtype=np.float64) / 64.0)
    tpos = np.arange(T, dtype=np.float64)
    fr = np.outer(th, tpos)
    cos_np = np.cos(fr)
    sin_np = np.sin(fr)
    ctab_np = np.zeros((128, T), np.float32)
    stab_np = np.zeros((128, T), np.float32)
    for blk in (0, 64):
        ctab_np[blk:blk + 32] = cos_np
        ctab_np[blk + 32:blk + 64] = cos_np
        stab_np[blk:blk + 32] = sin_np
        stab_np[blk + 32:blk + 64] = sin_np
    psw_np = np.zeros((128, 128), np.float32)
    for blk in (0, 64):
        for i in range(32):
            psw_np[blk + i, blk + i + 32] = 1.0
            psw_np[blk + i + 32, blk + i] = -1.0
    ch = nc.inline_tensor(ctab_np.astype(ml_dtypes.bfloat16), "c_l3cos")
    sh = nc.inline_tensor(stab_np.astype(ml_dtypes.bfloat16), "c_l3sin")
    ph = nc.inline_tensor(psw_np.astype(ml_dtypes.bfloat16), "c_l3swp")
    P.dma("SPQ", ctab[:, :], ch[:, :], writes=[rk(ctab)])
    P.dma("SPQ", stab[:, :], sh[:, :], writes=[rk(stab)])
    P.dma("SPQ", pswp[:, :], ph[:, :], writes=[rk(pswp)])

    wq_r = hb["L3_attn_WqkvT"].ap()

    def rope_relu(dst_ap, dkeys, col, wid, gcol_ap, gread, rbank, rowr, roff):
        P.op("DVE", lambda e: e.scalar_tensor_tensor(
            dst_ap, dst_ap, gcol_ap, rbank, ALU.mult, ALU.mult),
            reads=dkeys + [gread, rk(rbank)], writes=dkeys)
        P.op("PE", lambda e: e.matmul(
            bank[2][:, 0:wid], pswp[:, :], dst_ap, start=True, stop=True),
            reads=dkeys + [rk(pswp)], writes=[rk(bank[2])])
        P.op("DVE", lambda e: e.tensor_tensor(
            ctx.ftmp[:, 0:wid], dst_ap, ctab[:, col:col + wid], ALU.mult),
            reads=dkeys + [rk(ctab)], writes=[rk(ctx.ftmp)])
        P.op("DVE", lambda e: e.tensor_tensor(
            ctx.sq[:, 0:wid], bank[2][:, 0:wid], stab[:, col:col + wid], ALU.mult),
            reads=[rk(bank[2]), rk(stab)], writes=[rk(ctx.sq)])
        P.op("DVE", lambda e: e.tensor_tensor(
            ctx.ftmp[:, 0:wid], ctx.ftmp[:, 0:wid], ctx.sq[:, 0:wid], ALU.add),
            reads=[rk(ctx.ftmp), rk(ctx.sq)], writes=[rk(ctx.ftmp)])
        P.op("DVE", lambda e: e.tensor_single_scalar(
            dst_ap, ctx.ftmp[:, 0:wid], 0.0, ALU.max),
            reads=[rk(ctx.ftmp)], writes=dkeys)

    _mk2 = ctx.scr_mark()
    wk_sb = ctx.scr("l3_wk", [128, NCH, NCH, 128], BF16)
    wv_sb = ctx.scr("l3_wv", [128, NCH, NCH, 128], BF16)
    kc = ctx.scr("l3_kc", [128, NCH, 128], BF16)
    vc = ctx.scr("l3_vc", [128, NCH, 128], BF16)
    kT_sb = ctx.scr("l3_kT", [128, 64], BF16)
    vT_sb = ctx.scr("l3_vT", [128, 65], BF16)
    for fc in range(NCH):
        P.dma("PQ", wk_sb[:, :, fc, :], wq_r[8 + fc],
              writes=[rk(wk_sb, fc)])
        P.dma("PQ", wv_sb[:, :, fc, :], wq_r[16 + fc],
              writes=[rk(wv_sb, fc)])
    for c in range(NCH):
        P.op("DVE", lambda e, c=c: e.memset(vk_f[:, c, :], 0.0),
             writes=[rk(vk_f, c)])
    P.op("DVE", lambda e: e.memset(vT_sb[:, 64:65], 1.0), writes=[rk(vT_sb, "one")])
    b7, b6 = bank[7], bank[6]
    ctx.psum_free(bank[7])
    ctx.psum_free(bank[6])
    tp0 = ctx.psum(ctx.u("l3tp"), [128, 512], BF16)
    P.inherit(rk(tp0), rk(b7))
    nb6 = ctx.psum(ctx.u("l3b6"), [128, 512], F32)
    P.inherit(rk(nb6), rk(b6))
    for tc in range(16):
        csl = slice(tc * 128, (tc + 1) * 128)
        for fc in range(NCH):
            for c in range(NCH):
                P.op("PE", lambda e, fc=fc, c=c, csl=csl: e.matmul(
                    nb6[:, 0:128], wk_sb[:, c, fc, :], xn_bf[:, c, csl],
                    start=(c == 0), stop=(c == NCH - 1), skip_group_check=True),
                    reads=[rk(wk_sb, fc), rk(xn_bf, c)], writes=[rk(nb6)])
            P.op("ACT", lambda e, fc=fc: e.activation(
                kc[:, fc, :], nb6[:, 0:128], AF.Identity,
                bias=bqkv[:, 8 + fc:9 + fc]),
                reads=[rk(nb6), rk(bqkv)], writes=[rk(kc, fc)])
            for c in range(NCH):
                P.op("PE", lambda e, fc=fc, c=c, csl=csl: e.matmul(
                    bank[5][:, 0:128], wv_sb[:, c, fc, :], xn_bf[:, c, csl],
                    start=(c == 0), stop=(c == NCH - 1), skip_group_check=True),
                    reads=[rk(wv_sb, fc), rk(xn_bf, c)], writes=[rk(bank[5])])
            P.op("ACT", lambda e, fc=fc: e.activation(
                vc[:, fc, :], bank[5][:, 0:128], AF.Identity,
                bias=bqkv[:, 16 + fc:17 + fc]),
                reads=[rk(bank[5]), rk(bqkv)], writes=[rk(vc, fc)])
        for fc in range(NCH):
            P.op("ACT", lambda e, fc=fc: e.activation(
                ctx.sq[:, 0:128], kc[:, fc, :], AF.Square),
                reads=[rk(kc, fc)], writes=[rk(ctx.sq)])
            P.op("PE", lambda e, fc=fc: e.matmul(
                bank[0][0:1, 0:128], ctx.ones_f[:, 0:1], ctx.sq[:, 0:128],
                start=(fc == 0), stop=(fc == NCH - 1), skip_group_check=True),
                reads=[rk(ctx.sq), rk(ctx.ones_f)], writes=[rk(bank[0])])
        P.op("ACT", lambda e: e.activation(
            ctx.rowB[0:1, 0:128], bank[0][0:1, 0:128], AF.Ln, bias=RMSE,
            scale=1.0 / D),
            reads=[rk(bank[0])], writes=[rk(ctx.rowB)])
        P.op("ACT", lambda e: e.activation(
            ctx.rowB[0:1, 0:128], ctx.rowB[0:1, 0:128], AF.Exp, scale=-0.5),
            reads=[rk(ctx.rowB)], writes=[rk(ctx.rowB)])
        P.op("PE", lambda e: e.matmul(
            bank[3][:, 0:128], ctx.ones_f[0:1, :], ctx.rowB[0:1, 0:128],
            start=True, stop=True),
            reads=[rk(ctx.rowB), rk(ctx.ones_f)], writes=[rk(bank[3])])
        for fc in range(NCH):
            rope_relu(kc[:, fc, :], [rk(kc, fc)], tc * 128, 128,
                      gk[:, fc:fc + 1], rk(gk), bank[3][:, 0:128], None, 0)
        for h in range(16):
            hc, hb64 = h // 2, 64 * (h % 2)
            P.op("PE", lambda e, hc=hc, hb64=hb64: e.transpose(
                tp0[:, 0:64], kc[hb64:hb64 + 64, hc, :],
                ctx.id_b[hb64:hb64 + 64, hb64:hb64 + 64]),
                reads=[rk(kc, hc), rk(ctx.id_b)], writes=[rk(tp0)])
            P.op("ACT", lambda e: e.copy(kT_sb[:, :], tp0[:, 0:64]),
                 reads=[rk(tp0)], writes=[rk(kT_sb)])
            P.op("PE", lambda e, hc=hc, hb64=hb64: e.transpose(
                tp0[:, 64:128], vc[hb64:hb64 + 64, hc, :],
                ctx.id_b[hb64:hb64 + 64, hb64:hb64 + 64]),
                reads=[rk(vc, hc), rk(ctx.id_b)], writes=[rk(tp0)])
            P.op("ACT", lambda e: e.copy(vT_sb[:, 0:64], tp0[:, 64:128]),
                 reads=[rk(tp0)], writes=[rk(vT_sb)])
            P.op("PE", lambda e, hb64=hb64: e.matmul(
                bank[5][hb64:hb64 + 64, 0:65], kT_sb[:, :], vT_sb[:, :],
                start=True, stop=True, skip_group_check=True),
                reads=[rk(kT_sb), rk(vT_sb), rk(vT_sb, "one")],
                writes=[rk(bank[5])]) if hb64 == 0 else \
            P.op("PE", lambda e, hb64=hb64: e.matmul(
                bank[5][hb64:hb64 + 64, 0:65], kT_sb[:, :], vT_sb[:, :],
                start=True, stop=True, skip_group_check=True),
                reads=[rk(kT_sb), rk(vT_sb), rk(vT_sb, "one")],
                writes=[rk(bank[5])])
            P.op("DVE", lambda e, hc=hc, hb64=hb64: e.tensor_tensor(
                vk_f[hb64:hb64 + 64, hc, :], vk_f[hb64:hb64 + 64, hc, :],
                bank[5][hb64:hb64 + 64, 0:65], ALU.add),
                reads=[rk(vk_f, hc), rk(bank[5])], writes=[rk(vk_f, hc)])
    for c in range(NCH):
        P.op("DVE", lambda e, c=c: e.tensor_copy(vk_bf[:, c, :], vk_f[:, c, :]),
             reads=[rk(vk_f, c)], writes=[rk(vk_bf, c)])
    ctx.psum_free(nb6)
    ctx.psum_free(tp0)
    rb7 = ctx.psum(ctx.u("b7r"), [128, 512], F32)
    rb6 = ctx.psum(ctx.u("b6r"), [128, 512], F32)
    P.inherit(rk(rb7), rk(tp0))
    P.inherit(rk(rb6), rk(nb6))
    ctx.bank[6] = rb6
    ctx.bank[7] = rb7
    bank = ctx.bank
    ctx.scr_release(_mk2)

    qf = ctx.scr("l3_qf", [128, NCH, T], BF16)
    def q_src(mb):
        return wq_r[mb]

    def xn_rhs(c, tt):
        return xn_bf[:, c, tt * TT:(tt + 1) * TT], [rk(xn_bf, c)]

    def q_evac(mb, tt, bk):
        tsl = slice(tt * TT, (tt + 1) * TT)
        P.op("ACT", lambda e, mb=mb, bk=bk: e.copy(
            qf[:, mb, tsl], bk[:, :]),
            reads=[rk(bk)], writes=[rk(qf, mb)]) if False else None
        P.op("ACT", lambda e, mb=mb, bk=bk, tsl=tsl: e.activation(
            qf[:, mb, tsl], bk[:, :], AF.Identity, bias=bqkv[:, mb:mb + 1]),
            reads=[rk(bk), rk(bqkv)], writes=[rk(qf, mb)])
    proj_stream(ctx, q_src, NCH, NCH, xn_rhs, q_evac)
    for tt in range(NTT):
        tsl = slice(tt * TT, (tt + 1) * TT)
        for c in range(NCH):
            P.op("ACT", lambda e, c=c, tsl=tsl: e.activation(
                ctx.sq[:, 0:TT], qf[:, c, tsl], AF.Square),
                reads=[rk(qf, c)], writes=[rk(ctx.sq)])
            P.op("PE", lambda e, c=c: e.matmul(
                bank[0][0:1, :], ctx.ones_f[:, 0:1], ctx.sq[:, 0:TT],
                start=(c == 0), stop=(c == NCH - 1), skip_group_check=True),
                reads=[rk(ctx.sq), rk(ctx.ones_f)], writes=[rk(bank[0])])
        P.op("ACT", lambda e, tsl=tsl: e.activation(
            ctx.rowA[0:1, tsl], bank[0][0:1, :], AF.Ln, bias=RMSE, scale=1.0 / D),
            reads=[rk(bank[0])], writes=[rk(ctx.rowA)])
    P.op("ACT", lambda e: e.activation(ctx.rowA[0:1, :], ctx.rowA[0:1, :],
                                       AF.Exp, scale=-0.5),
         reads=[rk(ctx.rowA)], writes=[rk(ctx.rowA)])
    for tt in range(NTT):
        tsl = slice(tt * TT, (tt + 1) * TT)
        ctx.bcast_row(ctx.rowA, tt * TT, bank[3])
        for c in range(NCH):
            rope_relu(qf[:, c, tsl], [rk(qf, c)], tt * TT, TT,
                      gq[:, c:c + 1], rk(gq), bank[3][:, :], None, 0)

    if "qf" in ctx.dbg:
        P.dma("SPQ", ctx.dbg["qf"].ap(), qf[:, :, :],
              reads=[rk(qf, c) for c in range(NCH)])
    if "vk" in ctx.dbg:
        P.dma("SPQ", ctx.dbg["vk"].ap(), vk_bf[:, :, :],
              reads=[rk(vk_bf, c) for c in range(NCH)])
    zrow_sb = ctx.scr("l3_zr", [65, TT], F32)
    att = xn_bf
    for hp in range(NCH):
        for hb64 in (0, 64):
            zrow = 64 - hb64
            for tt in range(NTT):
                tsl = slice(tt * TT, (tt + 1) * TT)
                P.op("PE", lambda e, hp=hp, hb64=hb64, tsl=tsl: e.matmul(
                    bank[1][hb64:hb64 + 64, :], vk_bf[hb64:hb64 + 64, hp, 0:64],
                    qf[hb64:hb64 + 64, hp, tsl], start=True, stop=True,
                    skip_group_check=True),
                    reads=[rk(vk_bf, hp), rk(qf, hp)], writes=[rk(bank[1])])
                P.op("PE", lambda e, hp=hp, hb64=hb64, tsl=tsl, zrow=zrow: e.matmul(
                    bank[1][zrow:zrow + 1, :], vk_bf[hb64:hb64 + 64, hp, 64:65],
                    qf[hb64:hb64 + 64, hp, tsl], start=True, stop=True,
                    skip_group_check=True),
                    reads=[rk(vk_bf, hp), rk(qf, hp)], writes=[rk(bank[1])])
                P.op("ACT", lambda e, hp=hp, hb64=hb64, tsl=tsl: e.copy(
                    att[hb64:hb64 + 64, hp, tsl], bank[1][hb64:hb64 + 64, :]),
                    reads=[rk(bank[1])], writes=[rk(att, hp)])
                P.op("ACT", lambda e, zrow=zrow: e.copy(
                    zrow_sb[zrow:zrow + 1, :], bank[1][zrow:zrow + 1, :]),
                    reads=[rk(bank[1])], writes=[rk(zrow_sb)])
                P.op("DVE", lambda e, zrow=zrow: e.tensor_single_scalar(
                    zrow_sb[zrow:zrow + 1, :], zrow_sb[zrow:zrow + 1, :], 1e-6,
                    ALU.add),
                    reads=[rk(zrow_sb)], writes=[rk(zrow_sb)])
                P.op("DVE", lambda e, zrow=zrow: e.reciprocal(
                    zrow_sb[zrow:zrow + 1, :], zrow_sb[zrow:zrow + 1, :]),
                    reads=[rk(zrow_sb)], writes=[rk(zrow_sb)])
                P.op("PE", lambda e, hb64=hb64, zrow=zrow: e.matmul(
                    bank[3][hb64:hb64 + 64, :], ctx.ones_f[zrow:zrow + 1, 0:64],
                    zrow_sb[zrow:zrow + 1, :], start=True, stop=True),
                    reads=[rk(zrow_sb), rk(ctx.ones_f)], writes=[rk(bank[3])])
                P.op("DVE", lambda e, hp=hp, hb64=hb64, tsl=tsl: e.tensor_tensor(
                    att[hb64:hb64 + 64, hp, tsl], att[hb64:hb64 + 64, hp, tsl],
                    bank[3][hb64:hb64 + 64, :], ALU.mult),
                    reads=[rk(att, hp), rk(bank[3])], writes=[rk(att, hp)])

    pre = ctx.scr("l3_pre", [128, NCH, TT], BF16)
    wo_r = hb["L3_attn_WoutT"].ap()
    wb = ctx.wstream
    for tt in range(NTT):
        tsl = slice(tt * TT, (tt + 1) * TT)
        for mb in range(NCH):
            b = mb % 3
            P.dma("PQ", wb[:, b, 0:NCH, :],
                  wo_r[mb], writes=[rk(wb, b)])
            bk = bank[6 + mb % 2]
            for c in range(NCH):
                P.op("PE", lambda e, mb=mb, b=b, c=c, tsl=tsl, bk=bk: e.matmul(
                    bk[:, :], wb[:, b, c, :], att[:, c, tsl],
                    start=(c == 0), stop=(c == NCH - 1), skip_group_check=True),
                    reads=[rk(wb, b), rk(att, c)], writes=[rk(bk)])
            P.op("ACT", lambda e, mb=mb, bk=bk: e.activation(
                pre[:, mb, :], bk[:, :], AF.Identity, bias=bout[:, mb:mb + 1]),
                reads=[rk(bk), rk(bout)], writes=[rk(pre, mb)])
        for c in range(NCH):
            P.op("ACT", lambda e, c=c: e.activation(
                ctx.sq[:, 0:TT], pre[:, c, :], AF.Square),
                reads=[rk(pre, c)], writes=[rk(ctx.sq)])
            P.op("PE", lambda e, c=c: e.matmul(
                bank[0][0:1, :], ctx.ones_f[:, 0:1], ctx.sq[:, 0:TT],
                start=(c == 0), stop=(c == NCH - 1), skip_group_check=True),
                reads=[rk(ctx.sq), rk(ctx.ones_f)], writes=[rk(bank[0])])
        P.op("ACT", lambda e: e.activation(
            ctx.rowA[0:1, 0:TT], bank[0][0:1, :], AF.Ln, bias=RMSE, scale=1.0 / D),
            reads=[rk(bank[0])], writes=[rk(ctx.rowA)])
        P.op("ACT", lambda e: e.activation(
            ctx.rowA[0:1, 0:TT], ctx.rowA[0:1, 0:TT], AF.Exp, scale=-0.5),
            reads=[rk(ctx.rowA)], writes=[rk(ctx.rowA)])
        ctx.bcast_row(ctx.rowA, 0, bank[3])
        for c in range(NCH):
            P.op("DVE", lambda e, c=c: e.scalar_tensor_tensor(
                ctx.ftmp[:, :], pre[:, c, :], gout[:, c:c + 1], bank[3][:, :],
                ALU.mult, ALU.mult),
                reads=[rk(pre, c), rk(gout), rk(bank[3])], writes=[rk(ctx.ftmp)])
            P.op("DVE", lambda e, c=c, tsl=tsl: e.tensor_tensor(
                x_sb[:, c, tsl], x_sb[:, c, tsl], ctx.ftmp[:, :], ALU.add),
                reads=[rk(x_sb, c), rk(ctx.ftmp)], writes=[rk(x_sb, c)])
    ctx.scr_release(_mk)


def build_launch2(arrs, blocks=(2, 3), debug=None):
    ctx = Ctx()
    nc = ctx.nc
    hb = {}
    for name, a in arrs.items():
        if name.startswith(("L2_", "L3_")):
            hb[name] = nc.dram_tensor(name, list(a.shape), F32, kind="ExternalInput")
    x_in = nc.dram_tensor("x_in", [128, NCH, T], F32, kind="ExternalInput")
    xn_in = nc.dram_tensor("xn_in", [128, NCH, T], BF16, kind="ExternalInput")
    i1_in = nc.dram_tensor("i1_in", [128, 16, 8], F32, kind="ExternalInput")
    i2_in = nc.dram_tensor("i2_in", [128, 16, 8], F32, kind="ExternalInput")
    p1_in = nc.dram_tensor("p1_in", [128, 16], F32, kind="ExternalInput")
    p2_in = nc.dram_tensor("p2_in", [128, 16], F32, kind="ExternalInput")
    cn_in = nc.dram_tensor("cn_in", [8, 8], F32, kind="ExternalInput")
    cm_in = nc.dram_tensor("cm_in", [8, 1], F32, kind="ExternalInput")
    x_out = nc.dram_tensor("x_out", [128, NCH, T], F32, kind="ExternalOutput")
    ctx.setup()
    P = ctx.P
    x_sb = ctx.sb("x_sb", [128, NCH, T], F32)
    xn_bf = ctx.sb("xn_bf", [128, NCH, T], BF16)
    ind1 = ctx.sb("ind1", [128, 16, 8], F32)
    ind2 = ctx.sb("ind2", [128, 16, 8], F32)
    p1 = ctx.sb("p1", [128, 16], F32)
    p2 = ctx.sb("p2", [128, 16], F32)
    cnts = ctx.sb("cnts", [8, 8], F32)
    cmask = ctx.sb("cmask", [8, 1], F32)
    ctx.init_scratch()
    ctx.dbg = {}
    if debug:
        for nm, shape, dt_ in debug:
            ctx.dbg[nm] = nc.dram_tensor("dbg_" + nm, shape, dt_,
                                         kind="ExternalOutput")
    P.dma("SPQ", x_sb[:, :, :], x_in.ap(),
          writes=[rk(x_sb, c) for c in range(NCH)])
    P.dma("SPQ", xn_bf[:, :, :], xn_in.ap(),
          writes=[rk(xn_bf, c) for c in range(NCH)])
    P.dma("SPQ", ind1[:, :, :], i1_in.ap(),
          writes=[rk(ind1, j) for j in range(16)])
    P.dma("SPQ", ind2[:, :, :], i2_in.ap(),
          writes=[rk(ind2, j) for j in range(16)])
    P.dma("SPQ", p1[:, :], p1_in.ap(), writes=[rk(p1)])
    P.dma("SPQ", p2[:, :], p2_in.ap(), writes=[rk(p2)])
    P.dma("SPQ", cnts[:, :], cn_in.ap(), writes=[rk(cnts)])
    P.dma("SPQ", cmask[:, :], cm_in.ap(), writes=[rk(cmask)])

    if 2 in blocks:
        l2_attn(ctx, hb, x_sb, xn_bf)
        l2_ffn(ctx, hb, x_sb, xn_bf, ind1, ind2, p1, p2, cnts, cmask)
    if 3 in blocks:
        ln_block(ctx, x_sb, xn_bf)
        if "xn3" in ctx.dbg:
            P.dma("SPQ", ctx.dbg["xn3"].ap(), xn_bf[:, :, :],
                  reads=[rk(xn_bf, c) for c in range(NCH)])
        l3_ffn(ctx, hb, x_sb, xn_bf)
        if "xf3" in ctx.dbg:
            P.dma("SPQ", ctx.dbg["xf3"].ap(), x_sb[:, :, :],
                  reads=[rk(x_sb, c) for c in range(NCH)])
        l3_attn(ctx, hb, x_sb, xn_bf)

    P.dma("SPQ", x_out.ap(), x_sb[:, :, :],
          reads=[rk(x_sb, c) for c in range(NCH)])
    with nc.Block() as block:
        P.emit(block)
    return ctx


def build_fused(arrs, debug=None):
    ctx = Ctx()
    nc = ctx.nc
    hb = {}
    for name, a in arrs.items():
        hb[name] = nc.dram_tensor(name, list(a.shape), F32, kind="ExternalInput")
    x_in = nc.dram_tensor("x_in", [128, NCH, T], F32, kind="ExternalInput")
    cm_in = nc.dram_tensor("cm_in", [8, 1], F32, kind="ExternalInput")
    x_out = nc.dram_tensor("x_out", [128, NCH, T], F32, kind="ExternalOutput")
    cc_in = nc.dram_tensor("cc_cnt_in", [1, 8], F32)
    cc_out = nc.dram_tensor("cc_cnt_out", [8, 8], F32)
    ctx.setup()
    P = ctx.P
    x_sb = ctx.sb("x_sb", [128, NCH, T], F32)
    xn_bf = ctx.sb("xn_bf", [128, NCH, T], BF16)
    ind1 = ctx.sb("ind1", [128, 16, 8], F32)
    ind2 = ctx.sb("ind2", [128, 16, 8], F32)
    p1 = ctx.sb("p1", [128, 16], F32)
    p2 = ctx.sb("p2", [128, 16], F32)
    cnt_sb = ctx.sb("cnt_sb", [1, 8], F32)
    cnts = ctx.sb("cnts", [8, 8], F32)
    cmask = ctx.sb("cmask", [8, 1], F32)
    ctx.init_scratch()
    ctx.dbg = {}
    if debug:
        for nm, shape, dt_ in debug:
            ctx.dbg[nm] = nc.dram_tensor("dbg_" + nm, shape, dt_,
                                         kind="ExternalOutput")
    P.dma("SPQ", x_sb[:, :, :], x_in.ap(),
          writes=[rk(x_sb, c) for c in range(NCH)])
    P.dma("SPQ", cmask[:, :], cm_in.ap(), writes=[rk(cmask)])

    mk = ctx.scr_mark()
    gate_sb = ctx.scr("l0_gate", [128, NCH, 8], F32)
    lg = ctx.scr("l0_lg", [128, 16, 8], F32)
    c_tm = ctx.scr("l0_ctm", [128, 16, 8], F32)
    P.dma("SPQ", gate_sb[:, :, :],
          hb["L0_ffn_gate_W"].ap().rearrange("(c p) e -> p c e", p=128),
          writes=[rk(gate_sb)])
    ln_block(ctx, x_sb, xn_bf, router_w=gate_sb, lg_tm=lg)
    l0_attn(ctx, hb, x_sb, xn_bf)
    topk_combine_l0(ctx, lg, c_tm)
    moe_ffn(ctx, hb, x_sb, xn_bf, c_tm, "l0f", "L0_ffn_W1", "L0_ffn_b1",
            "L0_ffn_W2", "L0_ffn_b2", act=AF.Gelu,
            shared={"w": "L0_ffn_shared_W", "b": "L0_ffn_shared_b_col"},
            out_ln={"g": "L0_ffn_gamma_col", "b": "L0_ffn_beta_col"})
    ctx.scr_release(mk)

    mk = ctx.scr_mark()
    r_sb = ctx.scr("l1_wr", [128, NCH, 8], F32)
    br_bc = ctx.scr("l1_br", [128, 8], F32)
    lg = ctx.scr("l1_lg", [128, 16, 8], F32)
    c_tm = ctx.scr("l1_ctm", [128, 16, 8], F32)
    P.dma("SPQ", r_sb[:, :, :],
          hb["L1_ffn_Wr"].ap().rearrange("(c p) e -> p c e", p=128),
          writes=[rk(r_sb)])
    P.dma("SPQ", br_bc[:, :],
          hb["L1_ffn_br"].ap().partition_broadcast(128)[:, 0, :],
          writes=[rk(br_bc)])
    ln_block(ctx, x_sb, xn_bf, router_w=r_sb, lg_tm=lg)
    l1_attn(ctx, hb, x_sb, xn_bf)
    topk_combine_l1(ctx, lg, c_tm, br_bc)
    moe_ffn(ctx, hb, x_sb, xn_bf, c_tm, "l1f", "L1_ffn_W1", "L1_ffn_b1",
            "L1_ffn_W2", "L1_ffn_b2", act=AF.Relu)
    ctx.scr_release(mk)

    mk = ctx.scr_mark()
    r2_sb = ctx.scr("l2_wr", [128, NCH, 8], F32)
    lg2 = ctx.scr("l2_lg", [128, 16, 8], F32)
    P.dma("SPQ", r2_sb[:, :, :],
          hb["L2_ffn_Wr"].ap().rearrange("(c p) e -> p c e", p=128),
          writes=[rk(r2_sb)])
    ln_block(ctx, x_sb, xn_bf, router_w=r2_sb, lg_tm=lg2)
    topk_l2_launch1(ctx, lg2, ind1, ind2, p1, p2, cnt_sb)
    ctx.scr_release(mk)
    P.dma("PQ", cc_in.ap(), cnt_sb[:, :], reads=[rk(cnt_sb)],
          writes=[("ccin",)])
    P.op("POOL", lambda e: e.collective_compute(
        "AllGather", ALU.bypass, replica_groups=[list(range(8))],
        ins=[cc_in.ap().opt()], outs=[cc_out.ap().opt()]),
        reads=[("ccin",)], writes=[("ccout",)])

    l2_attn(ctx, hb, x_sb, xn_bf)
    P.dma("PQ", cnts[:, :], cc_out.ap(), reads=[("ccout",)],
          writes=[rk(cnts)])
    l2_ffn(ctx, hb, x_sb, xn_bf, ind1, ind2, p1, p2, cnts, cmask)

    ln_block(ctx, x_sb, xn_bf)
    l3_ffn(ctx, hb, x_sb, xn_bf)
    l3_attn(ctx, hb, x_sb, xn_bf)

    P.dma("SPQ", x_out.ap(), x_sb[:, :, :],
          reads=[rk(x_sb, c) for c in range(NCH)])
    with nc.Block() as block:
        P.emit(block)
    return ctx


def _fm(xb):
    return np.ascontiguousarray(xb.T.reshape(NCH, 128, T).transpose(1, 0, 2))


def _fm_inv(a):
    return a.transpose(1, 0, 2).reshape(D, T).T


def kernel(**inputs):
    arrs = host_prep(inputs)
    x = np.asarray(inputs["L0_attn_x"], dtype=np.float32)
    ctx = build_fused(arrs)
    in_maps = []
    for b in range(NCORES):
        m = dict(arrs)
        m["x_in"] = _fm(x[b])
        m["cm_in"] = np.ascontiguousarray(
            (np.arange(NCORES) < b).astype(np.float32)[:, None])
        in_maps.append(m)
    res = run_bass_kernel_spmd(ctx.nc, in_maps, core_ids=list(range(NCORES)))
    out = np.empty((NCORES, T, D), np.float32)
    for b in range(NCORES):
        out[b] = _fm_inv(res.results[b]["x_out"])
    return out
